# version 21 (final)
# baseline (speedup 1.0000x reference)
import math
from contextlib import ExitStack

import numpy as np
import ml_dtypes
import concourse.bass as bass
import concourse.mybir as mybir
from concourse.bass_utils import run_bass_kernel_spmd

F32 = mybir.dt.float32
BF16 = mybir.dt.bfloat16
AF = mybir.ActivationFunctionType
ALU = mybir.AluOpType
AX = mybir.AxisListType

B, S, D = 16, 2048, 1024
NCORES = 8
SEQ_PER_CORE = B // NCORES
NCH = D // 128
QC = 512
NQC = S // QC
NKT = S // 128


COMPUTE_ENGINES = ("pe", "act", "dve", "pool", "sp")


class Prog:

    def __init__(self):
        self.ops = []
        self.last_writer = {}
        self.readers = {}
        self.deps = []
        self.region_pitch = {}

    def _keys(self, a):
        if not hasattr(a, "tensor"):
            return [a]
        nm = a.tensor.name
        rp = self.region_pitch.get(nm)
        if rp is None:
            return [nm]
        pitch, R = rp
        off = a.offset % pitch
        ext = 1
        for step, cnt in a.ap:
            if step >= pitch:
                continue
            ext += (cnt - 1) * step
        return [f"{nm}#{r}" for r in range(off // R, (off + ext - 1) // R + 1)]

    def add(self, engine, fn, ins=(), outs=(), dma=False):
        idx = len(self.ops)
        dep = set()
        rbufs = [k for a in ins for k in self._keys(a)]
        wbufs = [k for a in outs for k in self._keys(a)]
        for b in rbufs:
            w = self.last_writer.get(b)
            if w is not None:
                dep.add(w)
        for b in wbufs:
            w = self.last_writer.get(b)
            if w is not None:
                dep.add(w)
            for r in self.readers.get(b, ()):
                dep.add(r)
        for b in wbufs:
            self.last_writer[b] = idx
            self.readers[b] = []
        for b in rbufs:
            self.readers.setdefault(b, []).append(idx)
        stream = ("dma", wbufs[0]) if dma else engine
        self.ops.append((engine, fn, rbufs, wbufs, dma, stream))
        self.deps.append(dep)
        return idx

    def barrier(self, engine="sp"):
        idx = len(self.ops)
        dep = set()
        per_stream_last = {}
        for i, op in enumerate(self.ops):
            per_stream_last[op[5]] = i
        dep.update(per_stream_last.values())
        self.ops.append((engine, None, [], [], False, engine))
        self.deps.append(dep)
        self.last_writer = {b: idx for b in set(list(self.last_writer) + list(self.readers))}
        self.readers = {}
        return idx

    def emit(self, nc, stack):
        slim_deps = []
        for i, dep in enumerate(self.deps):
            per_stream = {}
            my_eng = self.ops[i][0]
            my_dma = self.ops[i][4]
            for d in dep:
                ds = self.ops[d][5]
                if ds == "pe" and my_eng == "pe" and not my_dma and not self.ops[i][4]:
                    continue
                per_stream[ds] = max(per_stream.get(ds, -1), d)
            slim_deps.append(per_stream)
        need_inc = set()
        for per_stream in slim_deps:
            need_inc.update(per_stream.values())
        streams = []
        for op in self.ops:
            if op[5] not in streams:
                streams.append(op[5])
        sems = {}
        for sidx, s in enumerate(streams):
            nm = s if isinstance(s, str) else f"dma{sidx}"
            sems[s] = stack.enter_context(nc.semaphore(f"S_{nm}"))
        inc_count = {}
        stream_tot = {}
        for i, op in enumerate(self.ops):
            stream = op[5]
            if op[4]:
                need_inc.add(i)
            if i in need_inc:
                step = 16 if op[4] else 1
                stream_tot[stream] = stream_tot.get(stream, 0) + step
                inc_count[i] = stream_tot[stream]
        for s, v in stream_tot.items():
            assert v < 30000, f"sem overflow risk on {s}: {v}"
        per_engine_ops = {e: [] for e in COMPUTE_ENGINES}
        for i, op in enumerate(self.ops):
            per_engine_ops[op[0]].append(i)
        self._sems = sems
        self._inc_count = inc_count
        self._need_inc = need_inc
        self._slim_deps = slim_deps
        self._per_engine_ops = per_engine_ops

    def run_engine(self, engine, builder):
        sems = self._sems
        hw = {}
        for i in self._per_engine_ops[engine]:
            eng, fn, _, _, dma, stream = self.ops[i]
            for pstream, d in sorted(self._slim_deps[i].items(),
                                     key=lambda kv: str(kv[0])):
                v = self._inc_count[d]
                if hw.get(pstream, -1) >= v:
                    continue
                builder.wait_ge(sems[pstream], v)
                hw[pstream] = v
            if fn is None:
                if i in self._need_inc:
                    builder.sem_inc(sems[stream], 1)
                continue
            inst = fn(builder)
            if i in self._need_inc:
                inst.then_inc(sems[stream], 16 if dma else 1)



def pmaj(wT, osub):
    K, O = wT.shape
    kc = K // 128
    nch = O // osub
    return np.ascontiguousarray(
        wT.reshape(kc, 128, nch, osub).transpose(2, 1, 0, 3))


def vec128(v):
    n = v.shape[0] // 128
    return np.ascontiguousarray(v.reshape(n, 128).T)


def build_constants():
    c = {}
    j = np.arange(16, dtype=np.float64)
    inv_freq = 10000.0 ** (-(2 * j) / 32.0)
    ang = np.arange(S, dtype=np.float64)[None, :] * inv_freq[:, None]
    c["rope_cos"] = np.cos(ang).astype(np.float32)
    c["rope_sin"] = np.sin(ang).astype(np.float32)
    masks = np.zeros((4, 128, QC), dtype=np.float32)
    for jj in range(4):
        kk = np.arange(128)[:, None]
        qq = np.arange(QC)[None, :]
        masks[jj] = (qq - jj * 128 - kk >= 0).astype(np.float32)
    c["masks"] = masks
    c["ones_128x1"] = np.ones((128, 1), dtype=np.float32)
    c["ones_1x128"] = np.ones((1, 128), dtype=np.float32)
    c["eye64"] = np.eye(64, dtype=np.float32)
    c["eye128"] = np.eye(128, dtype=np.float32)
    return c


def prep_weights(inp):
    w = {}
    w["l0_wdqt"] = pmaj(np.ascontiguousarray(inp["L0_attn_W_DQ"].T), 128)
    w["l0_wdkvt"] = pmaj(np.ascontiguousarray(inp["L0_attn_W_DKV"].T), 128)
    w["l0_wuqt"] = pmaj(np.ascontiguousarray(inp["L0_attn_W_UQ"].T), 64)
    w["l0_wukt"] = pmaj(np.ascontiguousarray(inp["L0_attn_W_UK"].T), 64)
    w["l0_wuvt"] = pmaj(np.ascontiguousarray(inp["L0_attn_W_UV"].T), 64)
    wqr = inp["L0_attn_W_QR"]
    perm = np.concatenate([np.concatenate([np.arange(h * 32, (h + 1) * 32, 2),
                                           np.arange(h * 32 + 1, (h + 1) * 32, 2)])
                           for h in range(16)])
    wqr_p = wqr[perm]
    w["l0_wqrt"] = pmaj(np.ascontiguousarray(wqr_p.T), 32)
    wkr = inp["L0_attn_W_KR"]
    perm32 = np.concatenate([np.arange(0, 32, 2), np.arange(1, 32, 2)])
    w["l0_wkrt"] = pmaj(np.ascontiguousarray(wkr[perm32].T), 32)[0]
    w["l0_wot"] = pmaj(np.ascontiguousarray(inp["L0_attn_W_O"].T), 128)
    w["l0_w1t"] = pmaj(np.ascontiguousarray(inp["L0_ffn_w1_w"].T), 256)
    w["l0_w2t"] = pmaj(np.ascontiguousarray(inp["L0_ffn_w2_w"].T), 256)
    w["l0_h1t"] = pmaj(np.ascontiguousarray(inp["L0_ffn_h1_w"].T), 128)
    w["l0_h2t"] = pmaj(np.ascontiguousarray(inp["L0_ffn_h2_w"].T), 128)
    w["l0_w3t"] = pmaj(np.ascontiguousarray(inp["L0_ffn_w3_w"].T), 128)
    w["l0_s1"] = vec128(inp["L0_ffn_w1_s"])
    w["l0_s2"] = vec128(inp["L0_ffn_w2_s"])
    w["l0_sh1"] = vec128(inp["L0_ffn_h1_s"])
    w["l0_sh2"] = vec128(inp["L0_ffn_h2_s"])
    w["l0_s3"] = vec128(inp["L0_ffn_w3_s"])
    w["l1_wq"] = pmaj(np.ascontiguousarray(inp["L1_attn_Wq"]), 128)
    w["l1_wkv"] = pmaj(np.ascontiguousarray(inp["L1_attn_Wkv"]), 128)
    w["l1_wo"] = pmaj(np.ascontiguousarray(inp["L1_attn_Wo"]), 128)
    w["l1_nw"] = vec128(inp["L1_attn_norm_w"])
    w["l1_nb"] = vec128(inp["L1_attn_norm_b"])
    w["l1_onw"] = vec128(inp["L1_attn_out_norm_w"])
    w["l1_onb"] = vec128(inp["L1_attn_out_norm_b"])
    w["l1_wrt"] = pmaj(np.ascontiguousarray(inp["L1_ffn_W_r"].T), 128)
    w["l1_wkt"] = pmaj(np.ascontiguousarray(inp["L1_ffn_W_k"].T), 128)
    w["l1_wvt"] = pmaj(np.ascontiguousarray(inp["L1_ffn_W_v"].T), 128)
    w["l1_mur"] = vec128(inp["L1_ffn_mu_r"])
    w["l1_muk"] = vec128(inp["L1_ffn_mu_k"])
    return w



def emit_ln_chunk(P, ctx, xres, xn_out, qc, gamma=None, beta=None,
                  src_bf16=False):
    bankA, bankB = ctx["psA"], ctx["psB"]
    bankD, bankE = ctx["psD"], ctx["psE"]
    ones1b = ctx["ones128x1_bf"]
    ones1x128f = ctx["ones_1x128_f"]
    row_m, row_r, row_mr, row_t = (ctx[k][:] for k in
                                   ("row_m", "row_r", "row_mr", "row_t"))
    tmp_f = ctx["tmp_f32"]
    cs = slice(qc * QC, (qc + 1) * QC)
    for c in range(NCH):
        tb = ctx["tmp_bf"] if c % 2 == 0 else ctx["tmp_bf2"]
        if src_bf16:
            xb = xres[:, c, cs]
        else:
            xs = xres[:, c, cs]
            P.add("dve", lambda v, xs=xs, tb=tb: v.tensor_copy(tb[:], xs),
                  ins=[xs], outs=[tb[:]])
            xb = tb[:]
        P.add("pe", lambda t, xb=xb, c=c: t.matmul(
            bankD[0:1, :], ones1b, xb, start=(c == 0), stop=(c == NCH - 1)),
            ins=[ones1b, xb], outs=[bankD[0:1, :]])
        tb2 = ctx["tmp_bf"] if c % 2 == 1 else ctx["tmp_bf2"]
        if src_bf16:
            P.add("act", lambda s, xb=xb, tb2=tb2: s.activation(
                tb2[:], xb, AF.Square), ins=[xb], outs=[tb2[:]])
            x2b = tb2[:]
        else:
            P.add("act", lambda s, xb=xb, tb2=tb2: s.activation(
                tb2[:], xb, AF.Square), ins=[xb], outs=[tb2[:]])
            x2b = tb2[:]
        P.add("pe", lambda t, x2b=x2b, c=c: t.matmul(
            bankE[0:1, :], ones1b, x2b, start=(c == 0), stop=(c == NCH - 1)),
            ins=[ones1b, x2b], outs=[bankE[0:1, :]])
    P.add("dve", lambda v: v.tensor_scalar_mul(row_m, bankD[0:1, :], 1.0 / D),
          ins=[bankD[0:1, :]], outs=[row_m])
    P.add("act", lambda s: s.activation(row_t, row_m, AF.Square),
          ins=[row_m], outs=[row_t])
    P.add("dve", lambda v: v.tensor_scalar_mul(row_r, bankE[0:1, :], 1.0 / D),
          ins=[bankE[0:1, :]], outs=[row_r])
    P.add("dve", lambda v: v.tensor_sub(row_r, row_r, row_t),
          ins=[row_r, row_t], outs=[row_r])
    P.add("dve", lambda v: v.tensor_scalar_add(row_r, row_r, 1e-5),
          ins=[row_r], outs=[row_r])
    P.add("act", lambda s: s.activation(row_r, row_r, AF.Ln),
          ins=[row_r], outs=[row_r])
    P.add("dve", lambda v: v.tensor_scalar_mul(row_r, row_r, -0.5),
          ins=[row_r], outs=[row_r])
    P.add("act", lambda s: s.activation(row_r, row_r, AF.Exp),
          ins=[row_r], outs=[row_r])
    P.add("dve", lambda v: v.scalar_tensor_tensor(
        row_mr, row_m, -1.0, row_r, ALU.mult, ALU.mult),
        ins=[row_m, row_r], outs=[row_mr])
    P.add("pe", lambda t: t.matmul(bankA[:], ones1x128f, row_r,
                                   start=True, stop=True),
          ins=[ones1x128f, row_r], outs=[bankA[:]])
    P.add("pe", lambda t: t.matmul(bankB[:], ones1x128f, row_mr,
                                   start=True, stop=True),
          ins=[ones1x128f, row_mr], outs=[bankB[:]])
    for c in range(NCH):
        xs = xres[:, c, cs]
        xno = xn_out[:, c, :] if xn_out.shape[2] == QC else xn_out[:, c, cs]
        P.add("dve", lambda v, xs=xs: v.tensor_mul(tmp_f[:], xs, bankA[:]),
              ins=[xs, bankA[:]], outs=[tmp_f[:]])
        if gamma is None:
            P.add("dve", lambda v, xno=xno: v.tensor_add(xno, tmp_f[:], bankB[:]),
                  ins=[tmp_f[:], bankB[:]], outs=[xno])
        else:
            P.add("dve", lambda v: v.tensor_add(tmp_f[:], tmp_f[:], bankB[:]),
                  ins=[tmp_f[:], bankB[:]], outs=[tmp_f[:]])
            g = gamma[:, c:c + 1]
            bta = beta[:, c:c + 1]
            P.add("dve", lambda v, xno=xno, g=g, bta=bta: v.tensor_scalar(
                xno, tmp_f[:], g, bta, ALU.mult, ALU.add),
                ins=[tmp_f[:], gamma[:, :], beta[:, :]], outs=[xno])


def emit_ln(P, ctx, xres, xn, seq_tag, gamma=None, beta=None, src_bf16=False):
    for qc in range(NQC):
        emit_ln_chunk(P, ctx, xres, xn, qc, gamma, beta, src_bf16)


def emit_l0_ffn(P, ctx, xres, xn, wst, seq, inline_ln=False):
    nc = ctx["nc"]
    bankA, bankB = ctx["psA"], ctx["psB"]
    bankA2, bankB2 = ctx["psA2"], ctx["psB2"]
    rot4 = [bankA, bankB, bankA2, bankB2]
    aBuf, bBuf = ctx["ffn_a"], ctx["ffn_b"]
    sil = ctx["tmp_bf"]
    s1, s2, sh1, sh2, s3 = (ctx[k] for k in ("l0_s1", "l0_s2", "l0_sh1",
                                             "l0_sh2", "l0_s3"))
    for tt in range(NQC):
        cs = slice(tt * QC, (tt + 1) * QC)
        if inline_ln:
            emit_ln_chunk(P, ctx, xres, xn[:, :, cs], tt)
        for oc in range(16):
            w1c = wst.fetch(P, "l0_w1t", oc, (tt, seq))
            w2c = wst.fetch(P, "l0_w2t", oc, (tt, seq))
            for sub in range(2):
                m = oc * 2 + sub
                ps1 = bankA
                ps2 = bankB
                for kc in range(NCH):
                    lhs = w1c[:, kc, sub * 128:(sub + 1) * 128]
                    rhs = xn[:, kc, cs]
                    P.add("pe", lambda t, lhs=lhs, rhs=rhs, kc=kc: t.matmul(
                        ps1[:], lhs, rhs, start=(kc == 0), stop=(kc == NCH - 1)),
                        ins=[lhs, rhs], outs=[ps1[:]])
                for kc in range(NCH):
                    lhs = w2c[:, kc, sub * 128:(sub + 1) * 128]
                    rhs = xn[:, kc, cs]
                    P.add("pe", lambda t, lhs=lhs, rhs=rhs, kc=kc: t.matmul(
                        ps2[:], lhs, rhs, start=(kc == 0), stop=(kc == NCH - 1)),
                        ins=[lhs, rhs], outs=[ps2[:]])
                P.add("act", lambda s, m=m: s.activation(
                    sil[:], ps1[:], AF.Silu, scale=s1[:, m:m + 1]),
                    ins=[ps1[:], s1[:, :]], outs=[sil[:]])
                ao = aBuf[:, m, :]
                P.add("dve", lambda v, ao=ao, m=m: v.scalar_tensor_tensor(
                    ao, ps2[:], s2[:, m:m + 1], sil[:], ALU.mult, ALU.mult),
                    ins=[ps2[:], s2[:, :], sil[:]], outs=[ao])
        for m in range(32):
            hc = wst.fetch(P, "l0_h1t", m, (tt, seq))
            ps = rot4[m % 4]
            for kc in range(32):
                lhs = hc[:, kc, :]
                rhs = aBuf[:, kc, :]
                P.add("pe", lambda t, lhs=lhs, rhs=rhs, kc=kc, ps=ps: t.matmul(
                    ps[:], lhs, rhs, start=(kc == 0), stop=(kc == 31)),
                    ins=[lhs, rhs], outs=[ps[:]])
            bo = bBuf[:, m, :]
            P.add("act", lambda s, bo=bo, m=m, ps=ps: s.activation(
                bo, ps[:], AF.Gelu, scale=sh1[:, m:m + 1]),
                ins=[ps[:], sh1[:, :]], outs=[bo])
        for m in range(32):
            hc = wst.fetch(P, "l0_h2t", m, (tt, seq))
            ps = rot4[m % 4]
            for kc in range(32):
                lhs = hc[:, kc, :]
                rhs = bBuf[:, kc, :]
                P.add("pe", lambda t, lhs=lhs, rhs=rhs, kc=kc, ps=ps: t.matmul(
                    ps[:], lhs, rhs, start=(kc == 0), stop=(kc == 31)),
                    ins=[lhs, rhs], outs=[ps[:]])
            co = aBuf[:, m, :]
            P.add("act", lambda s, co=co, m=m, ps=ps: s.activation(
                co, ps[:], AF.Gelu, scale=sh2[:, m:m + 1]),
                ins=[ps[:], sh2[:, :]], outs=[co])
        for m in range(NCH):
            wc = wst.fetch(P, "l0_w3t", m, (tt, seq))
            ps = rot4[m % 4]
            for kc in range(32):
                lhs = wc[:, kc, :]
                rhs = aBuf[:, kc, :]
                P.add("pe", lambda t, lhs=lhs, rhs=rhs, kc=kc, ps=ps: t.matmul(
                    ps[:], lhs, rhs, start=(kc == 0), stop=(kc == 31)),
                    ins=[lhs, rhs], outs=[ps[:]])
            xs = xres[:, m, cs]
            P.add("dve", lambda v, xs=xs, m=m, ps=ps: v.scalar_tensor_tensor(
                xs, ps[:], s3[:, m:m + 1], xs, ALU.mult, ALU.add),
                ins=[ps[:], s3[:, :], xs], outs=[xs])


class WeightStream:

    def __init__(self, ctx, nbufs=4, bufshape=None):
        nc = ctx["nc"]
        stack = ctx["stack"]
        self.ctx = ctx
        self.bufs = []
        if bufshape is None:
            bufshape = [128, 8192]
        for i in range(nbufs):
            t = stack.enter_context(
                nc.sbuf_tensor(f"wstream{i}", bufshape, BF16))
            self.bufs.append(t)
        self.next = 0

    def fetch(self, P, name, oc, tag=None):
        dram = self.ctx[name + "_dram"]
        _, _, kcn, osub = dram.shape
        buf = self.bufs[self.next]
        self.next = (self.next + 1) % len(self.bufs)
        view = buf[:, 0:kcn * osub].rearrange("p (k o) -> p k o", k=kcn)
        src = dram[oc]
        P.add("pool", lambda g, view=view, src=src: g.dma_start(out=view, in_=src),
              ins=[src], outs=[view], dma=True)
        return view



def common_setup(nc, stack, ctx, need=("xnfull",)):
    ctx["nc"] = nc
    ctx["stack"] = stack
    ctx["need"] = set(need)
    ctx["x_in"] = nc.dram_tensor("x_in", [SEQ_PER_CORE, D, S], F32,
                                 kind="ExternalInput").ap()
    ctx["x_out"] = nc.dram_tensor("x_out", [SEQ_PER_CORE, D, S], F32,
                                  kind="ExternalOutput").ap()
    shapes = {"rope_cos": [16, S], "rope_sin": [16, S], "masks": [4, 128, QC],
              "ones_128x1": [128, 1], "ones_1x128": [1, 128],
              "eye64": [64, 64], "eye128": [128, 128]}
    used = ["ones_128x1", "ones_1x128"]
    if "rope" in ctx["need"] or "ropearena" in ctx["need"]:
        used += ["rope_cos", "rope_sin"]
    if "masks" in ctx["need"]:
        used += ["masks"]
    if "eyes" in ctx["need"]:
        used += ["eye64", "eye128"]
    ctx["const_used"] = used
    for nm in used:
        ctx[nm + "_dram"] = nc.dram_tensor(nm, shapes[nm], F32,
                                           kind="ExternalInput").ap()
    ctx["ones128x1_bf"] = stack.enter_context(
        nc.sbuf_tensor("ones128x1_bf_s", [128, 1], BF16))[:]
    ctx["ones_1x128_f"] = stack.enter_context(
        nc.sbuf_tensor("ones_1x128_f_s", [1, 128], F32))[:]
    ctx["ones_1x128_bf"] = stack.enter_context(
        nc.sbuf_tensor("ones_1x128_bf_s", [1, 128], BF16))[:]
    ctx["ones_1x64_f"] = ctx["ones_1x128_f"][:, 0:64]
    ctx["ones_1x64_bf"] = ctx["ones_1x128_bf"][:, 0:64]
    ctx["rzb"] = stack.enter_context(
        nc.sbuf_tensor("rzb_s", [1, QC], BF16))[:]
    ctx["au0"] = stack.enter_context(
        nc.sbuf_tensor("au0_s", [64, QC], BF16))[:]
    ctx["au1"] = stack.enter_context(
        nc.sbuf_tensor("au1_s", [64, QC], BF16))[:]
    ctx["zbb"] = stack.enter_context(
        nc.sbuf_tensor("zbb_s", [64, QC], BF16))[:]
    if "eyes" in ctx["need"]:
        ctx["eye64_bf"] = stack.enter_context(
            nc.sbuf_tensor("eye64_bf_s", [64, 64], BF16))[:]

    if "rope" in ctx["need"]:
        ctx["rope_cos"] = stack.enter_context(
            nc.sbuf_tensor("rope_cos_s", [16, S], BF16))[:]
        ctx["rope_sin"] = stack.enter_context(
            nc.sbuf_tensor("rope_sin_s", [16, S], BF16))[:]
    if "masks" in ctx["need"]:
        ctx["masks"] = stack.enter_context(
            nc.sbuf_tensor("masks_sb", [128, 4, QC], BF16))[:]
    ctx["xres"] = stack.enter_context(
        nc.sbuf_tensor("xres", [128, NCH, S], F32))[:]
    if "xnfull" in ctx["need"]:
        ctx["xn"] = stack.enter_context(
            nc.sbuf_tensor("xn", [128, NCH, S], BF16))[:]
    ctx["tmp_bf"] = stack.enter_context(
        nc.sbuf_tensor("tmp_bf", [128, QC], BF16))
    ctx["tmp_bf2"] = stack.enter_context(
        nc.sbuf_tensor("tmp_bf2", [128, QC], BF16))
    ctx["tmp_f32"] = stack.enter_context(
        nc.sbuf_tensor("tmp_f32", [128, QC], F32))
    if "rowsarena" not in ctx["need"]:
        ctx["row_m"] = stack.enter_context(nc.sbuf_tensor("row_m", [1, QC], F32))
        ctx["row_r"] = stack.enter_context(nc.sbuf_tensor("row_r", [1, QC], F32))
        ctx["row_mr"] = stack.enter_context(nc.sbuf_tensor("row_mr", [1, QC], F32))
        ctx["row_t"] = stack.enter_context(nc.sbuf_tensor("row_t", [1, QC], F32))
    ctx["psA"] = stack.enter_context(nc.psum_tensor("psA", [128, QC], F32))
    ctx["psB"] = stack.enter_context(nc.psum_tensor("psB", [128, QC], F32))
    if "ffn4" in ctx["need"]:
        ctx["psA2"] = stack.enter_context(nc.psum_tensor("psA2", [128, QC], F32))
        ctx["psB2"] = stack.enter_context(nc.psum_tensor("psB2", [128, QC], F32))
    ctx["psC"] = stack.enter_context(nc.psum_tensor("psC", [65, QC], F32))
    ctx["psD"] = stack.enter_context(nc.psum_tensor("psD", [64, QC], F32))
    ctx["psE"] = stack.enter_context(nc.psum_tensor("psE", [64, QC], F32))


def load_common_consts(P, ctx):
    pairs = [("ones_128x1", ctx["ones128x1_bf"], None, "pool"),
             ("ones_1x128", ctx["ones_1x128_f"], None, "sp"),
             ("ones_1x128", ctx["ones_1x128_bf"], None, "pool")]
    if "eyes" in ctx["need"]:
        pairs += [("eye64", ctx["eye64_bf"], None, "pool")]
    if "rope" in ctx["need"]:
        pairs += [("rope_cos", ctx["rope_cos"], None, "pool"),
                  ("rope_sin", ctx["rope_sin"], None, "pool")]
    if "masks" in ctx["need"]:
        pairs += [("masks", ctx["masks"],
                   ctx["masks_dram"].rearrange("m p q -> p m q"), "pool")]
    for nm, dst, src_ap, eng in pairs:
        if src_ap is None:
            src_ap = ctx[nm + "_dram"]
        P.add(eng, lambda e, dst=dst, src_ap=src_ap: e.dma_start(out=dst, in_=src_ap),
              ins=[], outs=[dst], dma=True)
    P.barrier()


def load_resident_weight(P, ctx, name, shape):
    nc, stack = ctx["nc"], ctx["stack"]
    dram = nc.dram_tensor(name, list(shape), F32, kind="ExternalInput").ap()
    sb = stack.enter_context(nc.sbuf_tensor(name + "_sb", list(shape), BF16))[:]
    ctx[name + "_dram"] = dram
    ctx[name] = sb
    P.add("pool", lambda g: g.dma_start(out=sb, in_=dram),
          ins=[], outs=[sb], dma=True)
    return sb


def load_resident_vec(P, ctx, name, ncols):
    nc, stack = ctx["nc"], ctx["stack"]
    dram = nc.dram_tensor(name, [128, ncols], F32, kind="ExternalInput").ap()
    sb = stack.enter_context(nc.sbuf_tensor(name + "_sb", [128, ncols], F32))[:]
    ctx[name + "_dram"] = dram
    ctx[name] = sb
    P.add("sp", lambda e: e.dma_start(out=sb, in_=dram),
          ins=[], outs=[sb], dma=True)
    return sb


def declare_stream_weight(P, ctx, name, shape):
    nc = ctx["nc"]
    ctx[name + "_dram"] = nc.dram_tensor(name, list(shape), F32,
                                         kind="ExternalInput").ap()


def emit_load_x(P, ctx, seq):
    xres = ctx["xres"]
    src = ctx["x_in"][seq].rearrange("(c p) t -> p c t", p=128)
    P.add("sp", lambda e: e.dma_start(out=xres, in_=src),
          ins=[], outs=[xres], dma=True)


def emit_store_x(P, ctx, seq):
    xres = ctx["xres"]
    dst = ctx["x_out"][seq].rearrange("(c p) t -> p c t", p=128)
    P.add("sp", lambda e: e.dma_start(out=dst, in_=xres),
          ins=[xres], outs=[dst, "x_out_done"], dma=True)


def build_l0_ffn_program():
    nc = bass.Bass("TRN2", target_bir_lowering=False)
    stack = ExitStack()
    ctx = {}
    with stack:
        common_setup(nc, stack, ctx, need=("xnfull", "ffn4"))
        declare_stream_weight(None, ctx, "l0_w1t", (16, 128, 8, 256))
        declare_stream_weight(None, ctx, "l0_w2t", (16, 128, 8, 256))
        declare_stream_weight(None, ctx, "l0_h1t", (32, 128, 32, 128))
        declare_stream_weight(None, ctx, "l0_h2t", (32, 128, 32, 128))
        declare_stream_weight(None, ctx, "l0_w3t", (8, 128, 32, 128))
        ctx["ffn_a"] = stack.enter_context(
            nc.sbuf_tensor("ffn_a", [128, 32, QC], BF16))[:]
        ctx["ffn_b"] = stack.enter_context(
            nc.sbuf_tensor("ffn_b", [128, 32, QC], BF16))[:]
        P = Prog()
        for nm in ("l0_s1", "l0_s2", "l0_sh1", "l0_sh2"):
            load_resident_vec(P, ctx, nm, 32)
        load_resident_vec(P, ctx, "l0_s3", 8)
        load_common_consts(P, ctx)
        wst = WeightStream(ctx, nbufs=3, bufshape=[128, 2048])
        for seq in range(SEQ_PER_CORE):
            emit_load_x(P, ctx, seq)
            emit_ln(P, ctx, ctx["xres"], ctx["xn"], f"s{seq}")
            emit_l0_ffn(P, ctx, ctx["xres"], ctx["xn"], wst, seq)
            emit_store_x(P, ctx, seq)
        P.barrier()
        P.emit(nc, stack)
        block = stack.enter_context(nc.Block())
        @block.tensor
        def _(t):
            P.run_engine("pe", t)
        @block.scalar
        def _(s):
            P.run_engine("act", s)
        @block.vector
        def _(v):
            P.run_engine("dve", v)
        @block.gpsimd
        def _(g):
            P.run_engine("pool", g)
        @block.sync
        def _(sy):
            P.run_engine("sp", sy)
    return nc





RSQ96 = 1.0 / math.sqrt(96.0)
CLIP_HI = math.exp(80.0 / math.sqrt(96.0))
CLIP_LO = math.exp(-80.0 / math.sqrt(96.0))


def emit_l0_attn(P, ctx, seq):
    xres = ctx["xres"]
    bankA, bankB = ctx["psA"], ctx["psB"]
    bankC, bankD, bankE = ctx["psC"], ctx["psD"], ctx["psE"]
    psV, psV2 = ctx["psV"], ctx["psV2"]
    cq, ckv = ctx["cq"], ctx["ckv"]
    kr1, kr2 = ctx["kr1"], ctx["kr2"]
    kst, vtm = ctx["kst"], ctx["vtm"]
    qcs_bufs = (ctx["qcs"], ctx["qcs2"])
    qr1s_bufs = (ctx["qr1s"], ctx["qr1s2"])
    qr2s_bufs = (ctx["qr2s"], ctx["qr2s2"])
    rt1, rt2 = ctx["rt1"], ctx["rt2"]
    ebufs = (ctx["e0"], ctx["e1"], ctx["e2"], ctx["e3"])
    afm = ctx["attn_fm"]
    cosv, sinv = ctx["rope_cos"], ctx["rope_sin"]
    masks = ctx["masks"]
    ones1b = ctx["ones128x1_bf"]
    ones1x64b = ctx["ones_1x64_bf"]
    bankB2 = ctx["psB2"]
    zrow, rzb = ctx["row_m"][:], ctx["rzb"]
    zb = ctx["zbb"]
    aubufs = (ctx["au0"], ctx["au1"])
    ctx["_pend"] = ctx.get("_pend", [])
    wkrt = ctx["l0_wkrt"]
    wst = ctx["wst"]

    def rope_pair(psX1, psX2, out1, out2, cols):
        cs_ = cosv[:, cols]
        sn_ = sinv[:, cols]
        P.add("dve", lambda v: v.tensor_mul(rt1[:], psX1[0:16, :], cs_),
              ins=[psX1[0:16, :], cs_], outs=[rt1[:]])
        P.add("dve", lambda v: v.tensor_mul(rt2[:], psX2[0:16, :], sn_),
              ins=[psX2[0:16, :], sn_], outs=[rt2[:]])
        P.add("dve", lambda v: v.tensor_sub(out1, rt1[:], rt2[:]),
              ins=[rt1[:], rt2[:]], outs=[out1])
        P.add("dve", lambda v: v.tensor_mul(rt1[:], psX1[0:16, :], sn_),
              ins=[psX1[0:16, :], sn_], outs=[rt1[:]])
        P.add("dve", lambda v: v.tensor_mul(rt2[:], psX2[0:16, :], cs_),
              ins=[psX2[0:16, :], cs_], outs=[rt2[:]])
        P.add("dve", lambda v: v.tensor_add(out2, rt1[:], rt2[:]),
              ins=[rt1[:], rt2[:]], outs=[out2])

    for qc in range(NQC):
        cs = slice(qc * QC, (qc + 1) * QC)
        xnc = ctx["xnc0"] if qc % 2 == 0 else ctx["xnc1"]
        emit_ln_chunk(P, ctx, xres, xnc[:], qc)
        for ch in range(4):
            ps = bankA if ch % 2 == 0 else bankB
            wdq_c = wst.fetch(P, "l0_wdqt", ch)
            for kc in range(NCH):
                lhs = wdq_c[:, kc, :]
                rhs = xnc[:, kc, :]
                P.add("pe", lambda t, lhs=lhs, rhs=rhs, kc=kc, ps=ps: t.matmul(
                    ps[:], lhs, rhs, start=(kc == 0), stop=(kc == NCH - 1)),
                    ins=[lhs, rhs], outs=[ps[:]])
            dst = cq[:, ch, cs]
            P.add("dve", lambda v, dst=dst, ps=ps: v.tensor_scalar_mul(
                dst, ps[:], RSQ96), ins=[ps[:]], outs=[dst])
        for ch in range(4):
            ps = bankA if ch % 2 == 0 else bankB
            wdkv_c = wst.fetch(P, "l0_wdkvt", ch)
            for kc in range(NCH):
                lhs = wdkv_c[:, kc, :]
                rhs = xnc[:, kc, :]
                P.add("pe", lambda t, lhs=lhs, rhs=rhs, kc=kc, ps=ps: t.matmul(
                    ps[:], lhs, rhs, start=(kc == 0), stop=(kc == NCH - 1)),
                    ins=[lhs, rhs], outs=[ps[:]])
            dst = ckv[:, ch, cs]
            P.add("act", lambda s, dst=dst, ps=ps: s.copy(dst, ps[:]),
                  ins=[ps[:]], outs=[dst])
        for kc in range(NCH):
            lhs1 = wkrt[:, kc, 0:16]
            lhs2 = wkrt[:, kc, 16:32]
            rhs = xnc[:, kc, :]
            P.add("pe", lambda t, lhs1=lhs1, rhs=rhs, kc=kc: t.matmul(
                bankD[0:16, :], lhs1, rhs, start=(kc == 0), stop=(kc == NCH - 1)),
                ins=[lhs1, rhs], outs=[bankD[0:16, :]])
            P.add("pe", lambda t, lhs2=lhs2, rhs=rhs, kc=kc: t.matmul(
                bankE[0:16, :], lhs2, rhs, start=(kc == 0), stop=(kc == NCH - 1)),
                ins=[lhs2, rhs], outs=[bankE[0:16, :]])
        rope_pair(bankD, bankE, kst[64:80, cs], kr2[0:16, cs], cs)

    P.add("sp", lambda e: e.dma_start(out=kst[80:96, :], in_=kr2[0:16, :]),
          ins=[kr2[0:16, :]], outs=[kst[80:96, :]], dma=True)

    for h in range(16):
        def stage(qc):
            wuqt_h = wst.fetch(P, "l0_wuqt", h)
            wqrt_h = wst.fetch(P, "l0_wqrt", h)
            qs_ = slice(qc * QC, (qc + 1) * QC)
            qcs_ = qcs_bufs[qc % 2]
            qr1s_ = qr1s_bufs[qc % 2]
            qr2s_ = qr2s_bufs[qc % 2]
            for lc in range(4):
                lhs = wuqt_h[:, lc, :]
                rhs = cq[:, lc, qs_]
                P.add("pe", lambda t, lhs=lhs, rhs=rhs, lc=lc: t.matmul(
                    bankD[0:64, :], lhs, rhs, start=(lc == 0), stop=(lc == 3)),
                    ins=[lhs, rhs], outs=[bankD[:]])
            P.add("dve", lambda v, qcs_=qcs_: v.tensor_copy(qcs_[0:64, :], bankD[0:64, :]),
                  ins=[bankD[:]], outs=[qcs_[:]])
            qr2s_stage = qr2s_
            for lc in range(4):
                lhs1 = wqrt_h[:, lc, 0:16]
                lhs2 = wqrt_h[:, lc, 16:32]
                rhs = cq[:, lc, qs_]
                P.add("pe", lambda t, lhs1=lhs1, rhs=rhs, lc=lc: t.matmul(
                    bankE[0:16, :], lhs1, rhs, start=(lc == 0), stop=(lc == 3)),
                    ins=[lhs1, rhs], outs=[bankE[:]])
                P.add("pe", lambda t, lhs2=lhs2, rhs=rhs, lc=lc: t.matmul(
                    ctx["psB2"][0:16, :], lhs2, rhs, start=(lc == 0), stop=(lc == 3)),
                    ins=[lhs2, rhs], outs=[ctx["psB2"][:]])
            rope_pair(bankE, ctx["psB2"], qcs_[64:80, :], qr2s_[0:16, :], qs_)
            P.add("sp", lambda e, qcs_=qcs_, qr2s_=qr2s_: e.dma_start(
                out=qcs_[80:96, :], in_=qr2s_[0:16, :]),
                ins=[qr2s_[0:16, :]], outs=[qcs_[80:96, :]], dma=True)

        stage(0)
        wukt_h = wst.fetch(P, "l0_wukt", h)
        for cc in range(NQC):
            ccs = slice(cc * QC, (cc + 1) * QC)
            kb = bankE if cc % 2 == 0 else ctx["psB2"]
            for lc in range(4):
                lhs = wukt_h[:, lc, :]
                rhs = ckv[:, lc, ccs]
                P.add("pe", lambda t, lhs=lhs, rhs=rhs, lc=lc, kb=kb: t.matmul(
                    kb[0:64, :], lhs, rhs, start=(lc == 0), stop=(lc == 3)),
                    ins=[lhs, rhs], outs=[kb[:]])
            dst = kst[0:64, ccs]
            P.add("act", lambda s, dst=dst, kb=kb: s.copy(dst, kb[0:64, :]),
                  ins=[kb[:]], outs=[dst])
        wuvt_h = wst.fetch(P, "l0_wuvt", h)
        for cc in range(NQC):
            ccs = slice(cc * QC, (cc + 1) * QC)
            kb = bankE if cc % 2 == 0 else ctx["psB2"]
            for lc in range(4):
                lhs = wuvt_h[:, lc, :]
                rhs = ckv[:, lc, ccs]
                P.add("pe", lambda t, lhs=lhs, rhs=rhs, lc=lc, kb=kb: t.matmul(
                    kb[0:64, :], lhs, rhs, start=(lc == 0), stop=(lc == 3)),
                    ins=[lhs, rhs], outs=[kb[:]])
            vt = ctx["tmp_bf"] if cc % 2 == 0 else ctx["tmp_bf2"]
            P.add("act", lambda s, vt=vt, kb=kb: s.copy(vt[0:64, :], kb[0:64, :]),
                  ins=[kb[:]], outs=[vt[:]])
            for j in range(4):
                kt = cc * 4 + j
                pv = psV if j % 2 == 0 else psV2
                src_ = vt[0:64, j * 128:(j + 1) * 128]
                P.add("pe", lambda t, src_=src_, pv=pv: t.transpose(
                    pv[:], src_, ctx["eye64_bf"]),
                    ins=[vt[:], ctx["eye64_bf"]], outs=[pv[:]])
                dst = vtm[:, kt, 0:64]
                P.add("act", lambda s, dst=dst, pv=pv: s.copy(dst, pv[:]),
                      ins=[pv[:]], outs=[dst])
        for qc in range(NQC):
            qs = slice(qc * QC, (qc + 1) * QC)
            qcs = qcs_bufs[qc % 2]
            qr1s = qr1s_bufs[qc % 2]
            qr2s = qr2s_bufs[qc % 2]
            nkt = 4 * qc + 4
            sbanks = (bankA, bankB, ctx["psA2"])
            pend_av = []
            for kt in range(nkt):
                ks = slice(kt * 128, (kt + 1) * 128)
                ps = sbanks[kt % 3]
                eb = ebufs[kt % 4][:]
                P.add("pe", lambda t, ks=ks, ps=ps, qcs=qcs: t.matmul(
                    ps[:], kst[0:96, ks], qcs[0:96, :], start=True, stop=True),
                    ins=[kst[0:96, ks], qcs[0:96, :]], outs=[ps[:]])
                if len(pend_av) >= 3:
                    pend_av.pop(0)()
                if kt == 1:
                    for fn in ctx["_pend"]:
                        fn()
                    ctx["_pend"] = []
                    if qc + 1 < NQC:
                        stage(qc + 1)
                P.add("act", lambda s, eb=eb, ps=ps: s.activation(
                    eb, ps[:], AF.Exp), ins=[ps[:]], outs=[eb])
                j = kt - 4 * qc
                if j >= 0:
                    mk = masks[:, j, :]
                    P.add("dve", lambda v, eb=eb, mk=mk: v.scalar_tensor_tensor(
                        eb, eb, CLIP_LO, mk, ALU.max, ALU.mult),
                        ins=[eb, mk], outs=[eb])
                    P.add("dve", lambda v, eb=eb: v.tensor_scalar_min(
                        eb, eb, CLIP_HI), ins=[eb], outs=[eb])
                else:
                    P.add("pool", lambda g, eb=eb: g.tensor_scalar(
                        eb, eb, CLIP_HI, CLIP_LO, ALU.min, ALU.max),
                        ins=[eb], outs=[eb])
                def mk_av(eb=eb, kt=kt, nkt=nkt):
                    P.add("pe", lambda t: t.matmul(
                        bankC[:], vtm[:, kt, :], eb,
                        start=(kt == 0), stop=(kt == nkt - 1)),
                        ins=[vtm[:, kt, :], eb], outs=[bankC[:]])
                pend_av.append(mk_av)
            au = aubufs[(h * NQC + qc) % 2]
            dst = afm[(h % 2) * 64:(h % 2) * 64 + 64, h // 2, qs]

            def tail_fn(tail=tuple(pend_av), au=au):
                for fn in tail:
                    fn()
                P.add("dve", lambda v, au=au: v.tensor_copy(au[0:64, :], bankC[0:64, :]),
                      ins=[bankC[:]], outs=[au[:]])
                P.add("act", lambda s: s.copy(zrow, bankC[64:65, :]),
                      ins=[bankC[:]], outs=[zrow])
                P.add("act", lambda s: s.activation(zrow, zrow, AF.Ln),
                      ins=[zrow], outs=[zrow])
                P.add("dve", lambda v: v.tensor_scalar_mul(zrow, zrow, -1.0),
                      ins=[zrow], outs=[zrow])
                P.add("act", lambda s: s.activation(rzb, zrow, AF.Exp),
                      ins=[zrow], outs=[rzb])
            pend_av = []

            def mk_norm(au=au, dst=dst):
                P.add("pe", lambda t: t.matmul(bankB2[0:64, :], ones1x64b, rzb,
                                               start=True, stop=True),
                      ins=[ones1x64b, rzb], outs=[bankB2[:]])
                P.add("act", lambda s: s.copy(zb[:], bankB2[0:64, :]),
                      ins=[bankB2[:]], outs=[zb[:]])
                P.add("dve", lambda v: v.tensor_mul(dst, au[0:64, :], zb[:]),
                      ins=[au[:], zb[:]], outs=[dst])
            ctx["_pend"] = [tail_fn, mk_norm]
    for fn in ctx["_pend"]:
        fn()
    ctx["_pend"] = []
    for qc in range(NQC):
        qs = slice(qc * QC, (qc + 1) * QC)
        for dch in range(NCH):
            wc = wst.fetch(P, "l0_wot", dch)
            ps = bankA if dch % 2 == 0 else bankB
            for oc in range(NCH):
                lhs = wc[:, oc, :]
                rhs = afm[:, oc, qs]
                P.add("pe", lambda t, lhs=lhs, rhs=rhs, oc=oc, ps=ps: t.matmul(
                    ps[:], lhs, rhs, start=(oc == 0), stop=(oc == NCH - 1)),
                    ins=[lhs, rhs], outs=[ps[:]])
            xs = xres[:, dch, qs]
            P.add("dve", lambda v, xs=xs, ps=ps: v.tensor_add(xs, xs, ps[:]),
                  ins=[xs, ps[:]], outs=[xs])


def build_l0_attn_program():
    nc = bass.Bass("TRN2", target_bir_lowering=False)
    stack = ExitStack()
    ctx = {}
    with stack:
        common_setup(nc, stack, ctx, need=("rope", "masks", "eyes", "ffn4"))
        P = Prog()
        declare_stream_weight(P, ctx, "l0_wdqt", (4, 128, 8, 128))
        declare_stream_weight(P, ctx, "l0_wdkvt", (4, 128, 8, 128))
        declare_stream_weight(P, ctx, "l0_wuqt", (16, 128, 4, 64))
        declare_stream_weight(P, ctx, "l0_wukt", (16, 128, 4, 64))
        declare_stream_weight(P, ctx, "l0_wuvt", (16, 128, 4, 64))
        declare_stream_weight(P, ctx, "l0_wqrt", (16, 128, 4, 32))
        load_resident_weight(P, ctx, "l0_wkrt", (128, 8, 32))
        declare_stream_weight(P, ctx, "l0_wot", (8, 128, 8, 128))
        sb = lambda nm, shp, dt: stack.enter_context(nc.sbuf_tensor(nm, shp, dt))[:]
        ctx["xnc0"] = stack.enter_context(nc.sbuf_tensor("xnc0", [128, NCH, QC], BF16))
        ctx["attn_fm"] = sb("attn_fm", [128, NCH, S], BF16)
        ctx["cq"] = sb("cq", [128, 4, S], BF16)
        ctx["ckv"] = sb("ckv", [128, 4, S], BF16)
        ctx["kr1"] = sb("kr1", [16, S], BF16)
        ctx["kr2"] = sb("kr2", [16, S], BF16)
        ctx["kst"] = sb("kst", [64, S], BF16)
        ctx["vtm"] = sb("vtm", [128, NKT, 65], BF16)
        ctx["qcs"] = sb("qcs", [64, QC], BF16)
        ctx["qcs2"] = sb("qcs2", [64, QC], BF16)
        ctx["qr1s"] = sb("qr1s", [16, QC], BF16)
        ctx["qr1s2"] = sb("qr1s2", [16, QC], BF16)
        ctx["qr2s"] = sb("qr2s", [16, QC], BF16)
        ctx["qr2s2"] = sb("qr2s2", [16, QC], BF16)
        ctx["rt1"] = sb("rt1", [16, QC], F32)
        ctx["rt2"] = sb("rt2", [16, QC], F32)
        ctx["e0"] = sb("e0", [128, QC], BF16)
        ctx["e1"] = sb("e1", [128, QC], BF16)
        ctx["e2"] = sb("e2", [128, QC], BF16)
        ctx["zb"] = sb("zb", [64, QC], F32)
        ctx.setdefault("zbb", ctx["zb"])
        ctx["psVt"] = stack.enter_context(nc.psum_tensor("psV", [128, 128], BF16))
        ctx["psV"] = ctx["psVt"][:, 0:64]
        ctx["psV2"] = ctx["psVt"][:, 0:64]
        ctx["wst"] = WeightStream(ctx, nbufs=3, bufshape=[128, 2048])
        load_common_consts(P, ctx)
        P.add("dve", lambda v, d=ctx["vtm"]: v.memset(d[:, :, 64:65], 1.0),
              ins=[], outs=[ctx["vtm"]])
        for seq in range(SEQ_PER_CORE):
            emit_load_x(P, ctx, seq)
            emit_l0_attn(P, ctx, seq)
            emit_store_x(P, ctx, seq)
        P.barrier()
        P.emit(nc, stack)
        block = stack.enter_context(nc.Block())
        @block.tensor
        def _(t):
            P.run_engine("pe", t)
        @block.scalar
        def _(s):
            P.run_engine("act", s)
        @block.vector
        def _(v):
            P.run_engine("dve", v)
        @block.gpsimd
        def _(g):
            P.run_engine("pool", g)
        @block.sync
        def _(sy):
            P.run_engine("sp", sy)
    return nc





def emit_ln_chunk_into_resid(P, ctx, ysrc, xres, qc, gamma, beta):
    bankA, bankB = ctx["psA"], ctx["psB"]
    bankD, bankE = ctx["psD"], ctx["psE"]
    ones1b = ctx["ones128x1_bf"]
    ones1x128f = ctx["ones_1x128_f"]
    row_m, row_r, row_mr, row_t = (ctx[k][:] for k in
                                   ("row_m", "row_r", "row_mr", "row_t"))
    tmp_f = ctx["tmp_f32"]
    cs = slice(qc * QC, (qc + 1) * QC)
    for c in range(NCH):
        yb = ysrc[:, c, :]
        tb2 = ctx["tmp_bf"] if c % 2 == 0 else ctx["tmp_bf2"]
        P.add("pe", lambda t, yb=yb, c=c: t.matmul(
            bankD[0:1, :], ones1b, yb, start=(c == 0), stop=(c == NCH - 1)),
            ins=[ones1b, yb], outs=[bankD[0:1, :]])
        P.add("act", lambda s, yb=yb, tb2=tb2: s.activation(
            tb2[:], yb, AF.Square), ins=[yb], outs=[tb2[:]])
        P.add("pe", lambda t, tb2=tb2, c=c: t.matmul(
            bankE[0:1, :], ones1b, tb2[:], start=(c == 0), stop=(c == NCH - 1)),
            ins=[ones1b, tb2[:]], outs=[bankE[0:1, :]])
    P.add("dve", lambda v: v.tensor_scalar_mul(row_m, bankD[0:1, :], 1.0 / D),
          ins=[bankD[0:1, :]], outs=[row_m])
    P.add("act", lambda s: s.activation(row_t, row_m, AF.Square),
          ins=[row_m], outs=[row_t])
    P.add("dve", lambda v: v.tensor_scalar_mul(row_r, bankE[0:1, :], 1.0 / D),
          ins=[bankE[0:1, :]], outs=[row_r])
    P.add("dve", lambda v: v.tensor_sub(row_r, row_r, row_t),
          ins=[row_r, row_t], outs=[row_r])
    P.add("dve", lambda v: v.tensor_scalar_add(row_r, row_r, 1e-5),
          ins=[row_r], outs=[row_r])
    P.add("act", lambda s: s.activation(row_r, row_r, AF.Ln),
          ins=[row_r], outs=[row_r])
    P.add("dve", lambda v: v.tensor_scalar_mul(row_r, row_r, -0.5),
          ins=[row_r], outs=[row_r])
    P.add("act", lambda s: s.activation(row_r, row_r, AF.Exp),
          ins=[row_r], outs=[row_r])
    P.add("dve", lambda v: v.scalar_tensor_tensor(
        row_mr, row_m, -1.0, row_r, ALU.mult, ALU.mult),
        ins=[row_m, row_r], outs=[row_mr])
    P.add("pe", lambda t: t.matmul(bankA[:], ones1x128f, row_r,
                                   start=True, stop=True),
          ins=[ones1x128f, row_r], outs=[bankA[:]])
    P.add("pe", lambda t: t.matmul(bankB[:], ones1x128f, row_mr,
                                   start=True, stop=True),
          ins=[ones1x128f, row_mr], outs=[bankB[:]])
    for c in range(NCH):
        yb = ysrc[:, c, :]
        xs = xres[:, c, cs]
        g = gamma[:, c:c + 1]
        bta = beta[:, c:c + 1]
        P.add("dve", lambda v, yb=yb: v.tensor_mul(tmp_f[:], yb, bankA[:]),
              ins=[yb, bankA[:]], outs=[tmp_f[:]])
        P.add("dve", lambda v: v.tensor_add(tmp_f[:], tmp_f[:], bankB[:]),
              ins=[tmp_f[:], bankB[:]], outs=[tmp_f[:]])
        P.add("dve", lambda v, xs=xs, g=g: v.scalar_tensor_tensor(
            tmp_f[:], tmp_f[:], g, xs, ALU.mult, ALU.add),
            ins=[tmp_f[:], gamma[:, :], xs], outs=[tmp_f[:]])
        P.add("dve", lambda v, xs=xs, bta=bta: v.tensor_scalar(
            xs, tmp_f[:], bta, None, ALU.add),
            ins=[tmp_f[:], beta[:, :]], outs=[xs])


def emit_l1_attn(P, ctx, seq):
    xres = ctx["xres"]
    bankA, bankB = ctx["psA"], ctx["psB"]
    bankC, bankD, bankE = ctx["psC"], ctx["psD"], ctx["psE"]
    bankC2 = ctx["psC2"]
    psV, psV2 = ctx["psV"], ctx["psV2"]
    q4 = ctx["q4"]
    kst, vtm = ctx["kst"], ctx["vtm"]
    qst_bufs = (ctx["qcs"], ctx["qcs2"])
    negb_bufs = (ctx["negb"], ctx["negb2"])
    onesS = ctx["onesS"]
    ebufs = (ctx["e0"], ctx["e1"], ctx["e2"], ctx["e3"])
    afm = ctx["attn_fm"]
    ychunk = ctx["ychunk"]
    zrow = ctx["row_m"][:]
    rzb = ctx["rzb"]
    aubufs = (ctx["au0"], ctx["au1"])
    ctx["_pend"] = ctx.get("_pend", [])
    rowq = ctx["row_mr"][:]
    maxk = ctx["maxk_sb"][0:1, 0:1]
    mk2 = ctx["maxk_sb"][0:1, 1:2]
    zb = ctx["zbb"]
    bankB2 = ctx["psB2"]
    wst = ctx["wst"]
    nw, nb = ctx["l1_nw"], ctx["l1_nb"]
    onw, onb = ctx["l1_onw"], ctx["l1_onb"]
    ones1b = ctx["ones128x1_bf"]
    ones1x64b = ctx["ones_1x64_bf"]

    for qc in range(NQC):
        qs = slice(qc * QC, (qc + 1) * QC)
        xn1 = ctx["xnc0"]
        xn2 = ctx["xnc1"]
        emit_ln_chunk(P, ctx, xres, xn1[:], qc)
        emit_ln_chunk(P, ctx, xn1[:], xn2[:], 0, gamma=nw, beta=nb,
                      src_bf16=True)
        for ch in range(4):
            ps = bankA if ch % 2 == 0 else bankB
            wq_c = wst.fetch(P, "l1_wq", ch)
            for kc in range(NCH):
                lhs = wq_c[:, kc, :]
                rhs = xn2[:, kc, :]
                P.add("pe", lambda t, lhs=lhs, rhs=rhs, kc=kc, ps=ps: t.matmul(
                    ps[:], lhs, rhs, start=(kc == 0), stop=(kc == NCH - 1)),
                    ins=[lhs, rhs], outs=[ps[:]])
            dst = q4[:, ch, qs]
            P.add("dve", lambda v, dst=dst, ps=ps: v.tensor_scalar_mul(
                dst, ps[:], 0.125), ins=[ps[:]], outs=[dst])
        wkv_c = wst.fetch(P, "l1_wkv", 0)
        for kc in range(NCH):
            lhs = wkv_c[:, kc, :]
            rhs = xn2[:, kc, :]
            P.add("pe", lambda t, lhs=lhs, rhs=rhs, kc=kc: t.matmul(
                bankC2[:], lhs, rhs, start=(kc == 0), stop=(kc == NCH - 1)),
                ins=[lhs, rhs], outs=[bankC2[:]])
        dstk = kst[0:64, qs]
        P.add("act", lambda s, dstk=dstk: s.copy(dstk, bankC2[0:64, :]),
              ins=[bankC2[:]], outs=[dstk])
        vt = ctx["tmp_bf"]
        P.add("act", lambda s, vt=vt: s.copy(vt[0:64, :], bankC2[64:128, :]),
              ins=[bankC2[:]], outs=[vt[:]])
        for j in range(4):
            kt = qc * 4 + j
            pv = psV if j % 2 == 0 else psV2
            src_ = vt[0:64, j * 128:(j + 1) * 128]
            P.add("pe", lambda t, src_=src_, pv=pv: t.transpose(
                pv[:], src_, ctx["eye64_bf"]),
                ins=[vt[:], ctx["eye64_bf"]], outs=[pv[:]])
            dst = vtm[:, kt, 0:64]
            P.add("act", lambda s, dst=dst, pv=pv: s.copy(dst, pv[:]),
                  ins=[pv[:]], outs=[dst])
        tb2 = ctx["tmp_bf2"]
        P.add("act", lambda s, dstk=dstk, tb2=tb2: s.activation(
            tb2[0:64, :], dstk, AF.Square), ins=[dstk], outs=[tb2[:]])
        P.add("pe", lambda t, tb2=tb2: t.matmul(
            bankE[0:1, :], ones1b[0:64, :], tb2[0:64, :], start=True, stop=True),
            ins=[ones1b, tb2[:]], outs=[bankE[:]])
        if qc == 0:
            P.add("dve", lambda v: v.reduce_max(maxk, bankE[0:1, :], axis=AX.X),
                  ins=[bankE[:]], outs=[maxk])
        else:
            P.add("dve", lambda v: v.reduce_max(mk2, bankE[0:1, :], axis=AX.X),
                  ins=[bankE[:]], outs=[mk2])
            P.add("dve", lambda v: v.tensor_max(maxk, maxk, mk2),
                  ins=[maxk, mk2], outs=[maxk])
    P.add("dve", lambda v: v.memset(kst[64:65, :], 1.0),
          ins=[], outs=[kst[64:65, :]])
    P.add("act", lambda s: s.activation(mk2, maxk, AF.Ln),
          ins=[maxk], outs=[mk2])
    P.add("dve", lambda v: v.tensor_scalar_mul(mk2, mk2, 0.5),
          ins=[mk2], outs=[mk2])
    P.add("act", lambda s: s.activation(mk2, mk2, AF.Exp),
          ins=[mk2], outs=[mk2])

    for qc in range(NQC):
        qs = slice(qc * QC, (qc + 1) * QC)

        def stage_l1(h, sqc=None):
            if sqc is None:
                sqc = qc
            sqs = slice(sqc * QC, (sqc + 1) * QC)
            qst_ = qst_bufs[h % 2]
            negb_ = negb_bufs[h % 2]
            qsrc = q4[(h % 2) * 64:(h % 2) * 64 + 64, h // 2, sqs]
            P.add("dve", lambda v, qsrc=qsrc, qst_=qst_: v.tensor_copy(
                qst_[0:64, :], qsrc), ins=[qsrc], outs=[qst_[:]])
            tbq = ctx["tmp_bf2"]
            P.add("act", lambda s, tbq=tbq, qst_=qst_: s.activation(
                tbq[0:64, :], qst_[0:64, :], AF.Square), ins=[qst_[:]], outs=[tbq[:]])
            P.add("pe", lambda t, tbq=tbq: t.matmul(
                bankE[0:1, :], ones1b[0:64, :], tbq[0:64, :], start=True, stop=True),
                ins=[ones1b, tbq[:]], outs=[bankE[:]])
            P.add("act", lambda s: s.copy(rowq, bankE[0:1, :]),
                  ins=[bankE[:]], outs=[rowq])
            P.add("act", lambda s: s.activation(rowq, rowq, AF.Ln),
                  ins=[rowq], outs=[rowq])
            P.add("dve", lambda v: v.tensor_scalar_mul(rowq, rowq, 0.5),
                  ins=[rowq], outs=[rowq])
            P.add("act", lambda s: s.activation(rowq, rowq, AF.Exp),
                  ins=[rowq], outs=[rowq])
            P.add("dve", lambda v, qst_=qst_: v.tensor_scalar(
                qst_[64:65, :], rowq, mk2, -1.0, ALU.mult, ALU.mult),
                ins=[rowq, ctx["maxk_sb"][0:1, 1:2]], outs=[qst_[64:65, :]])

        if qc == 0:
            stage_l1(0)
        for h in range(8):
            qst = qst_bufs[h % 2]
            negb = negb_bufs[h % 2]
            sbanks = (bankA, bankB, bankC2)
            pend_av = []
            for kt in range(NKT):
                ks = slice(kt * 128, (kt + 1) * 128)
                ps = sbanks[kt % 3]
                eb = ebufs[kt % 4][:]
                P.add("pe", lambda t, ks=ks, ps=ps, qst=qst: t.matmul(
                    ps[:], kst[0:65, ks], qst[0:65, :], start=True, stop=True),
                    ins=[kst[0:65, ks], qst[0:65, :]], outs=[ps[:]])
                if len(pend_av) >= 3:
                    pend_av.pop(0)()
                if kt == 1:
                    for fn in ctx["_pend"]:
                        fn()
                    ctx["_pend"] = []
                    if h + 1 < 8:
                        stage_l1(h + 1)
                P.add("act", lambda s, eb=eb, ps=ps: s.activation(
                    eb, ps[:], AF.Exp), ins=[ps[:]], outs=[eb])
                def mk_av(eb=eb, kt=kt):
                    P.add("pe", lambda t: t.matmul(
                        bankC[:], vtm[:, kt, :], eb,
                        start=(kt == 0), stop=(kt == NKT - 1)),
                        ins=[vtm[:, kt, :], eb], outs=[bankC[:]])
                pend_av.append(mk_av)
            au = aubufs[(h + qc) % 2]
            dst = afm[(h % 2) * 64:(h % 2) * 64 + 64, h // 2, qs]

            def tail_fn(tail=tuple(pend_av), au=au):
                for fn in tail:
                    fn()
                P.add("dve", lambda v, au=au: v.tensor_copy(au[0:64, :], bankC[0:64, :]),
                      ins=[bankC[:]], outs=[au[:]])
                P.add("act", lambda s: s.copy(zrow, bankC[64:65, :]),
                      ins=[bankC[:]], outs=[zrow])
                P.add("act", lambda s: s.activation(zrow, zrow, AF.Ln),
                      ins=[zrow], outs=[zrow])
                P.add("dve", lambda v: v.tensor_scalar_mul(zrow, zrow, -1.0),
                      ins=[zrow], outs=[zrow])
                P.add("act", lambda s: s.activation(rzb, zrow, AF.Exp),
                      ins=[zrow], outs=[rzb])
            pend_av = []

            def mk_norm(au=au, dst=dst):
                P.add("pe", lambda t: t.matmul(bankB2[0:64, :], ones1x64b, rzb,
                                               start=True, stop=True),
                      ins=[ones1x64b, rzb], outs=[bankB2[:]])
                P.add("act", lambda s: s.copy(zb[:], bankB2[0:64, :]),
                      ins=[bankB2[:]], outs=[zb[:]])
                P.add("dve", lambda v: v.tensor_mul(dst, au[0:64, :], zb[:]),
                      ins=[au[:], zb[:]], outs=[dst])
            ctx["_pend"] = [tail_fn, mk_norm]
        for fn in ctx["_pend"]:
            fn()
        ctx["_pend"] = []
        if qc + 1 < NQC:
            stage_l1(0, qc + 1)
        for fn in ctx["_pend"]:
            fn()
        ctx["_pend"] = []
        for dch in range(NCH):
            ps = bankA if dch % 2 == 0 else bankB
            wo_c = wst.fetch(P, "l1_wo", dch)
            for oc in range(4):
                lhs = wo_c[:, oc, :]
                rhs = afm[:, oc, qs]
                P.add("pe", lambda t, lhs=lhs, rhs=rhs, oc=oc, ps=ps: t.matmul(
                    ps[:], lhs, rhs, start=(oc == 0), stop=(oc == 3)),
                    ins=[lhs, rhs], outs=[ps[:]])
            dst = ychunk[:, dch, :]
            P.add("act", lambda s, dst=dst, ps=ps: s.copy(dst, ps[:]),
                  ins=[ps[:]], outs=[dst])
        emit_ln_chunk_into_resid(P, ctx, ychunk, xres, qc, onw, onb)


def build_l1_attn_program():
    nc = bass.Bass("TRN2", target_bir_lowering=False)
    stack = ExitStack()
    ctx = {}
    with stack:
        common_setup(nc, stack, ctx, need=("eyes", "ffn4"))
        P = Prog()
        declare_stream_weight(P, ctx, "l1_wq", (4, 128, 8, 128))
        declare_stream_weight(P, ctx, "l1_wkv", (1, 128, 8, 128))
        declare_stream_weight(P, ctx, "l1_wo", (8, 128, 4, 128))
        for nm in ("l1_nw", "l1_nb", "l1_onw", "l1_onb"):
            load_resident_vec(P, ctx, nm, NCH)
        sb = lambda nm, shp, dt: stack.enter_context(nc.sbuf_tensor(nm, shp, dt))[:]
        ctx["xnc0"] = stack.enter_context(nc.sbuf_tensor("xnc0", [128, NCH, QC], BF16))
        ctx["xnc1"] = stack.enter_context(nc.sbuf_tensor("xnc1", [128, NCH, QC], BF16))
        ctx["q4"] = sb("q4", [128, 4, S], BF16)
        ctx["attn_fm"] = sb("attn_fm", [128, 4, S], BF16)
        ctx["ychunk"] = sb("ychunk", [128, NCH, QC], BF16)
        ctx["kst"] = sb("kst", [64, S], BF16)
        ctx["vtm"] = sb("vtm", [128, NKT, 65], BF16)
        ctx["qcs"] = sb("qcs", [64, QC], BF16)
        ctx["e2"] = sb("e2", [128, QC], BF16)
        ctx["negb"] = sb("negb", [1, QC], BF16)
        ctx["negb2"] = sb("negb2", [1, QC], BF16)
        ctx["qcs2"] = sb("qcs2l1", [64, QC], BF16)
        ctx["onesS"] = sb("onesS", [1, S], BF16)
        ctx["maxk_sb"] = ctx["nc"] and stack.enter_context(nc.sbuf_tensor("maxk_sbx", [1, 2], F32))[:]

        ctx["e0"] = sb("e0", [128, QC], BF16)
        ctx["e1"] = sb("e1", [128, QC], BF16)
        ctx["zb"] = sb("zb", [64, QC], F32)
        ctx.setdefault("zbb", ctx["zb"])
        ctx["psVt"] = stack.enter_context(nc.psum_tensor("psV", [128, 128], BF16))
        ctx["psV"] = ctx["psVt"][:, 0:64]
        ctx["psV2"] = ctx["psVt"][:, 0:64]
        ctx["psC2"] = ctx["psA2"]
        ctx["wst"] = WeightStream(ctx, nbufs=3, bufshape=[128, 4096])
        load_common_consts(P, ctx)
        P.add("dve", lambda v, d=ctx["vtm"]: v.memset(d[:, :, 64:65], 1.0),
              ins=[], outs=[ctx["vtm"]])
        P.add("dve", lambda v: v.memset(ctx["onesS"], 1.0),
              ins=[], outs=[ctx["onesS"]])
        for seq in range(SEQ_PER_CORE):
            emit_load_x(P, ctx, seq)
            emit_l1_attn(P, ctx, seq)
            emit_store_x(P, ctx, seq)
        P.barrier()
        P.emit(nc, stack)
        block = stack.enter_context(nc.Block())
        @block.tensor
        def _(t):
            P.run_engine("pe", t)
        @block.scalar
        def _(s):
            P.run_engine("act", s)
        @block.vector
        def _(v):
            P.run_engine("dve", v)
        @block.gpsimd
        def _(g):
            P.run_engine("pool", g)
        @block.sync
        def _(sy):
            P.run_engine("sp", sy)
    return nc



def emit_l1_ffn(P, ctx, seq):
    xres = ctx["xres"]
    xn = ctx["xn"]
    bankA, bankB = ctx["psA"], ctx["psB"]
    rbuf = ctx["rbuf"]
    xr_bufs = (ctx["xrb"], ctx["xrb2"])
    xk_bufs = (ctx["xkb"], ctx["xkb2"])
    kact = ctx["kact"]
    wst = ctx["wst"]
    mur, muk = ctx["l1_mur"], ctx["l1_muk"]
    onemur, onemuk = ctx["onemur"], ctx["onemuk"]
    tmp_f = ctx["tmp_f32"]
    rot4f = [bankA, bankB, ctx.get("psA2", bankA), ctx.get("psB2", bankB)]

    def lnmix(qc):
        cs = slice(qc * QC, (qc + 1) * QC)
        xr = xr_bufs[qc % 2]
        xk = xk_bufs[qc % 2]
        emit_ln_chunk(P, ctx, xres, xn[:, :, cs], qc)
        for c in range(NCH):
            for (dst, mu, onemu) in ((xr, mur, onemur), (xk, muk, onemuk)):
                t1 = ctx["tmp_bf"]
                xcs = xn[:, c, cs]
                P.add("dve", lambda v, xcs=xcs, mu=mu, c=c, t1=t1: v.tensor_scalar(
                    t1[:], xcs, mu[:, c:c + 1], None, ALU.mult),
                    ins=[xcs, mu[:, :]], outs=[t1[:]])
                d = dst[:, c, :]
                if qc == 0:
                    P.add("dve", lambda v, d=d, t1=t1: v.tensor_copy(d[:, 0:1], t1[:, 0:1]),
                          ins=[t1[:]], outs=[d])
                    xprev = xn[:, c, 0:QC - 1]
                    P.add("dve", lambda v, d=d, xprev=xprev, onemu=onemu, c=c, t1=t1:
                          v.scalar_tensor_tensor(d[:, 1:QC], xprev, onemu[:, c:c + 1],
                                                 t1[:, 1:QC], ALU.mult, ALU.add),
                          ins=[xprev, onemu[:, :], t1[:]], outs=[d])
                else:
                    xprev = xn[:, c, qc * QC - 1:(qc + 1) * QC - 1]
                    P.add("dve", lambda v, d=d, xprev=xprev, onemu=onemu, c=c, t1=t1:
                          v.scalar_tensor_tensor(d, xprev, onemu[:, c:c + 1],
                                                 t1[:], ALU.mult, ALU.add),
                          ins=[xprev, onemu[:, :], t1[:]], outs=[d])

    lnmix(0)
    for qc in range(NQC):
        cs = slice(qc * QC, (qc + 1) * QC)
        xr = xr_bufs[qc % 2]
        xk = xk_bufs[qc % 2]
        for dch in range(NCH):
            ps = rot4f[dch % 4]
            wr_c = wst.fetch(P, "l1_wrt", dch)
            for kc in range(NCH):
                lhs = wr_c[:, kc, :]
                rhs = xr[:, kc, :]
                P.add("pe", lambda t, lhs=lhs, rhs=rhs, kc=kc, ps=ps: t.matmul(
                    ps[:], lhs, rhs, start=(kc == 0), stop=(kc == NCH - 1)),
                    ins=[lhs, rhs], outs=[ps[:]])
            dst = rbuf[:, dch, :]
            P.add("act", lambda s, dst=dst, ps=ps: s.activation(
                dst, ps[:], AF.Sigmoid), ins=[ps[:]], outs=[dst])
            if dch == 0 and qc + 1 < NQC:
                lnmix(qc + 1)
        for dch in range(NCH):
            ps = rot4f[dch % 4]
            wk_c = wst.fetch(P, "l1_wkt", dch)
            for kc in range(NCH):
                lhs = wk_c[:, kc, :]
                rhs = xk[:, kc, :]
                P.add("pe", lambda t, lhs=lhs, rhs=rhs, kc=kc, ps=ps: t.matmul(
                    ps[:], lhs, rhs, start=(kc == 0), stop=(kc == NCH - 1)),
                    ins=[lhs, rhs], outs=[ps[:]])
            tb = ctx["tmp_bf2"]
            P.add("dve", lambda v, tb=tb, ps=ps: v.tensor_scalar_max(
                tb[:], ps[:], 0.0), ins=[ps[:]], outs=[tb[:]])
            dst = kact[:, dch, :]
            P.add("act", lambda s, dst=dst, tb=tb: s.activation(
                dst, tb[:], AF.Square), ins=[tb[:]], outs=[dst])
        for dch in range(NCH):
            ps = rot4f[dch % 4]
            wv_c = wst.fetch(P, "l1_wvt", dch)
            for kc in range(NCH):
                lhs = wv_c[:, kc, :]
                rhs = kact[:, kc, :]
                P.add("pe", lambda t, lhs=lhs, rhs=rhs, kc=kc, ps=ps: t.matmul(
                    ps[:], lhs, rhs, start=(kc == 0), stop=(kc == NCH - 1)),
                    ins=[lhs, rhs], outs=[ps[:]])
            rs = rbuf[:, dch, :]
            xs = xres[:, dch, cs]
            P.add("dve", lambda v, rs=rs, ps=ps: v.tensor_mul(tmp_f[:], ps[:], rs),
                  ins=[ps[:], rs], outs=[tmp_f[:]])
            P.add("dve", lambda v, xs=xs: v.tensor_add(xs, xs, tmp_f[:]),
                  ins=[xs, tmp_f[:]], outs=[xs])


def build_l1_ffn_program():
    phases = ("l0a", "l0f", "l1a", "l1f")
    nc = bass.Bass("TRN2", target_bir_lowering=False)
    stack = ExitStack()
    ctx = {}
    with stack:
        common_setup(nc, stack, ctx, need=("xnfull",))
        P = Prog()
        declare_stream_weight(P, ctx, "l1_wrt", (8, 128, 8, 128))
        declare_stream_weight(P, ctx, "l1_wkt", (8, 128, 8, 128))
        declare_stream_weight(P, ctx, "l1_wvt", (8, 128, 8, 128))
        load_resident_vec(P, ctx, "l1_mur", NCH)
        load_resident_vec(P, ctx, "l1_muk", NCH)
        sb = lambda nm, shp, dt: stack.enter_context(nc.sbuf_tensor(nm, shp, dt))[:]
        ctx["rbuf"] = sb("rbuf", [128, NCH, QC], F32)
        ctx["xrb"] = sb("xrb", [128, NCH, QC], BF16)
        ctx["xkb"] = sb("xkb", [128, NCH, QC], BF16)
        ctx["xrb2"] = sb("xrb2", [128, NCH, QC], BF16)
        ctx["xkb2"] = sb("xkb2", [128, NCH, QC], BF16)
        ctx["kact"] = sb("kact", [128, NCH, QC], BF16)
        ctx["onemur"] = sb("onemur", [128, NCH], F32)
        ctx["onemuk"] = sb("onemuk", [128, NCH], F32)
        ctx["wst"] = WeightStream(ctx, nbufs=3, bufshape=[128, 4096])
        load_common_consts(P, ctx)
        P.add("dve", lambda v: v.tensor_scalar(
            ctx["onemur"], ctx["l1_mur"], -1.0, 1.0, ALU.mult, ALU.add),
            ins=[ctx["l1_mur"]], outs=[ctx["onemur"]])
        P.add("dve", lambda v: v.tensor_scalar(
            ctx["onemuk"], ctx["l1_muk"], -1.0, 1.0, ALU.mult, ALU.add),
            ins=[ctx["l1_muk"]], outs=[ctx["onemuk"]])
        for seq in range(SEQ_PER_CORE):
            emit_load_x(P, ctx, seq)
            if "l1f" in phases:
                emit_l1_ffn(P, ctx, seq)
            emit_store_x(P, ctx, seq)
        P.barrier()
        P.emit(nc, stack)
        block = stack.enter_context(nc.Block())
        @block.tensor
        def _(t):
            P.run_engine("pe", t)
        @block.scalar
        def _(s):
            P.run_engine("act", s)
        @block.vector
        def _(v):
            P.run_engine("dve", v)
        @block.gpsimd
        def _(g):
            P.run_engine("pool", g)
        @block.sync
        def _(sy):
            P.run_engine("sp", sy)
    return nc





def _carve(ctx, arena, start, shape):
    ar = ctx[arena]
    if len(shape) == 3:
        p, a, b = shape
        assert p == 128
        return ar[:, start:start + a * b].rearrange("p (a b) -> p a b", a=a)
    p, n = shape
    return ar[0:p, start:start + n]


def build_fused_program(phases=("l0a", "l0f", "l1a", "l1f")):
    nc = bass.Bass("TRN2", target_bir_lowering=False)
    stack = ExitStack()
    ctx = {}
    with stack:
        common_setup(nc, stack, ctx, need=("eyes", "ffn4", "masks", "ropearena", "rowsarena"))
        ctx["psC2"] = ctx["psA2"]
        ctx["psVt"] = stack.enter_context(nc.psum_tensor("psV", [128, 128], BF16))
        ctx["psV"] = ctx["psVt"][:, 0:64]
        ctx["psV2"] = ctx["psVt"][:, 0:64]
        ctx["arBF_h"] = stack.enter_context(
            nc.sbuf_tensor("arBF", [128, 49152], BF16))
        ctx["arF_h"] = stack.enter_context(
            nc.sbuf_tensor("arF", [128, 3072], F32))
        ctx["tmp_f32b"] = stack.enter_context(
            nc.sbuf_tensor("tmp_f32b", [128, QC], F32))
        ctx["negb"] = stack.enter_context(nc.sbuf_tensor("negb", [1, QC], BF16))[:]
        ctx["negb2"] = stack.enter_context(nc.sbuf_tensor("negb2", [1, QC], BF16))[:]
        ctx["maxk_sb"] = stack.enter_context(nc.sbuf_tensor("maxk_sb", [1, 2], F32))[:]
        P = Prog()
        P.region_pitch["arBF"] = (49152, 512)
        P.region_pitch["psV"] = (128, 64)
        P.region_pitch["arF"] = (3072, 512)
        ctx["row_m"] = _carve(ctx, "arF_h", 0, (1, QC))
        ctx["row_r"] = _carve(ctx, "arF_h", 512, (1, QC))
        ctx["row_mr"] = _carve(ctx, "arF_h", 1024, (1, QC))
        ctx["row_t"] = _carve(ctx, "arF_h", 1536, (1, QC))
        ctx["rt1"] = _carve(ctx, "arF_h", 2048, (16, QC))
        ctx["rt2"] = _carve(ctx, "arF_h", 2560, (16, QC))
        for nm, shp in [
            ("l0_wdqt", (4, 128, 8, 128)), ("l0_wdkvt", (4, 128, 8, 128)),
            ("l0_wuqt", (16, 128, 4, 64)), ("l0_wukt", (16, 128, 4, 64)),
            ("l0_wuvt", (16, 128, 4, 64)), ("l0_wqrt", (16, 128, 4, 32)),
            ("l0_wot", (8, 128, 8, 128)),
            ("l0_w1t", (16, 128, 8, 256)), ("l0_w2t", (16, 128, 8, 256)),
            ("l0_h1t", (32, 128, 32, 128)), ("l0_h2t", (32, 128, 32, 128)),
            ("l0_w3t", (8, 128, 32, 128)),
            ("l1_wq", (4, 128, 8, 128)), ("l1_wkv", (1, 128, 8, 128)),
            ("l1_wo", (8, 128, 4, 128)),
            ("l1_wrt", (8, 128, 8, 128)), ("l1_wkt", (8, 128, 8, 128)),
            ("l1_wvt", (8, 128, 8, 128)),
        ]:
            declare_stream_weight(P, ctx, nm, shp)
        load_resident_weight(P, ctx, "l0_wkrt", (128, 8, 32))
        for nm in ("l0_s1", "l0_s2", "l0_sh1", "l0_sh2"):
            load_resident_vec(P, ctx, nm, 32)
        load_resident_vec(P, ctx, "l0_s3", 8)
        for nm in ("l1_nw", "l1_nb", "l1_onw", "l1_onb", "l1_mur", "l1_muk"):
            load_resident_vec(P, ctx, nm, NCH)
        ctx["onemur"] = stack.enter_context(
            nc.sbuf_tensor("onemur", [128, NCH], F32))[:]
        ctx["onemuk"] = stack.enter_context(
            nc.sbuf_tensor("onemuk", [128, NCH], F32))[:]
        ctx["wst"] = WeightStream(ctx, nbufs=2, bufshape=[128, 4096])
        wst = ctx["wst"]
        load_common_consts(P, ctx)
        P.add("dve", lambda v: v.tensor_scalar(
            ctx["onemur"], ctx["l1_mur"], -1.0, 1.0, ALU.mult, ALU.add),
            ins=[ctx["l1_mur"]], outs=[ctx["onemur"]])
        P.add("dve", lambda v: v.tensor_scalar(
            ctx["onemuk"], ctx["l1_muk"], -1.0, 1.0, ALU.mult, ALU.add),
            ins=[ctx["l1_muk"]], outs=[ctx["onemuk"]])

        for seq in range(SEQ_PER_CORE):
            emit_load_x(P, ctx, seq)
            if "l0a" not in phases:
                pass
            ctx["attn_fm"] = _carve(ctx, "arBF_h", 0, (128, NCH, S))
            ctx["cq"] = _carve(ctx, "arBF_h", 16384, (128, 4, S))
            ctx["ckv"] = _carve(ctx, "arBF_h", 24576, (128, 4, S))
            ctx["kr1"] = _carve(ctx, "arBF_h", 32768, (16, S))
            ctx["kr2"] = _carve(ctx, "arBF_h", 34816, (16, S))
            ctx["kst"] = _carve(ctx, "arBF_h", 36864, (96, S))
            ctx["vtm"] = _carve(ctx, "arBF_h", 38912, (128, NKT, 65))
            ctx["rope_cos"] = _carve(ctx, "arBF_h", 40448, (16, S))
            ctx["rope_sin"] = _carve(ctx, "arBF_h", 42496, (16, S))
            ctx["e0"] = _carve(ctx, "arBF_h", 44544, (128, QC))
            ctx["e1"] = _carve(ctx, "arBF_h", 45056, (128, QC))
            ctx["e2"] = _carve(ctx, "arBF_h", 45568, (128, QC))
            ctx["e3"] = _carve(ctx, "arBF_h", 32768, (128, QC))
            ctx["qcs"] = _carve(ctx, "arBF_h", 46080, (96, QC))
            ctx["qcs2"] = _carve(ctx, "arBF_h", 46592, (96, QC))
            ctx["qr1s"] = _carve(ctx, "arBF_h", 47104, (16, QC))
            ctx["qr1s2"] = _carve(ctx, "arBF_h", 47616, (16, QC))
            ctx["qr2s"] = _carve(ctx, "arBF_h", 48128, (16, QC))
            ctx["qr2s2"] = _carve(ctx, "arBF_h", 48640, (16, QC))
            ctx["xnc0"] = _carve(ctx, "arBF_h", 0, (128, NCH, QC))
            ctx["xnc1"] = _carve(ctx, "arBF_h", 4096, (128, NCH, QC))
            P.add("dve", lambda v, d=ctx["vtm"]: v.memset(d[:, :, 64:65], 1.0),
                  ins=[], outs=[ctx["vtm"]])
            if "l0a" in phases:
                for nm, eng in (("rope_cos", "pool"), ("rope_sin", "pool")):
                    dst, sap = ctx[nm], ctx[nm + "_dram"]
                    P.add(eng, lambda e, dst=dst, sap=sap: e.dma_start(out=dst, in_=sap),
                          ins=[], outs=[dst], dma=True)
            if "l0a" in phases:
                emit_l0_attn(P, ctx, seq)
            ctx["xn"] = _carve(ctx, "arBF_h", 32768, (128, NCH, S))
            ctx["ffn_a"] = _carve(ctx, "arBF_h", 0, (128, 32, QC))
            ctx["ffn_b"] = _carve(ctx, "arBF_h", 16384, (128, 32, QC))
            if "l0f" in phases:
                emit_l0_ffn(P, ctx, ctx["xres"], ctx["xn"], wst, seq,
                            inline_ln=True)
            ctx["q4"] = _carve(ctx, "arBF_h", 0, (128, 4, S))
            ctx["attn_fm"] = _carve(ctx, "arBF_h", 8192, (128, 4, S))
            ctx["ychunk"] = _carve(ctx, "arBF_h", 16384, (128, NCH, QC))
            ctx["kst"] = _carve(ctx, "arBF_h", 20480, (65, S))
            ctx["vtm"] = _carve(ctx, "arBF_h", 22528, (128, NKT, 65))
            ctx["onesS"] = _carve(ctx, "arBF_h", 24064, (1, S))
            ctx["e0"] = _carve(ctx, "arBF_h", 26112, (128, QC))
            ctx["e1"] = _carve(ctx, "arBF_h", 26624, (128, QC))
            ctx["e2"] = _carve(ctx, "arBF_h", 27136, (128, QC))
            ctx["e3"] = _carve(ctx, "arBF_h", 24064, (128, QC))
            ctx["qcs"] = _carve(ctx, "arBF_h", 27648, (65, QC))
            ctx["qcs2"] = _carve(ctx, "arBF_h", 28160, (65, QC))
            ctx["xnc0"] = _carve(ctx, "arBF_h", 28672, (128, NCH, QC))
            ctx["xnc1"] = _carve(ctx, "arBF_h", 32768, (128, NCH, QC))
            if "l1a" in phases:
                P.add("dve", lambda v, d=ctx["onesS"]: v.memset(d, 1.0),
                      ins=[], outs=[ctx["onesS"]])
                P.add("dve", lambda v, d=ctx["vtm"]: v.memset(d[:, :, 64:65], 1.0),
                      ins=[], outs=[ctx["vtm"]])
            if "l1a" in phases:
                emit_l1_attn(P, ctx, seq)
            ctx["xn"] = _carve(ctx, "arBF_h", 0, (128, NCH, S))
            ctx["xrb"] = _carve(ctx, "arBF_h", 16384, (128, NCH, QC))
            ctx["xkb"] = _carve(ctx, "arBF_h", 20480, (128, NCH, QC))
            ctx["kact"] = _carve(ctx, "arBF_h", 24576, (128, NCH, QC))
            ctx["rbuf"] = _carve(ctx, "arBF_h", 28672, (128, NCH, QC))
            ctx["xrb2"] = _carve(ctx, "arBF_h", 32768, (128, NCH, QC))
            ctx["xkb2"] = _carve(ctx, "arBF_h", 36864, (128, NCH, QC))
            if "l1f" in phases:
                emit_l1_ffn(P, ctx, seq)
            emit_store_x(P, ctx, seq)
        P.barrier()
        P.emit(nc, stack)
        block = stack.enter_context(nc.Block())
        @block.tensor
        def _(t):
            P.run_engine("pe", t)
        @block.scalar
        def _(s):
            P.run_engine("act", s)
        @block.vector
        def _(v):
            P.run_engine("dve", v)
        @block.gpsimd
        def _(g):
            P.run_engine("pool", g)
        @block.sync
        def _(sy):
            P.run_engine("sp", sy)
    return nc



FUSED = True

ALL_WEIGHT_NAMES = [
    "l0_wdqt", "l0_wdkvt", "l0_wuqt", "l0_wukt", "l0_wuvt", "l0_wqrt",
    "l0_wkrt", "l0_wot",
    "l0_w1t", "l0_w2t", "l0_h1t", "l0_h2t", "l0_w3t",
    "l0_s1", "l0_s2", "l0_sh1", "l0_sh2", "l0_s3",
    "l1_wq", "l1_wkv", "l1_wo", "l1_nw", "l1_nb", "l1_onw", "l1_onb",
    "l1_wrt", "l1_wkt", "l1_wvt", "l1_mur", "l1_muk",
]
ALL_CONST_NAMES = ["ones_128x1", "ones_1x128", "rope_cos", "rope_sin",
                   "masks", "eye64", "eye128"]


def kernel(**inputs):
    consts = build_constants()
    w = prep_weights(inputs)
    x = np.asarray(inputs["L0_attn_x"])
    xT = np.ascontiguousarray(x.transpose(0, 2, 1))
    cur = [xT[c * SEQ_PER_CORE:(c + 1) * SEQ_PER_CORE] for c in range(NCORES)]

    if FUSED:
        nc = build_fused_program()
        in_maps = []
        for c in range(NCORES):
            m = {"x_in": cur[c]}
            m.update({k: consts[k] for k in ALL_CONST_NAMES})
            for nm in ALL_WEIGHT_NAMES:
                m[nm] = w[nm]
            in_maps.append(m)
        res = run_bass_kernel_spmd(nc, in_maps, core_ids=list(range(NCORES)))
        cur = [np.asarray(r["x_out"]) for r in res.results]
    else:
        launches = [
            (build_l0_attn_program,
             ["l0_wdqt", "l0_wdkvt", "l0_wuqt", "l0_wukt", "l0_wuvt",
              "l0_wqrt", "l0_wkrt", "l0_wot"],
             ["ones_128x1", "ones_1x128", "rope_cos", "rope_sin", "masks",
              "eye64", "eye128"]),
            (build_l0_ffn_program,
             ["l0_w1t", "l0_w2t", "l0_h1t", "l0_h2t", "l0_w3t",
              "l0_s1", "l0_s2", "l0_sh1", "l0_sh2", "l0_s3"],
             ["ones_128x1", "ones_1x128"]),
            (build_l1_attn_program,
             ["l1_wq", "l1_wkv", "l1_wo", "l1_nw", "l1_nb", "l1_onw", "l1_onb"],
             ["ones_128x1", "ones_1x128", "eye64", "eye128"]),
            (build_l1_ffn_program,
             ["l1_wrt", "l1_wkt", "l1_wvt", "l1_mur", "l1_muk"],
             ["ones_128x1", "ones_1x128"]),
        ]
        for build_fn, wnames, cnames in launches:
            nc = build_fn()
            in_maps = []
            for c in range(NCORES):
                m = {"x_in": cur[c]}
                m.update({k: consts[k] for k in cnames})
                for nm in wnames:
                    m[nm] = w[nm]
                in_maps.append(m)
            res = run_bass_kernel_spmd(nc, in_maps, core_ids=list(range(NCORES)))
            cur = [np.asarray(r["x_out"]) for r in res.results]
    outT = np.concatenate(cur, axis=0)
    return np.ascontiguousarray(outT.transpose(0, 2, 1))


# version 22
# speedup vs baseline: 1.0135x; 1.0135x over previous
import math
from contextlib import ExitStack

import numpy as np
import ml_dtypes
import concourse.bass as bass
import concourse.mybir as mybir
from concourse.bass_utils import run_bass_kernel_spmd

F32 = mybir.dt.float32
BF16 = mybir.dt.bfloat16
AF = mybir.ActivationFunctionType
ALU = mybir.AluOpType
AX = mybir.AxisListType

B, S, D = 16, 2048, 1024
NCORES = 8
SEQ_PER_CORE = B // NCORES
NCH = D // 128
QC = 512
NQC = S // QC
NKT = S // 128


COMPUTE_ENGINES = ("pe", "act", "dve", "pool", "sp")


class Prog:

    def __init__(self):
        self.ops = []
        self.last_writer = {}
        self.readers = {}
        self.deps = []
        self.region_pitch = {}

    def _keys(self, a):
        if not hasattr(a, "tensor"):
            return [a]
        nm = a.tensor.name
        rp = self.region_pitch.get(nm)
        if rp is None:
            return [nm]
        pitch, R = rp
        off = a.offset % pitch
        ext = 1
        for step, cnt in a.ap:
            if step >= pitch:
                continue
            ext += (cnt - 1) * step
        return [f"{nm}#{r}" for r in range(off // R, (off + ext - 1) // R + 1)]

    def add(self, engine, fn, ins=(), outs=(), dma=False):
        idx = len(self.ops)
        dep = set()
        rbufs = [k for a in ins for k in self._keys(a)]
        wbufs = [k for a in outs for k in self._keys(a)]
        for b in rbufs:
            w = self.last_writer.get(b)
            if w is not None:
                dep.add(w)
        for b in wbufs:
            w = self.last_writer.get(b)
            if w is not None:
                dep.add(w)
            for r in self.readers.get(b, ()):
                dep.add(r)
        for b in wbufs:
            self.last_writer[b] = idx
            self.readers[b] = []
        for b in rbufs:
            self.readers.setdefault(b, []).append(idx)
        stream = ("dma", wbufs[0]) if dma else engine
        self.ops.append((engine, fn, rbufs, wbufs, dma, stream))
        self.deps.append(dep)
        return idx

    def barrier(self, engine="sp"):
        idx = len(self.ops)
        dep = set()
        per_stream_last = {}
        for i, op in enumerate(self.ops):
            per_stream_last[op[5]] = i
        dep.update(per_stream_last.values())
        self.ops.append((engine, None, [], [], False, engine))
        self.deps.append(dep)
        self.last_writer = {b: idx for b in set(list(self.last_writer) + list(self.readers))}
        self.readers = {}
        return idx

    def emit(self, nc, stack):
        slim_deps = []
        for i, dep in enumerate(self.deps):
            per_stream = {}
            my_eng = self.ops[i][0]
            my_dma = self.ops[i][4]
            for d in dep:
                ds = self.ops[d][5]
                if ds == "pe" and my_eng == "pe" and not my_dma and not self.ops[i][4]:
                    continue
                per_stream[ds] = max(per_stream.get(ds, -1), d)
            slim_deps.append(per_stream)
        need_inc = set()
        for per_stream in slim_deps:
            need_inc.update(per_stream.values())
        streams = []
        for op in self.ops:
            if op[5] not in streams:
                streams.append(op[5])
        sems = {}
        for sidx, s in enumerate(streams):
            nm = s if isinstance(s, str) else f"dma{sidx}"
            sems[s] = stack.enter_context(nc.semaphore(f"S_{nm}"))
        inc_count = {}
        stream_tot = {}
        for i, op in enumerate(self.ops):
            stream = op[5]
            if op[4]:
                need_inc.add(i)
            if i in need_inc:
                step = 16 if op[4] else 1
                stream_tot[stream] = stream_tot.get(stream, 0) + step
                inc_count[i] = stream_tot[stream]
        for s, v in stream_tot.items():
            assert v < 30000, f"sem overflow risk on {s}: {v}"
        per_engine_ops = {e: [] for e in COMPUTE_ENGINES}
        for i, op in enumerate(self.ops):
            per_engine_ops[op[0]].append(i)
        self._sems = sems
        self._inc_count = inc_count
        self._need_inc = need_inc
        self._slim_deps = slim_deps
        self._per_engine_ops = per_engine_ops

    def run_engine(self, engine, builder):
        sems = self._sems
        hw = {}
        for i in self._per_engine_ops[engine]:
            eng, fn, _, _, dma, stream = self.ops[i]
            for pstream, d in sorted(self._slim_deps[i].items(),
                                     key=lambda kv: str(kv[0])):
                v = self._inc_count[d]
                if hw.get(pstream, -1) >= v:
                    continue
                builder.wait_ge(sems[pstream], v)
                hw[pstream] = v
            if fn is None:
                if i in self._need_inc:
                    builder.sem_inc(sems[stream], 1)
                continue
            inst = fn(builder)
            if i in self._need_inc:
                inst.then_inc(sems[stream], 16 if dma else 1)



def pmaj(wT, osub):
    K, O = wT.shape
    kc = K // 128
    nch = O // osub
    return np.ascontiguousarray(
        wT.reshape(kc, 128, nch, osub).transpose(2, 1, 0, 3))


def vec128(v):
    n = v.shape[0] // 128
    return np.ascontiguousarray(v.reshape(n, 128).T)


def build_constants():
    c = {}
    j = np.arange(16, dtype=np.float64)
    inv_freq = 10000.0 ** (-(2 * j) / 32.0)
    ang = np.arange(S, dtype=np.float64)[None, :] * inv_freq[:, None]
    c["rope_cos"] = np.cos(ang).astype(np.float32)
    c["rope_sin"] = np.sin(ang).astype(np.float32)
    masks = np.zeros((4, 128, QC), dtype=np.float32)
    for jj in range(4):
        kk = np.arange(128)[:, None]
        qq = np.arange(QC)[None, :]
        masks[jj] = (qq - jj * 128 - kk >= 0).astype(np.float32)
    c["masks"] = masks
    c["ones_128x1"] = np.ones((128, 1), dtype=np.float32)
    c["ones_1x128"] = np.ones((1, 128), dtype=np.float32)
    c["eye64"] = np.eye(64, dtype=np.float32)
    c["eye128"] = np.eye(128, dtype=np.float32)
    return c


def prep_weights(inp):
    w = {}
    w["l0_wdqt"] = pmaj(np.ascontiguousarray(inp["L0_attn_W_DQ"].T), 128)
    w["l0_wdkvt"] = pmaj(np.ascontiguousarray(inp["L0_attn_W_DKV"].T), 128)
    w["l0_wuqt"] = pmaj(np.ascontiguousarray(inp["L0_attn_W_UQ"].T), 64)
    w["l0_wukt"] = pmaj(np.ascontiguousarray(inp["L0_attn_W_UK"].T), 64)
    w["l0_wuvt"] = pmaj(np.ascontiguousarray(inp["L0_attn_W_UV"].T), 64)
    wqr = inp["L0_attn_W_QR"]
    perm = np.concatenate([np.concatenate([np.arange(h * 32, (h + 1) * 32, 2),
                                           np.arange(h * 32 + 1, (h + 1) * 32, 2)])
                           for h in range(16)])
    wqr_p = wqr[perm]
    w["l0_wqrt"] = pmaj(np.ascontiguousarray(wqr_p.T), 32)
    wkr = inp["L0_attn_W_KR"]
    perm32 = np.concatenate([np.arange(0, 32, 2), np.arange(1, 32, 2)])
    w["l0_wkrt"] = pmaj(np.ascontiguousarray(wkr[perm32].T), 32)[0]
    w["l0_wot"] = pmaj(np.ascontiguousarray(inp["L0_attn_W_O"].T), 128)
    w["l0_w1t"] = pmaj(np.ascontiguousarray(inp["L0_ffn_w1_w"].T), 256)
    w["l0_w2t"] = pmaj(np.ascontiguousarray(inp["L0_ffn_w2_w"].T), 256)
    w["l0_h1t"] = pmaj(np.ascontiguousarray(inp["L0_ffn_h1_w"].T), 128)
    w["l0_h2t"] = pmaj(np.ascontiguousarray(inp["L0_ffn_h2_w"].T), 128)
    w["l0_w3t"] = pmaj(np.ascontiguousarray(inp["L0_ffn_w3_w"].T), 128)
    w["l0_s1"] = vec128(inp["L0_ffn_w1_s"])
    w["l0_s2"] = vec128(inp["L0_ffn_w2_s"])
    w["l0_sh1"] = vec128(inp["L0_ffn_h1_s"])
    w["l0_sh2"] = vec128(inp["L0_ffn_h2_s"])
    w["l0_s3"] = vec128(inp["L0_ffn_w3_s"])
    w["l1_wq"] = pmaj(np.ascontiguousarray(inp["L1_attn_Wq"]), 128)
    w["l1_wkv"] = pmaj(np.ascontiguousarray(inp["L1_attn_Wkv"]), 128)
    w["l1_wo"] = pmaj(np.ascontiguousarray(inp["L1_attn_Wo"]), 128)
    w["l1_nw"] = vec128(inp["L1_attn_norm_w"])
    w["l1_nb"] = vec128(inp["L1_attn_norm_b"])
    w["l1_onw"] = vec128(inp["L1_attn_out_norm_w"])
    w["l1_onb"] = vec128(inp["L1_attn_out_norm_b"])
    w["l1_wrt"] = pmaj(np.ascontiguousarray(inp["L1_ffn_W_r"].T), 128)
    w["l1_wkt"] = pmaj(np.ascontiguousarray(inp["L1_ffn_W_k"].T), 128)
    w["l1_wvt"] = pmaj(np.ascontiguousarray(inp["L1_ffn_W_v"].T), 128)
    w["l1_mur"] = vec128(inp["L1_ffn_mu_r"])
    w["l1_muk"] = vec128(inp["L1_ffn_mu_k"])
    return w



def emit_ln_chunk(P, ctx, xres, xn_out, qc, gamma=None, beta=None,
                  src_bf16=False):
    bankA, bankB = ctx["psA"], ctx["psB"]
    bankD, bankE = ctx["psD"], ctx["psE"]
    ones1b = ctx["ones128x1_bf"]
    ones1x128f = ctx["ones_1x128_f"]
    row_m, row_r, row_mr, row_t = (ctx[k][:] for k in
                                   ("row_m", "row_r", "row_mr", "row_t"))
    tmp_f = ctx["tmp_f32"]
    cs = slice(qc * QC, (qc + 1) * QC)
    for c in range(NCH):
        tb = ctx["tmp_bf"] if c % 2 == 0 else ctx["tmp_bf2"]
        if src_bf16:
            xb = xres[:, c, cs]
        else:
            xs = xres[:, c, cs]
            P.add("dve", lambda v, xs=xs, tb=tb: v.tensor_copy(tb[:], xs),
                  ins=[xs], outs=[tb[:]])
            xb = tb[:]
        P.add("pe", lambda t, xb=xb, c=c: t.matmul(
            bankD[0:1, :], ones1b, xb, start=(c == 0), stop=(c == NCH - 1)),
            ins=[ones1b, xb], outs=[bankD[0:1, :]])
        tb2 = ctx["tmp_bf"] if c % 2 == 1 else ctx["tmp_bf2"]
        if src_bf16:
            P.add("act", lambda s, xb=xb, tb2=tb2: s.activation(
                tb2[:], xb, AF.Square), ins=[xb], outs=[tb2[:]])
            x2b = tb2[:]
        else:
            P.add("act", lambda s, xb=xb, tb2=tb2: s.activation(
                tb2[:], xb, AF.Square), ins=[xb], outs=[tb2[:]])
            x2b = tb2[:]
        P.add("pe", lambda t, x2b=x2b, c=c: t.matmul(
            bankE[0:1, :], ones1b, x2b, start=(c == 0), stop=(c == NCH - 1)),
            ins=[ones1b, x2b], outs=[bankE[0:1, :]])
    P.add("dve", lambda v: v.tensor_scalar_mul(row_m, bankD[0:1, :], 1.0 / D),
          ins=[bankD[0:1, :]], outs=[row_m])
    P.add("act", lambda s: s.activation(row_t, row_m, AF.Square),
          ins=[row_m], outs=[row_t])
    P.add("dve", lambda v: v.tensor_scalar_mul(row_r, bankE[0:1, :], 1.0 / D),
          ins=[bankE[0:1, :]], outs=[row_r])
    P.add("dve", lambda v: v.tensor_sub(row_r, row_r, row_t),
          ins=[row_r, row_t], outs=[row_r])
    P.add("dve", lambda v: v.tensor_scalar_add(row_r, row_r, 1e-5),
          ins=[row_r], outs=[row_r])
    P.add("act", lambda s: s.activation(row_r, row_r, AF.Ln),
          ins=[row_r], outs=[row_r])
    P.add("dve", lambda v: v.tensor_scalar_mul(row_r, row_r, -0.5),
          ins=[row_r], outs=[row_r])
    P.add("act", lambda s: s.activation(row_r, row_r, AF.Exp),
          ins=[row_r], outs=[row_r])
    P.add("dve", lambda v: v.scalar_tensor_tensor(
        row_mr, row_m, -1.0, row_r, ALU.mult, ALU.mult),
        ins=[row_m, row_r], outs=[row_mr])
    P.add("pe", lambda t: t.matmul(bankA[:], ones1x128f, row_r,
                                   start=True, stop=True),
          ins=[ones1x128f, row_r], outs=[bankA[:]])
    P.add("pe", lambda t: t.matmul(bankB[:], ones1x128f, row_mr,
                                   start=True, stop=True),
          ins=[ones1x128f, row_mr], outs=[bankB[:]])
    for c in range(NCH):
        xs = xres[:, c, cs]
        xno = xn_out[:, c, :] if xn_out.shape[2] == QC else xn_out[:, c, cs]
        P.add("dve", lambda v, xs=xs: v.tensor_mul(tmp_f[:], xs, bankA[:]),
              ins=[xs, bankA[:]], outs=[tmp_f[:]])
        if gamma is None:
            P.add("dve", lambda v, xno=xno: v.tensor_add(xno, tmp_f[:], bankB[:]),
                  ins=[tmp_f[:], bankB[:]], outs=[xno])
        else:
            P.add("dve", lambda v: v.tensor_add(tmp_f[:], tmp_f[:], bankB[:]),
                  ins=[tmp_f[:], bankB[:]], outs=[tmp_f[:]])
            g = gamma[:, c:c + 1]
            bta = beta[:, c:c + 1]
            P.add("dve", lambda v, xno=xno, g=g, bta=bta: v.tensor_scalar(
                xno, tmp_f[:], g, bta, ALU.mult, ALU.add),
                ins=[tmp_f[:], gamma[:, :], beta[:, :]], outs=[xno])


def emit_ln(P, ctx, xres, xn, seq_tag, gamma=None, beta=None, src_bf16=False):
    for qc in range(NQC):
        emit_ln_chunk(P, ctx, xres, xn, qc, gamma, beta, src_bf16)


def emit_l0_ffn(P, ctx, xres, xn, wst, seq, inline_ln=False):
    nc = ctx["nc"]
    bankA, bankB = ctx["psA"], ctx["psB"]
    bankA2, bankB2 = ctx["psA2"], ctx["psB2"]
    rot4 = [bankA, bankB, bankA2, bankB2]
    aBuf, bBuf = ctx["ffn_a"], ctx["ffn_b"]
    sil = ctx["tmp_bf"]
    s1, s2, sh1, sh2, s3 = (ctx[k] for k in ("l0_s1", "l0_s2", "l0_sh1",
                                             "l0_sh2", "l0_s3"))
    for tt in range(NQC):
        cs = slice(tt * QC, (tt + 1) * QC)
        if inline_ln:
            emit_ln_chunk(P, ctx, xres, xn[:, :, cs], tt)
        for oc in range(16):
            w1c = wst.fetch(P, "l0_w1t", oc, (tt, seq))
            w2c = wst.fetch(P, "l0_w2t", oc, (tt, seq))
            for sub in range(2):
                m = oc * 2 + sub
                ps1 = bankA
                ps2 = bankB
                for kc in range(NCH):
                    lhs = w1c[:, kc, sub * 128:(sub + 1) * 128]
                    rhs = xn[:, kc, cs]
                    P.add("pe", lambda t, lhs=lhs, rhs=rhs, kc=kc: t.matmul(
                        ps1[:], lhs, rhs, start=(kc == 0), stop=(kc == NCH - 1)),
                        ins=[lhs, rhs], outs=[ps1[:]])
                for kc in range(NCH):
                    lhs = w2c[:, kc, sub * 128:(sub + 1) * 128]
                    rhs = xn[:, kc, cs]
                    P.add("pe", lambda t, lhs=lhs, rhs=rhs, kc=kc: t.matmul(
                        ps2[:], lhs, rhs, start=(kc == 0), stop=(kc == NCH - 1)),
                        ins=[lhs, rhs], outs=[ps2[:]])
                P.add("act", lambda s, m=m: s.activation(
                    sil[:], ps1[:], AF.Silu, scale=s1[:, m:m + 1]),
                    ins=[ps1[:], s1[:, :]], outs=[sil[:]])
                ao = aBuf[:, m, :]
                P.add("dve", lambda v, ao=ao, m=m: v.scalar_tensor_tensor(
                    ao, ps2[:], s2[:, m:m + 1], sil[:], ALU.mult, ALU.mult),
                    ins=[ps2[:], s2[:, :], sil[:]], outs=[ao])
        for m in range(32):
            hc = wst.fetch(P, "l0_h1t", m, (tt, seq))
            ps = rot4[m % 4]
            for kc in range(32):
                lhs = hc[:, kc, :]
                rhs = aBuf[:, kc, :]
                P.add("pe", lambda t, lhs=lhs, rhs=rhs, kc=kc, ps=ps: t.matmul(
                    ps[:], lhs, rhs, start=(kc == 0), stop=(kc == 31)),
                    ins=[lhs, rhs], outs=[ps[:]])
            bo = bBuf[:, m, :]
            P.add("act", lambda s, bo=bo, m=m, ps=ps: s.activation(
                bo, ps[:], AF.Gelu, scale=sh1[:, m:m + 1]),
                ins=[ps[:], sh1[:, :]], outs=[bo])
        for m in range(32):
            hc = wst.fetch(P, "l0_h2t", m, (tt, seq))
            ps = rot4[m % 4]
            for kc in range(32):
                lhs = hc[:, kc, :]
                rhs = bBuf[:, kc, :]
                P.add("pe", lambda t, lhs=lhs, rhs=rhs, kc=kc, ps=ps: t.matmul(
                    ps[:], lhs, rhs, start=(kc == 0), stop=(kc == 31)),
                    ins=[lhs, rhs], outs=[ps[:]])
            co = aBuf[:, m, :]
            P.add("act", lambda s, co=co, m=m, ps=ps: s.activation(
                co, ps[:], AF.Gelu, scale=sh2[:, m:m + 1]),
                ins=[ps[:], sh2[:, :]], outs=[co])
        for m in range(NCH):
            wc = wst.fetch(P, "l0_w3t", m, (tt, seq))
            ps = rot4[m % 4]
            for kc in range(32):
                lhs = wc[:, kc, :]
                rhs = aBuf[:, kc, :]
                P.add("pe", lambda t, lhs=lhs, rhs=rhs, kc=kc, ps=ps: t.matmul(
                    ps[:], lhs, rhs, start=(kc == 0), stop=(kc == 31)),
                    ins=[lhs, rhs], outs=[ps[:]])
            xs = xres[:, m, cs]
            P.add("dve", lambda v, xs=xs, m=m, ps=ps: v.scalar_tensor_tensor(
                xs, ps[:], s3[:, m:m + 1], xs, ALU.mult, ALU.add),
                ins=[ps[:], s3[:, :], xs], outs=[xs])


class WeightStream:

    def __init__(self, ctx, nbufs=4, bufshape=None):
        nc = ctx["nc"]
        stack = ctx["stack"]
        self.ctx = ctx
        self.bufs = []
        if bufshape is None:
            bufshape = [128, 8192]
        for i in range(nbufs):
            t = stack.enter_context(
                nc.sbuf_tensor(f"wstream{i}", bufshape, BF16))
            self.bufs.append(t)
        self.next = 0

    def fetch(self, P, name, oc, tag=None):
        dram = self.ctx[name + "_dram"]
        _, _, kcn, osub = dram.shape
        buf = self.bufs[self.next]
        self.next = (self.next + 1) % len(self.bufs)
        view = buf[:, 0:kcn * osub].rearrange("p (k o) -> p k o", k=kcn)
        src = dram[oc]
        P.add("pool", lambda g, view=view, src=src: g.dma_start(out=view, in_=src),
              ins=[src], outs=[view], dma=True)
        return view



def common_setup(nc, stack, ctx, need=("xnfull",)):
    ctx["nc"] = nc
    ctx["stack"] = stack
    ctx["need"] = set(need)
    ctx["x_in"] = nc.dram_tensor("x_in", [SEQ_PER_CORE, D, S], F32,
                                 kind="ExternalInput").ap()
    ctx["x_out"] = nc.dram_tensor("x_out", [SEQ_PER_CORE, D, S], F32,
                                  kind="ExternalOutput").ap()
    shapes = {"rope_cos": [16, S], "rope_sin": [16, S], "masks": [4, 128, QC],
              "ones_128x1": [128, 1], "ones_1x128": [1, 128],
              "eye64": [64, 64], "eye128": [128, 128]}
    used = ["ones_128x1", "ones_1x128"]
    if "rope" in ctx["need"] or "ropearena" in ctx["need"]:
        used += ["rope_cos", "rope_sin"]
    if "masks" in ctx["need"]:
        used += ["masks"]
    if "eyes" in ctx["need"]:
        used += ["eye64", "eye128"]
    ctx["const_used"] = used
    for nm in used:
        ctx[nm + "_dram"] = nc.dram_tensor(nm, shapes[nm], F32,
                                           kind="ExternalInput").ap()
    ctx["ones128x1_bf"] = stack.enter_context(
        nc.sbuf_tensor("ones128x1_bf_s", [128, 1], BF16))[:]
    ctx["ones_1x128_f"] = stack.enter_context(
        nc.sbuf_tensor("ones_1x128_f_s", [1, 128], F32))[:]
    ctx["ones_1x128_bf"] = stack.enter_context(
        nc.sbuf_tensor("ones_1x128_bf_s", [1, 128], BF16))[:]
    ctx["ones_1x64_f"] = ctx["ones_1x128_f"][:, 0:64]
    ctx["ones_1x64_bf"] = ctx["ones_1x128_bf"][:, 0:64]
    ctx["rzb"] = stack.enter_context(
        nc.sbuf_tensor("rzb_s", [1, QC], BF16))[:]
    ctx["au0"] = stack.enter_context(
        nc.sbuf_tensor("au0_s", [64, QC], BF16))[:]
    ctx["au1"] = stack.enter_context(
        nc.sbuf_tensor("au1_s", [64, QC], BF16))[:]
    ctx["zbb"] = stack.enter_context(
        nc.sbuf_tensor("zbb_s", [64, QC], BF16))[:]
    if "eyes" in ctx["need"]:
        ctx["eye64_bf"] = stack.enter_context(
            nc.sbuf_tensor("eye64_bf_s", [64, 64], BF16))[:]

    if "rope" in ctx["need"]:
        ctx["rope_cos"] = stack.enter_context(
            nc.sbuf_tensor("rope_cos_s", [16, S], BF16))[:]
        ctx["rope_sin"] = stack.enter_context(
            nc.sbuf_tensor("rope_sin_s", [16, S], BF16))[:]
    if "masks" in ctx["need"]:
        ctx["masks"] = stack.enter_context(
            nc.sbuf_tensor("masks_sb", [128, 4, QC], BF16))[:]
    ctx["xres"] = stack.enter_context(
        nc.sbuf_tensor("xres", [128, NCH, S], F32))[:]
    if "xnfull" in ctx["need"]:
        ctx["xn"] = stack.enter_context(
            nc.sbuf_tensor("xn", [128, NCH, S], BF16))[:]
    ctx["tmp_bf"] = stack.enter_context(
        nc.sbuf_tensor("tmp_bf", [128, QC], BF16))
    ctx["tmp_bf2"] = stack.enter_context(
        nc.sbuf_tensor("tmp_bf2", [128, QC], BF16))
    ctx["tmp_f32"] = stack.enter_context(
        nc.sbuf_tensor("tmp_f32", [128, QC], F32))
    if "rowsarena" not in ctx["need"]:
        ctx["row_m"] = stack.enter_context(nc.sbuf_tensor("row_m", [1, QC], F32))
        ctx["row_r"] = stack.enter_context(nc.sbuf_tensor("row_r", [1, QC], F32))
        ctx["row_mr"] = stack.enter_context(nc.sbuf_tensor("row_mr", [1, QC], F32))
        ctx["row_t"] = stack.enter_context(nc.sbuf_tensor("row_t", [1, QC], F32))
    ctx["psA"] = stack.enter_context(nc.psum_tensor("psA", [128, QC], F32))
    ctx["psB"] = stack.enter_context(nc.psum_tensor("psB", [128, QC], F32))
    if "ffn4" in ctx["need"]:
        ctx["psA2"] = stack.enter_context(nc.psum_tensor("psA2", [128, QC], F32))
        ctx["psB2"] = stack.enter_context(nc.psum_tensor("psB2", [128, QC], F32))
    ctx["psC"] = stack.enter_context(nc.psum_tensor("psC", [65, QC], F32))
    ctx["psD"] = stack.enter_context(nc.psum_tensor("psD", [64, QC], F32))
    ctx["psE"] = stack.enter_context(nc.psum_tensor("psE", [64, QC], F32))


def load_common_consts(P, ctx):
    pairs = [("ones_128x1", ctx["ones128x1_bf"], None, "pool"),
             ("ones_1x128", ctx["ones_1x128_f"], None, "sp"),
             ("ones_1x128", ctx["ones_1x128_bf"], None, "pool")]
    if "eyes" in ctx["need"]:
        pairs += [("eye64", ctx["eye64_bf"], None, "pool")]
    if "rope" in ctx["need"]:
        pairs += [("rope_cos", ctx["rope_cos"], None, "pool"),
                  ("rope_sin", ctx["rope_sin"], None, "pool")]
    if "masks" in ctx["need"]:
        pairs += [("masks", ctx["masks"],
                   ctx["masks_dram"].rearrange("m p q -> p m q"), "pool")]
    for nm, dst, src_ap, eng in pairs:
        if src_ap is None:
            src_ap = ctx[nm + "_dram"]
        P.add(eng, lambda e, dst=dst, src_ap=src_ap: e.dma_start(out=dst, in_=src_ap),
              ins=[], outs=[dst], dma=True)
    P.barrier()


def load_resident_weight(P, ctx, name, shape):
    nc, stack = ctx["nc"], ctx["stack"]
    dram = nc.dram_tensor(name, list(shape), F32, kind="ExternalInput").ap()
    sb = stack.enter_context(nc.sbuf_tensor(name + "_sb", list(shape), BF16))[:]
    ctx[name + "_dram"] = dram
    ctx[name] = sb
    P.add("pool", lambda g: g.dma_start(out=sb, in_=dram),
          ins=[], outs=[sb], dma=True)
    return sb


def load_resident_vec(P, ctx, name, ncols):
    nc, stack = ctx["nc"], ctx["stack"]
    dram = nc.dram_tensor(name, [128, ncols], F32, kind="ExternalInput").ap()
    sb = stack.enter_context(nc.sbuf_tensor(name + "_sb", [128, ncols], F32))[:]
    ctx[name + "_dram"] = dram
    ctx[name] = sb
    P.add("sp", lambda e: e.dma_start(out=sb, in_=dram),
          ins=[], outs=[sb], dma=True)
    return sb


def declare_stream_weight(P, ctx, name, shape):
    nc = ctx["nc"]
    ctx[name + "_dram"] = nc.dram_tensor(name, list(shape), F32,
                                         kind="ExternalInput").ap()


def emit_load_x(P, ctx, seq):
    xres = ctx["xres"]
    src = ctx["x_in"][seq].rearrange("(c p) t -> p c t", p=128)
    P.add("sp", lambda e: e.dma_start(out=xres, in_=src),
          ins=[], outs=[xres], dma=True)


def emit_store_x(P, ctx, seq):
    xres = ctx["xres"]
    dst = ctx["x_out"][seq].rearrange("(c p) t -> p c t", p=128)
    P.add("sp", lambda e: e.dma_start(out=dst, in_=xres),
          ins=[xres], outs=[dst, "x_out_done"], dma=True)


def build_l0_ffn_program():
    nc = bass.Bass("TRN2", target_bir_lowering=False)
    stack = ExitStack()
    ctx = {}
    with stack:
        common_setup(nc, stack, ctx, need=("xnfull", "ffn4"))
        declare_stream_weight(None, ctx, "l0_w1t", (16, 128, 8, 256))
        declare_stream_weight(None, ctx, "l0_w2t", (16, 128, 8, 256))
        declare_stream_weight(None, ctx, "l0_h1t", (32, 128, 32, 128))
        declare_stream_weight(None, ctx, "l0_h2t", (32, 128, 32, 128))
        declare_stream_weight(None, ctx, "l0_w3t", (8, 128, 32, 128))
        ctx["ffn_a"] = stack.enter_context(
            nc.sbuf_tensor("ffn_a", [128, 32, QC], BF16))[:]
        ctx["ffn_b"] = stack.enter_context(
            nc.sbuf_tensor("ffn_b", [128, 32, QC], BF16))[:]
        P = Prog()
        for nm in ("l0_s1", "l0_s2", "l0_sh1", "l0_sh2"):
            load_resident_vec(P, ctx, nm, 32)
        load_resident_vec(P, ctx, "l0_s3", 8)
        load_common_consts(P, ctx)
        wst = WeightStream(ctx, nbufs=3, bufshape=[128, 2048])
        for seq in range(SEQ_PER_CORE):
            emit_load_x(P, ctx, seq)
            emit_ln(P, ctx, ctx["xres"], ctx["xn"], f"s{seq}")
            emit_l0_ffn(P, ctx, ctx["xres"], ctx["xn"], wst, seq)
            emit_store_x(P, ctx, seq)
        P.barrier()
        P.emit(nc, stack)
        block = stack.enter_context(nc.Block())
        @block.tensor
        def _(t):
            P.run_engine("pe", t)
        @block.scalar
        def _(s):
            P.run_engine("act", s)
        @block.vector
        def _(v):
            P.run_engine("dve", v)
        @block.gpsimd
        def _(g):
            P.run_engine("pool", g)
        @block.sync
        def _(sy):
            P.run_engine("sp", sy)
    return nc





RSQ96 = 1.0 / math.sqrt(96.0)
CLIP_HI = math.exp(80.0 / math.sqrt(96.0))
CLIP_LO = math.exp(-80.0 / math.sqrt(96.0))


def emit_l0_attn(P, ctx, seq):
    xres = ctx["xres"]
    bankA, bankB = ctx["psA"], ctx["psB"]
    bankC, bankD, bankE = ctx["psC"], ctx["psD"], ctx["psE"]
    psV, psV2 = ctx["psV"], ctx["psV2"]
    cq, ckv = ctx["cq"], ctx["ckv"]
    kr1, kr2 = ctx["kr1"], ctx["kr2"]
    kst, vtm = ctx["kst"], ctx["vtm"]
    qcs_bufs = (ctx["qcs"], ctx["qcs2"])
    qr1s_bufs = (ctx["qr1s"], ctx["qr1s2"])
    qr2s_bufs = (ctx["qr2s"], ctx["qr2s2"])
    rt1, rt2 = ctx["rt1"], ctx["rt2"]
    ebufs = (ctx["e0"], ctx["e1"], ctx["e2"], ctx["e3"])
    afm = ctx["attn_fm"]
    cosv, sinv = ctx["rope_cos"], ctx["rope_sin"]
    masks = ctx["masks"]
    ones1b = ctx["ones128x1_bf"]
    ones1x64b = ctx["ones_1x64_bf"]
    bankB2 = ctx["psB2"]
    zrow, rzb = ctx["row_m"][:], ctx["rzb"]
    zb = ctx["zbb"]
    aubufs = (ctx["au0"], ctx["au1"])
    ctx["_pend"] = ctx.get("_pend", [])
    wkrt = ctx["l0_wkrt"]
    wst = ctx["wst"]

    def rope_pair(psX1, psX2, out1, out2, cols):
        cs_ = cosv[:, cols]
        sn_ = sinv[:, cols]
        P.add("dve", lambda v: v.tensor_mul(rt1[:], psX1[0:16, :], cs_),
              ins=[psX1[0:16, :], cs_], outs=[rt1[:]])
        P.add("dve", lambda v: v.tensor_mul(rt2[:], psX2[0:16, :], sn_),
              ins=[psX2[0:16, :], sn_], outs=[rt2[:]])
        P.add("dve", lambda v: v.tensor_sub(out1, rt1[:], rt2[:]),
              ins=[rt1[:], rt2[:]], outs=[out1])
        P.add("dve", lambda v: v.tensor_mul(rt1[:], psX1[0:16, :], sn_),
              ins=[psX1[0:16, :], sn_], outs=[rt1[:]])
        P.add("dve", lambda v: v.tensor_mul(rt2[:], psX2[0:16, :], cs_),
              ins=[psX2[0:16, :], cs_], outs=[rt2[:]])
        P.add("dve", lambda v: v.tensor_add(out2, rt1[:], rt2[:]),
              ins=[rt1[:], rt2[:]], outs=[out2])

    def lnchunk(qc):
        xnc_ = ctx["xnc0"] if qc % 2 == 0 else ctx["xnc1"]
        emit_ln_chunk(P, ctx, xres, xnc_[:], qc)

    lnchunk(0)
    for qc in range(NQC):
        cs = slice(qc * QC, (qc + 1) * QC)
        xnc = ctx["xnc0"] if qc % 2 == 0 else ctx["xnc1"]
        for ch in range(4):
            ps = ctx["psA2"] if ch % 2 == 0 else ctx["psB2"]
            wdq_c = wst.fetch(P, "l0_wdqt", ch)
            for kc in range(NCH):
                lhs = wdq_c[:, kc, :]
                rhs = xnc[:, kc, :]
                P.add("pe", lambda t, lhs=lhs, rhs=rhs, kc=kc, ps=ps: t.matmul(
                    ps[:], lhs, rhs, start=(kc == 0), stop=(kc == NCH - 1)),
                    ins=[lhs, rhs], outs=[ps[:]])
            dst = cq[:, ch, cs]
            P.add("dve", lambda v, dst=dst, ps=ps: v.tensor_scalar_mul(
                dst, ps[:], RSQ96), ins=[ps[:]], outs=[dst])
            if ch == 0 and qc + 1 < NQC:
                lnchunk(qc + 1)
        for ch in range(4):
            ps = ctx["psA2"] if ch % 2 == 0 else ctx["psB2"]
            wdkv_c = wst.fetch(P, "l0_wdkvt", ch)
            for kc in range(NCH):
                lhs = wdkv_c[:, kc, :]
                rhs = xnc[:, kc, :]
                P.add("pe", lambda t, lhs=lhs, rhs=rhs, kc=kc, ps=ps: t.matmul(
                    ps[:], lhs, rhs, start=(kc == 0), stop=(kc == NCH - 1)),
                    ins=[lhs, rhs], outs=[ps[:]])
            dst = ckv[:, ch, cs]
            P.add("act", lambda s, dst=dst, ps=ps: s.copy(dst, ps[:]),
                  ins=[ps[:]], outs=[dst])
        for kc in range(NCH):
            lhs1 = wkrt[:, kc, 0:16]
            lhs2 = wkrt[:, kc, 16:32]
            rhs = xnc[:, kc, :]
            P.add("pe", lambda t, lhs1=lhs1, rhs=rhs, kc=kc: t.matmul(
                bankD[0:16, :], lhs1, rhs, start=(kc == 0), stop=(kc == NCH - 1)),
                ins=[lhs1, rhs], outs=[bankD[0:16, :]])
            P.add("pe", lambda t, lhs2=lhs2, rhs=rhs, kc=kc: t.matmul(
                bankE[0:16, :], lhs2, rhs, start=(kc == 0), stop=(kc == NCH - 1)),
                ins=[lhs2, rhs], outs=[bankE[0:16, :]])
        rope_pair(bankD, bankE, kst[64:80, cs], kr2[0:16, cs], cs)

    P.add("sp", lambda e: e.dma_start(out=kst[80:96, :], in_=kr2[0:16, :]),
          ins=[kr2[0:16, :]], outs=[kst[80:96, :]], dma=True)

    for h in range(16):
        def stage(qc):
            wuqt_h = wst.fetch(P, "l0_wuqt", h)
            wqrt_h = wst.fetch(P, "l0_wqrt", h)
            qs_ = slice(qc * QC, (qc + 1) * QC)
            qcs_ = qcs_bufs[qc % 2]
            qr1s_ = qr1s_bufs[qc % 2]
            qr2s_ = qr2s_bufs[qc % 2]
            for lc in range(4):
                lhs = wuqt_h[:, lc, :]
                rhs = cq[:, lc, qs_]
                P.add("pe", lambda t, lhs=lhs, rhs=rhs, lc=lc: t.matmul(
                    bankD[0:64, :], lhs, rhs, start=(lc == 0), stop=(lc == 3)),
                    ins=[lhs, rhs], outs=[bankD[:]])
            P.add("dve", lambda v, qcs_=qcs_: v.tensor_copy(qcs_[0:64, :], bankD[0:64, :]),
                  ins=[bankD[:]], outs=[qcs_[:]])
            qr2s_stage = qr2s_
            for lc in range(4):
                lhs1 = wqrt_h[:, lc, 0:16]
                lhs2 = wqrt_h[:, lc, 16:32]
                rhs = cq[:, lc, qs_]
                P.add("pe", lambda t, lhs1=lhs1, rhs=rhs, lc=lc: t.matmul(
                    bankE[0:16, :], lhs1, rhs, start=(lc == 0), stop=(lc == 3)),
                    ins=[lhs1, rhs], outs=[bankE[:]])
                P.add("pe", lambda t, lhs2=lhs2, rhs=rhs, lc=lc: t.matmul(
                    ctx["psB2"][0:16, :], lhs2, rhs, start=(lc == 0), stop=(lc == 3)),
                    ins=[lhs2, rhs], outs=[ctx["psB2"][:]])
            rope_pair(bankE, ctx["psB2"], qcs_[64:80, :], qr2s_[0:16, :], qs_)
            P.add("sp", lambda e, qcs_=qcs_, qr2s_=qr2s_: e.dma_start(
                out=qcs_[80:96, :], in_=qr2s_[0:16, :]),
                ins=[qr2s_[0:16, :]], outs=[qcs_[80:96, :]], dma=True)

        stage(0)
        wukt_h = wst.fetch(P, "l0_wukt", h)
        for cc in range(NQC):
            ccs = slice(cc * QC, (cc + 1) * QC)
            kb = bankE if cc % 2 == 0 else ctx["psB2"]
            for lc in range(4):
                lhs = wukt_h[:, lc, :]
                rhs = ckv[:, lc, ccs]
                P.add("pe", lambda t, lhs=lhs, rhs=rhs, lc=lc, kb=kb: t.matmul(
                    kb[0:64, :], lhs, rhs, start=(lc == 0), stop=(lc == 3)),
                    ins=[lhs, rhs], outs=[kb[:]])
            dst = kst[0:64, ccs]
            P.add("act", lambda s, dst=dst, kb=kb: s.copy(dst, kb[0:64, :]),
                  ins=[kb[:]], outs=[dst])
        wuvt_h = wst.fetch(P, "l0_wuvt", h)
        for cc in range(NQC):
            ccs = slice(cc * QC, (cc + 1) * QC)
            kb = bankE if cc % 2 == 0 else ctx["psB2"]
            for lc in range(4):
                lhs = wuvt_h[:, lc, :]
                rhs = ckv[:, lc, ccs]
                P.add("pe", lambda t, lhs=lhs, rhs=rhs, lc=lc, kb=kb: t.matmul(
                    kb[0:64, :], lhs, rhs, start=(lc == 0), stop=(lc == 3)),
                    ins=[lhs, rhs], outs=[kb[:]])
            vt = ctx["tmp_bf"] if cc % 2 == 0 else ctx["tmp_bf2"]
            P.add("act", lambda s, vt=vt, kb=kb: s.copy(vt[0:64, :], kb[0:64, :]),
                  ins=[kb[:]], outs=[vt[:]])
            for j in range(4):
                kt = cc * 4 + j
                pv = psV if j % 2 == 0 else psV2
                src_ = vt[0:64, j * 128:(j + 1) * 128]
                P.add("pe", lambda t, src_=src_, pv=pv: t.transpose(
                    pv[:], src_, ctx["eye64_bf"]),
                    ins=[vt[:], ctx["eye64_bf"]], outs=[pv[:]])
                dst = vtm[:, kt, 0:64]
                P.add("act", lambda s, dst=dst, pv=pv: s.copy(dst, pv[:]),
                      ins=[pv[:]], outs=[dst])
        for qc in range(NQC):
            qs = slice(qc * QC, (qc + 1) * QC)
            qcs = qcs_bufs[qc % 2]
            qr1s = qr1s_bufs[qc % 2]
            qr2s = qr2s_bufs[qc % 2]
            nkt = 4 * qc + 4
            sbanks = (bankA, bankB, ctx["psA2"])
            pend_av = []
            for kt in range(nkt):
                ks = slice(kt * 128, (kt + 1) * 128)
                ps = sbanks[kt % 3]
                eb = ebufs[kt % 4][:]
                P.add("pe", lambda t, ks=ks, ps=ps, qcs=qcs: t.matmul(
                    ps[:], kst[0:96, ks], qcs[0:96, :], start=True, stop=True),
                    ins=[kst[0:96, ks], qcs[0:96, :]], outs=[ps[:]])
                if len(pend_av) >= 3:
                    pend_av.pop(0)()
                if kt == 1:
                    for fn in ctx["_pend"]:
                        fn()
                    ctx["_pend"] = []
                    if qc + 1 < NQC:
                        stage(qc + 1)
                P.add("act", lambda s, eb=eb, ps=ps: s.activation(
                    eb, ps[:], AF.Exp), ins=[ps[:]], outs=[eb])
                j = kt - 4 * qc
                if j >= 0:
                    mk = masks[:, j, :]
                    P.add("dve", lambda v, eb=eb, mk=mk: v.scalar_tensor_tensor(
                        eb, eb, CLIP_LO, mk, ALU.max, ALU.mult),
                        ins=[eb, mk], outs=[eb])
                    P.add("dve", lambda v, eb=eb: v.tensor_scalar_min(
                        eb, eb, CLIP_HI), ins=[eb], outs=[eb])
                else:
                    P.add("pool", lambda g, eb=eb: g.tensor_scalar(
                        eb, eb, CLIP_HI, CLIP_LO, ALU.min, ALU.max),
                        ins=[eb], outs=[eb])
                def mk_av(eb=eb, kt=kt, nkt=nkt):
                    P.add("pe", lambda t: t.matmul(
                        bankC[:], vtm[:, kt, :], eb,
                        start=(kt == 0), stop=(kt == nkt - 1)),
                        ins=[vtm[:, kt, :], eb], outs=[bankC[:]])
                pend_av.append(mk_av)
            au = aubufs[(h * NQC + qc) % 2]
            dst = afm[(h % 2) * 64:(h % 2) * 64 + 64, h // 2, qs]

            def tail_fn(tail=tuple(pend_av), au=au):
                for fn in tail:
                    fn()
                P.add("dve", lambda v, au=au: v.tensor_copy(au[0:64, :], bankC[0:64, :]),
                      ins=[bankC[:]], outs=[au[:]])
                P.add("act", lambda s: s.copy(zrow, bankC[64:65, :]),
                      ins=[bankC[:]], outs=[zrow])
                P.add("act", lambda s: s.activation(zrow, zrow, AF.Ln),
                      ins=[zrow], outs=[zrow])
                P.add("dve", lambda v: v.tensor_scalar_mul(zrow, zrow, -1.0),
                      ins=[zrow], outs=[zrow])
                P.add("act", lambda s: s.activation(rzb, zrow, AF.Exp),
                      ins=[zrow], outs=[rzb])
            pend_av = []

            def mk_norm(au=au, dst=dst):
                P.add("pe", lambda t: t.matmul(bankB2[0:64, :], ones1x64b, rzb,
                                               start=True, stop=True),
                      ins=[ones1x64b, rzb], outs=[bankB2[:]])
                P.add("act", lambda s: s.copy(zb[:], bankB2[0:64, :]),
                      ins=[bankB2[:]], outs=[zb[:]])
                P.add("dve", lambda v: v.tensor_mul(dst, au[0:64, :], zb[:]),
                      ins=[au[:], zb[:]], outs=[dst])
            ctx["_pend"] = [tail_fn, mk_norm]
    for fn in ctx["_pend"]:
        fn()
    ctx["_pend"] = []
    for qc in range(NQC):
        qs = slice(qc * QC, (qc + 1) * QC)
        for dch in range(NCH):
            wc = wst.fetch(P, "l0_wot", dch)
            ps = bankA if dch % 2 == 0 else bankB
            for oc in range(NCH):
                lhs = wc[:, oc, :]
                rhs = afm[:, oc, qs]
                P.add("pe", lambda t, lhs=lhs, rhs=rhs, oc=oc, ps=ps: t.matmul(
                    ps[:], lhs, rhs, start=(oc == 0), stop=(oc == NCH - 1)),
                    ins=[lhs, rhs], outs=[ps[:]])
            xs = xres[:, dch, qs]
            P.add("dve", lambda v, xs=xs, ps=ps: v.tensor_add(xs, xs, ps[:]),
                  ins=[xs, ps[:]], outs=[xs])


def build_l0_attn_program():
    nc = bass.Bass("TRN2", target_bir_lowering=False)
    stack = ExitStack()
    ctx = {}
    with stack:
        common_setup(nc, stack, ctx, need=("rope", "masks", "eyes", "ffn4"))
        P = Prog()
        declare_stream_weight(P, ctx, "l0_wdqt", (4, 128, 8, 128))
        declare_stream_weight(P, ctx, "l0_wdkvt", (4, 128, 8, 128))
        declare_stream_weight(P, ctx, "l0_wuqt", (16, 128, 4, 64))
        declare_stream_weight(P, ctx, "l0_wukt", (16, 128, 4, 64))
        declare_stream_weight(P, ctx, "l0_wuvt", (16, 128, 4, 64))
        declare_stream_weight(P, ctx, "l0_wqrt", (16, 128, 4, 32))
        load_resident_weight(P, ctx, "l0_wkrt", (128, 8, 32))
        declare_stream_weight(P, ctx, "l0_wot", (8, 128, 8, 128))
        sb = lambda nm, shp, dt: stack.enter_context(nc.sbuf_tensor(nm, shp, dt))[:]
        ctx["xnc0"] = stack.enter_context(nc.sbuf_tensor("xnc0", [128, NCH, QC], BF16))
        ctx["attn_fm"] = sb("attn_fm", [128, NCH, S], BF16)
        ctx["cq"] = sb("cq", [128, 4, S], BF16)
        ctx["ckv"] = sb("ckv", [128, 4, S], BF16)
        ctx["kr1"] = sb("kr1", [16, S], BF16)
        ctx["kr2"] = sb("kr2", [16, S], BF16)
        ctx["kst"] = sb("kst", [64, S], BF16)
        ctx["vtm"] = sb("vtm", [128, NKT, 65], BF16)
        ctx["qcs"] = sb("qcs", [64, QC], BF16)
        ctx["qcs2"] = sb("qcs2", [64, QC], BF16)
        ctx["qr1s"] = sb("qr1s", [16, QC], BF16)
        ctx["qr1s2"] = sb("qr1s2", [16, QC], BF16)
        ctx["qr2s"] = sb("qr2s", [16, QC], BF16)
        ctx["qr2s2"] = sb("qr2s2", [16, QC], BF16)
        ctx["rt1"] = sb("rt1", [16, QC], F32)
        ctx["rt2"] = sb("rt2", [16, QC], F32)
        ctx["e0"] = sb("e0", [128, QC], BF16)
        ctx["e1"] = sb("e1", [128, QC], BF16)
        ctx["e2"] = sb("e2", [128, QC], BF16)
        ctx["zb"] = sb("zb", [64, QC], F32)
        ctx.setdefault("zbb", ctx["zb"])
        ctx["psVt"] = stack.enter_context(nc.psum_tensor("psV", [128, 128], BF16))
        ctx["psV"] = ctx["psVt"][:, 0:64]
        ctx["psV2"] = ctx["psVt"][:, 0:64]
        ctx["wst"] = WeightStream(ctx, nbufs=3, bufshape=[128, 2048])
        load_common_consts(P, ctx)
        P.add("dve", lambda v, d=ctx["vtm"]: v.memset(d[:, :, 64:65], 1.0),
              ins=[], outs=[ctx["vtm"]])
        for seq in range(SEQ_PER_CORE):
            emit_load_x(P, ctx, seq)
            emit_l0_attn(P, ctx, seq)
            emit_store_x(P, ctx, seq)
        P.barrier()
        P.emit(nc, stack)
        block = stack.enter_context(nc.Block())
        @block.tensor
        def _(t):
            P.run_engine("pe", t)
        @block.scalar
        def _(s):
            P.run_engine("act", s)
        @block.vector
        def _(v):
            P.run_engine("dve", v)
        @block.gpsimd
        def _(g):
            P.run_engine("pool", g)
        @block.sync
        def _(sy):
            P.run_engine("sp", sy)
    return nc





def emit_ln_chunk_into_resid(P, ctx, ysrc, xres, qc, gamma, beta):
    bankA, bankB = ctx["psA"], ctx["psB"]
    bankD, bankE = ctx["psD"], ctx["psE"]
    ones1b = ctx["ones128x1_bf"]
    ones1x128f = ctx["ones_1x128_f"]
    row_m, row_r, row_mr, row_t = (ctx[k][:] for k in
                                   ("row_m", "row_r", "row_mr", "row_t"))
    tmp_f = ctx["tmp_f32"]
    cs = slice(qc * QC, (qc + 1) * QC)
    for c in range(NCH):
        yb = ysrc[:, c, :]
        tb2 = ctx["tmp_bf"] if c % 2 == 0 else ctx["tmp_bf2"]
        P.add("pe", lambda t, yb=yb, c=c: t.matmul(
            bankD[0:1, :], ones1b, yb, start=(c == 0), stop=(c == NCH - 1)),
            ins=[ones1b, yb], outs=[bankD[0:1, :]])
        P.add("act", lambda s, yb=yb, tb2=tb2: s.activation(
            tb2[:], yb, AF.Square), ins=[yb], outs=[tb2[:]])
        P.add("pe", lambda t, tb2=tb2, c=c: t.matmul(
            bankE[0:1, :], ones1b, tb2[:], start=(c == 0), stop=(c == NCH - 1)),
            ins=[ones1b, tb2[:]], outs=[bankE[0:1, :]])
    P.add("dve", lambda v: v.tensor_scalar_mul(row_m, bankD[0:1, :], 1.0 / D),
          ins=[bankD[0:1, :]], outs=[row_m])
    P.add("act", lambda s: s.activation(row_t, row_m, AF.Square),
          ins=[row_m], outs=[row_t])
    P.add("dve", lambda v: v.tensor_scalar_mul(row_r, bankE[0:1, :], 1.0 / D),
          ins=[bankE[0:1, :]], outs=[row_r])
    P.add("dve", lambda v: v.tensor_sub(row_r, row_r, row_t),
          ins=[row_r, row_t], outs=[row_r])
    P.add("dve", lambda v: v.tensor_scalar_add(row_r, row_r, 1e-5),
          ins=[row_r], outs=[row_r])
    P.add("act", lambda s: s.activation(row_r, row_r, AF.Ln),
          ins=[row_r], outs=[row_r])
    P.add("dve", lambda v: v.tensor_scalar_mul(row_r, row_r, -0.5),
          ins=[row_r], outs=[row_r])
    P.add("act", lambda s: s.activation(row_r, row_r, AF.Exp),
          ins=[row_r], outs=[row_r])
    P.add("dve", lambda v: v.scalar_tensor_tensor(
        row_mr, row_m, -1.0, row_r, ALU.mult, ALU.mult),
        ins=[row_m, row_r], outs=[row_mr])
    P.add("pe", lambda t: t.matmul(bankA[:], ones1x128f, row_r,
                                   start=True, stop=True),
          ins=[ones1x128f, row_r], outs=[bankA[:]])
    P.add("pe", lambda t: t.matmul(bankB[:], ones1x128f, row_mr,
                                   start=True, stop=True),
          ins=[ones1x128f, row_mr], outs=[bankB[:]])
    for c in range(NCH):
        yb = ysrc[:, c, :]
        xs = xres[:, c, cs]
        g = gamma[:, c:c + 1]
        bta = beta[:, c:c + 1]
        P.add("dve", lambda v, yb=yb: v.tensor_mul(tmp_f[:], yb, bankA[:]),
              ins=[yb, bankA[:]], outs=[tmp_f[:]])
        P.add("dve", lambda v: v.tensor_add(tmp_f[:], tmp_f[:], bankB[:]),
              ins=[tmp_f[:], bankB[:]], outs=[tmp_f[:]])
        P.add("dve", lambda v, xs=xs, g=g: v.scalar_tensor_tensor(
            tmp_f[:], tmp_f[:], g, xs, ALU.mult, ALU.add),
            ins=[tmp_f[:], gamma[:, :], xs], outs=[tmp_f[:]])
        P.add("dve", lambda v, xs=xs, bta=bta: v.tensor_scalar(
            xs, tmp_f[:], bta, None, ALU.add),
            ins=[tmp_f[:], beta[:, :]], outs=[xs])


def emit_l1_attn(P, ctx, seq):
    xres = ctx["xres"]
    bankA, bankB = ctx["psA"], ctx["psB"]
    bankC, bankD, bankE = ctx["psC"], ctx["psD"], ctx["psE"]
    bankC2 = ctx["psC2"]
    psV, psV2 = ctx["psV"], ctx["psV2"]
    q4 = ctx["q4"]
    kst, vtm = ctx["kst"], ctx["vtm"]
    qst_bufs = (ctx["qcs"], ctx["qcs2"])
    negb_bufs = (ctx["negb"], ctx["negb2"])
    onesS = ctx["onesS"]
    ebufs = (ctx["e0"], ctx["e1"], ctx["e2"], ctx["e3"])
    afm = ctx["attn_fm"]
    ychunk = ctx["ychunk"]
    zrow = ctx["row_m"][:]
    rzb = ctx["rzb"]
    aubufs = (ctx["au0"], ctx["au1"])
    ctx["_pend"] = ctx.get("_pend", [])
    rowq = ctx["row_mr"][:]
    maxk = ctx["maxk_sb"][0:1, 0:1]
    mk2 = ctx["maxk_sb"][0:1, 1:2]
    zb = ctx["zbb"]
    bankB2 = ctx["psB2"]
    wst = ctx["wst"]
    nw, nb = ctx["l1_nw"], ctx["l1_nb"]
    onw, onb = ctx["l1_onw"], ctx["l1_onb"]
    ones1b = ctx["ones128x1_bf"]
    ones1x64b = ctx["ones_1x64_bf"]

    for qc in range(NQC):
        qs = slice(qc * QC, (qc + 1) * QC)
        xn1 = ctx["xnc0"]
        xn2 = ctx["xnc1"]
        emit_ln_chunk(P, ctx, xres, xn1[:], qc)
        emit_ln_chunk(P, ctx, xn1[:], xn2[:], 0, gamma=nw, beta=nb,
                      src_bf16=True)
        for ch in range(4):
            ps = bankA if ch % 2 == 0 else bankB
            wq_c = wst.fetch(P, "l1_wq", ch)
            for kc in range(NCH):
                lhs = wq_c[:, kc, :]
                rhs = xn2[:, kc, :]
                P.add("pe", lambda t, lhs=lhs, rhs=rhs, kc=kc, ps=ps: t.matmul(
                    ps[:], lhs, rhs, start=(kc == 0), stop=(kc == NCH - 1)),
                    ins=[lhs, rhs], outs=[ps[:]])
            dst = q4[:, ch, qs]
            P.add("dve", lambda v, dst=dst, ps=ps: v.tensor_scalar_mul(
                dst, ps[:], 0.125), ins=[ps[:]], outs=[dst])
        wkv_c = wst.fetch(P, "l1_wkv", 0)
        for kc in range(NCH):
            lhs = wkv_c[:, kc, :]
            rhs = xn2[:, kc, :]
            P.add("pe", lambda t, lhs=lhs, rhs=rhs, kc=kc: t.matmul(
                bankC2[:], lhs, rhs, start=(kc == 0), stop=(kc == NCH - 1)),
                ins=[lhs, rhs], outs=[bankC2[:]])
        dstk = kst[0:64, qs]
        P.add("act", lambda s, dstk=dstk: s.copy(dstk, bankC2[0:64, :]),
              ins=[bankC2[:]], outs=[dstk])
        vt = ctx["tmp_bf"]
        P.add("act", lambda s, vt=vt: s.copy(vt[0:64, :], bankC2[64:128, :]),
              ins=[bankC2[:]], outs=[vt[:]])
        for j in range(4):
            kt = qc * 4 + j
            pv = psV if j % 2 == 0 else psV2
            src_ = vt[0:64, j * 128:(j + 1) * 128]
            P.add("pe", lambda t, src_=src_, pv=pv: t.transpose(
                pv[:], src_, ctx["eye64_bf"]),
                ins=[vt[:], ctx["eye64_bf"]], outs=[pv[:]])
            dst = vtm[:, kt, 0:64]
            P.add("act", lambda s, dst=dst, pv=pv: s.copy(dst, pv[:]),
                  ins=[pv[:]], outs=[dst])
        tb2 = ctx["tmp_bf2"]
        P.add("act", lambda s, dstk=dstk, tb2=tb2: s.activation(
            tb2[0:64, :], dstk, AF.Square), ins=[dstk], outs=[tb2[:]])
        P.add("pe", lambda t, tb2=tb2: t.matmul(
            bankE[0:1, :], ones1b[0:64, :], tb2[0:64, :], start=True, stop=True),
            ins=[ones1b, tb2[:]], outs=[bankE[:]])
        if qc == 0:
            P.add("dve", lambda v: v.reduce_max(maxk, bankE[0:1, :], axis=AX.X),
                  ins=[bankE[:]], outs=[maxk])
        else:
            P.add("dve", lambda v: v.reduce_max(mk2, bankE[0:1, :], axis=AX.X),
                  ins=[bankE[:]], outs=[mk2])
            P.add("dve", lambda v: v.tensor_max(maxk, maxk, mk2),
                  ins=[maxk, mk2], outs=[maxk])
    P.add("dve", lambda v: v.memset(kst[64:65, :], 1.0),
          ins=[], outs=[kst[64:65, :]])
    P.add("act", lambda s: s.activation(mk2, maxk, AF.Ln),
          ins=[maxk], outs=[mk2])
    P.add("dve", lambda v: v.tensor_scalar_mul(mk2, mk2, 0.5),
          ins=[mk2], outs=[mk2])
    P.add("act", lambda s: s.activation(mk2, mk2, AF.Exp),
          ins=[mk2], outs=[mk2])

    for qc in range(NQC):
        qs = slice(qc * QC, (qc + 1) * QC)

        def stage_l1(h, sqc=None):
            if sqc is None:
                sqc = qc
            sqs = slice(sqc * QC, (sqc + 1) * QC)
            qst_ = qst_bufs[h % 2]
            negb_ = negb_bufs[h % 2]
            qsrc = q4[(h % 2) * 64:(h % 2) * 64 + 64, h // 2, sqs]
            P.add("dve", lambda v, qsrc=qsrc, qst_=qst_: v.tensor_copy(
                qst_[0:64, :], qsrc), ins=[qsrc], outs=[qst_[:]])
            tbq = ctx["tmp_bf2"]
            P.add("act", lambda s, tbq=tbq, qst_=qst_: s.activation(
                tbq[0:64, :], qst_[0:64, :], AF.Square), ins=[qst_[:]], outs=[tbq[:]])
            P.add("pe", lambda t, tbq=tbq: t.matmul(
                bankE[0:1, :], ones1b[0:64, :], tbq[0:64, :], start=True, stop=True),
                ins=[ones1b, tbq[:]], outs=[bankE[:]])
            P.add("act", lambda s: s.copy(rowq, bankE[0:1, :]),
                  ins=[bankE[:]], outs=[rowq])
            P.add("act", lambda s: s.activation(rowq, rowq, AF.Ln),
                  ins=[rowq], outs=[rowq])
            P.add("dve", lambda v: v.tensor_scalar_mul(rowq, rowq, 0.5),
                  ins=[rowq], outs=[rowq])
            P.add("act", lambda s: s.activation(rowq, rowq, AF.Exp),
                  ins=[rowq], outs=[rowq])
            P.add("dve", lambda v, qst_=qst_: v.tensor_scalar(
                qst_[64:65, :], rowq, mk2, -1.0, ALU.mult, ALU.mult),
                ins=[rowq, ctx["maxk_sb"][0:1, 1:2]], outs=[qst_[64:65, :]])

        if qc == 0:
            stage_l1(0)
        for h in range(8):
            qst = qst_bufs[h % 2]
            negb = negb_bufs[h % 2]
            sbanks = (bankA, bankB, bankC2)
            pend_av = []
            for kt in range(NKT):
                ks = slice(kt * 128, (kt + 1) * 128)
                ps = sbanks[kt % 3]
                eb = ebufs[kt % 4][:]
                P.add("pe", lambda t, ks=ks, ps=ps, qst=qst: t.matmul(
                    ps[:], kst[0:65, ks], qst[0:65, :], start=True, stop=True),
                    ins=[kst[0:65, ks], qst[0:65, :]], outs=[ps[:]])
                if len(pend_av) >= 3:
                    pend_av.pop(0)()
                if kt == 1:
                    for fn in ctx["_pend"]:
                        fn()
                    ctx["_pend"] = []
                    if h + 1 < 8:
                        stage_l1(h + 1)
                P.add("act", lambda s, eb=eb, ps=ps: s.activation(
                    eb, ps[:], AF.Exp), ins=[ps[:]], outs=[eb])
                def mk_av(eb=eb, kt=kt):
                    P.add("pe", lambda t: t.matmul(
                        bankC[:], vtm[:, kt, :], eb,
                        start=(kt == 0), stop=(kt == NKT - 1)),
                        ins=[vtm[:, kt, :], eb], outs=[bankC[:]])
                pend_av.append(mk_av)
            au = aubufs[(h + qc) % 2]
            dst = afm[(h % 2) * 64:(h % 2) * 64 + 64, h // 2, qs]

            def tail_fn(tail=tuple(pend_av), au=au):
                for fn in tail:
                    fn()
                P.add("dve", lambda v, au=au: v.tensor_copy(au[0:64, :], bankC[0:64, :]),
                      ins=[bankC[:]], outs=[au[:]])
                P.add("act", lambda s: s.copy(zrow, bankC[64:65, :]),
                      ins=[bankC[:]], outs=[zrow])
                P.add("act", lambda s: s.activation(zrow, zrow, AF.Ln),
                      ins=[zrow], outs=[zrow])
                P.add("dve", lambda v: v.tensor_scalar_mul(zrow, zrow, -1.0),
                      ins=[zrow], outs=[zrow])
                P.add("act", lambda s: s.activation(rzb, zrow, AF.Exp),
                      ins=[zrow], outs=[rzb])
            pend_av = []

            def mk_norm(au=au, dst=dst):
                P.add("pe", lambda t: t.matmul(bankB2[0:64, :], ones1x64b, rzb,
                                               start=True, stop=True),
                      ins=[ones1x64b, rzb], outs=[bankB2[:]])
                P.add("act", lambda s: s.copy(zb[:], bankB2[0:64, :]),
                      ins=[bankB2[:]], outs=[zb[:]])
                P.add("dve", lambda v: v.tensor_mul(dst, au[0:64, :], zb[:]),
                      ins=[au[:], zb[:]], outs=[dst])
            ctx["_pend"] = [tail_fn, mk_norm]
        for fn in ctx["_pend"]:
            fn()
        ctx["_pend"] = []
        if qc + 1 < NQC:
            stage_l1(0, qc + 1)
        for fn in ctx["_pend"]:
            fn()
        ctx["_pend"] = []
        for dch in range(NCH):
            ps = bankA if dch % 2 == 0 else bankB
            wo_c = wst.fetch(P, "l1_wo", dch)
            for oc in range(4):
                lhs = wo_c[:, oc, :]
                rhs = afm[:, oc, qs]
                P.add("pe", lambda t, lhs=lhs, rhs=rhs, oc=oc, ps=ps: t.matmul(
                    ps[:], lhs, rhs, start=(oc == 0), stop=(oc == 3)),
                    ins=[lhs, rhs], outs=[ps[:]])
            dst = ychunk[:, dch, :]
            P.add("act", lambda s, dst=dst, ps=ps: s.copy(dst, ps[:]),
                  ins=[ps[:]], outs=[dst])
        emit_ln_chunk_into_resid(P, ctx, ychunk, xres, qc, onw, onb)


def build_l1_attn_program():
    nc = bass.Bass("TRN2", target_bir_lowering=False)
    stack = ExitStack()
    ctx = {}
    with stack:
        common_setup(nc, stack, ctx, need=("eyes", "ffn4"))
        P = Prog()
        declare_stream_weight(P, ctx, "l1_wq", (4, 128, 8, 128))
        declare_stream_weight(P, ctx, "l1_wkv", (1, 128, 8, 128))
        declare_stream_weight(P, ctx, "l1_wo", (8, 128, 4, 128))
        for nm in ("l1_nw", "l1_nb", "l1_onw", "l1_onb"):
            load_resident_vec(P, ctx, nm, NCH)
        sb = lambda nm, shp, dt: stack.enter_context(nc.sbuf_tensor(nm, shp, dt))[:]
        ctx["xnc0"] = stack.enter_context(nc.sbuf_tensor("xnc0", [128, NCH, QC], BF16))
        ctx["xnc1"] = stack.enter_context(nc.sbuf_tensor("xnc1", [128, NCH, QC], BF16))
        ctx["q4"] = sb("q4", [128, 4, S], BF16)
        ctx["attn_fm"] = sb("attn_fm", [128, 4, S], BF16)
        ctx["ychunk"] = sb("ychunk", [128, NCH, QC], BF16)
        ctx["kst"] = sb("kst", [64, S], BF16)
        ctx["vtm"] = sb("vtm", [128, NKT, 65], BF16)
        ctx["qcs"] = sb("qcs", [64, QC], BF16)
        ctx["e2"] = sb("e2", [128, QC], BF16)
        ctx["negb"] = sb("negb", [1, QC], BF16)
        ctx["negb2"] = sb("negb2", [1, QC], BF16)
        ctx["qcs2"] = sb("qcs2l1", [64, QC], BF16)
        ctx["onesS"] = sb("onesS", [1, S], BF16)
        ctx["maxk_sb"] = ctx["nc"] and stack.enter_context(nc.sbuf_tensor("maxk_sbx", [1, 2], F32))[:]

        ctx["e0"] = sb("e0", [128, QC], BF16)
        ctx["e1"] = sb("e1", [128, QC], BF16)
        ctx["zb"] = sb("zb", [64, QC], F32)
        ctx.setdefault("zbb", ctx["zb"])
        ctx["psVt"] = stack.enter_context(nc.psum_tensor("psV", [128, 128], BF16))
        ctx["psV"] = ctx["psVt"][:, 0:64]
        ctx["psV2"] = ctx["psVt"][:, 0:64]
        ctx["psC2"] = ctx["psA2"]
        ctx["wst"] = WeightStream(ctx, nbufs=3, bufshape=[128, 4096])
        load_common_consts(P, ctx)
        P.add("dve", lambda v, d=ctx["vtm"]: v.memset(d[:, :, 64:65], 1.0),
              ins=[], outs=[ctx["vtm"]])
        P.add("dve", lambda v: v.memset(ctx["onesS"], 1.0),
              ins=[], outs=[ctx["onesS"]])
        for seq in range(SEQ_PER_CORE):
            emit_load_x(P, ctx, seq)
            emit_l1_attn(P, ctx, seq)
            emit_store_x(P, ctx, seq)
        P.barrier()
        P.emit(nc, stack)
        block = stack.enter_context(nc.Block())
        @block.tensor
        def _(t):
            P.run_engine("pe", t)
        @block.scalar
        def _(s):
            P.run_engine("act", s)
        @block.vector
        def _(v):
            P.run_engine("dve", v)
        @block.gpsimd
        def _(g):
            P.run_engine("pool", g)
        @block.sync
        def _(sy):
            P.run_engine("sp", sy)
    return nc



def emit_l1_ffn(P, ctx, seq):
    xres = ctx["xres"]
    xn = ctx["xn"]
    bankA, bankB = ctx["psA"], ctx["psB"]
    rbuf = ctx["rbuf"]
    xr_bufs = (ctx["xrb"], ctx["xrb2"])
    xk_bufs = (ctx["xkb"], ctx["xkb2"])
    kact = ctx["kact"]
    wst = ctx["wst"]
    mur, muk = ctx["l1_mur"], ctx["l1_muk"]
    onemur, onemuk = ctx["onemur"], ctx["onemuk"]
    tmp_f = ctx["tmp_f32"]
    rot4f = [bankA, bankB, ctx.get("psA2", bankA), ctx.get("psB2", bankB)]

    def lnmix(qc):
        cs = slice(qc * QC, (qc + 1) * QC)
        xr = xr_bufs[qc % 2]
        xk = xk_bufs[qc % 2]
        emit_ln_chunk(P, ctx, xres, xn[:, :, cs], qc)
        for c in range(NCH):
            for (dst, mu, onemu) in ((xr, mur, onemur), (xk, muk, onemuk)):
                t1 = ctx["tmp_bf"]
                xcs = xn[:, c, cs]
                P.add("dve", lambda v, xcs=xcs, mu=mu, c=c, t1=t1: v.tensor_scalar(
                    t1[:], xcs, mu[:, c:c + 1], None, ALU.mult),
                    ins=[xcs, mu[:, :]], outs=[t1[:]])
                d = dst[:, c, :]
                if qc == 0:
                    P.add("dve", lambda v, d=d, t1=t1: v.tensor_copy(d[:, 0:1], t1[:, 0:1]),
                          ins=[t1[:]], outs=[d])
                    xprev = xn[:, c, 0:QC - 1]
                    P.add("dve", lambda v, d=d, xprev=xprev, onemu=onemu, c=c, t1=t1:
                          v.scalar_tensor_tensor(d[:, 1:QC], xprev, onemu[:, c:c + 1],
                                                 t1[:, 1:QC], ALU.mult, ALU.add),
                          ins=[xprev, onemu[:, :], t1[:]], outs=[d])
                else:
                    xprev = xn[:, c, qc * QC - 1:(qc + 1) * QC - 1]
                    P.add("dve", lambda v, d=d, xprev=xprev, onemu=onemu, c=c, t1=t1:
                          v.scalar_tensor_tensor(d, xprev, onemu[:, c:c + 1],
                                                 t1[:], ALU.mult, ALU.add),
                          ins=[xprev, onemu[:, :], t1[:]], outs=[d])

    lnmix(0)
    for qc in range(NQC):
        cs = slice(qc * QC, (qc + 1) * QC)
        xr = xr_bufs[qc % 2]
        xk = xk_bufs[qc % 2]
        for dch in range(NCH):
            ps = rot4f[dch % 4]
            wr_c = wst.fetch(P, "l1_wrt", dch)
            for kc in range(NCH):
                lhs = wr_c[:, kc, :]
                rhs = xr[:, kc, :]
                P.add("pe", lambda t, lhs=lhs, rhs=rhs, kc=kc, ps=ps: t.matmul(
                    ps[:], lhs, rhs, start=(kc == 0), stop=(kc == NCH - 1)),
                    ins=[lhs, rhs], outs=[ps[:]])
            dst = rbuf[:, dch, :]
            P.add("act", lambda s, dst=dst, ps=ps: s.activation(
                dst, ps[:], AF.Sigmoid), ins=[ps[:]], outs=[dst])
            if dch == 0 and qc + 1 < NQC:
                lnmix(qc + 1)
        for dch in range(NCH):
            ps = rot4f[dch % 4]
            wk_c = wst.fetch(P, "l1_wkt", dch)
            for kc in range(NCH):
                lhs = wk_c[:, kc, :]
                rhs = xk[:, kc, :]
                P.add("pe", lambda t, lhs=lhs, rhs=rhs, kc=kc, ps=ps: t.matmul(
                    ps[:], lhs, rhs, start=(kc == 0), stop=(kc == NCH - 1)),
                    ins=[lhs, rhs], outs=[ps[:]])
            tb = ctx["tmp_bf2"]
            P.add("dve", lambda v, tb=tb, ps=ps: v.tensor_scalar_max(
                tb[:], ps[:], 0.0), ins=[ps[:]], outs=[tb[:]])
            dst = kact[:, dch, :]
            P.add("act", lambda s, dst=dst, tb=tb: s.activation(
                dst, tb[:], AF.Square), ins=[tb[:]], outs=[dst])
        for dch in range(NCH):
            ps = rot4f[dch % 4]
            wv_c = wst.fetch(P, "l1_wvt", dch)
            for kc in range(NCH):
                lhs = wv_c[:, kc, :]
                rhs = kact[:, kc, :]
                P.add("pe", lambda t, lhs=lhs, rhs=rhs, kc=kc, ps=ps: t.matmul(
                    ps[:], lhs, rhs, start=(kc == 0), stop=(kc == NCH - 1)),
                    ins=[lhs, rhs], outs=[ps[:]])
            rs = rbuf[:, dch, :]
            xs = xres[:, dch, cs]
            P.add("dve", lambda v, rs=rs, ps=ps: v.tensor_mul(tmp_f[:], ps[:], rs),
                  ins=[ps[:], rs], outs=[tmp_f[:]])
            P.add("dve", lambda v, xs=xs: v.tensor_add(xs, xs, tmp_f[:]),
                  ins=[xs, tmp_f[:]], outs=[xs])


def build_l1_ffn_program():
    phases = ("l0a", "l0f", "l1a", "l1f")
    nc = bass.Bass("TRN2", target_bir_lowering=False)
    stack = ExitStack()
    ctx = {}
    with stack:
        common_setup(nc, stack, ctx, need=("xnfull",))
        P = Prog()
        declare_stream_weight(P, ctx, "l1_wrt", (8, 128, 8, 128))
        declare_stream_weight(P, ctx, "l1_wkt", (8, 128, 8, 128))
        declare_stream_weight(P, ctx, "l1_wvt", (8, 128, 8, 128))
        load_resident_vec(P, ctx, "l1_mur", NCH)
        load_resident_vec(P, ctx, "l1_muk", NCH)
        sb = lambda nm, shp, dt: stack.enter_context(nc.sbuf_tensor(nm, shp, dt))[:]
        ctx["rbuf"] = sb("rbuf", [128, NCH, QC], F32)
        ctx["xrb"] = sb("xrb", [128, NCH, QC], BF16)
        ctx["xkb"] = sb("xkb", [128, NCH, QC], BF16)
        ctx["xrb2"] = sb("xrb2", [128, NCH, QC], BF16)
        ctx["xkb2"] = sb("xkb2", [128, NCH, QC], BF16)
        ctx["kact"] = sb("kact", [128, NCH, QC], BF16)
        ctx["onemur"] = sb("onemur", [128, NCH], F32)
        ctx["onemuk"] = sb("onemuk", [128, NCH], F32)
        ctx["wst"] = WeightStream(ctx, nbufs=3, bufshape=[128, 4096])
        load_common_consts(P, ctx)
        P.add("dve", lambda v: v.tensor_scalar(
            ctx["onemur"], ctx["l1_mur"], -1.0, 1.0, ALU.mult, ALU.add),
            ins=[ctx["l1_mur"]], outs=[ctx["onemur"]])
        P.add("dve", lambda v: v.tensor_scalar(
            ctx["onemuk"], ctx["l1_muk"], -1.0, 1.0, ALU.mult, ALU.add),
            ins=[ctx["l1_muk"]], outs=[ctx["onemuk"]])
        for seq in range(SEQ_PER_CORE):
            emit_load_x(P, ctx, seq)
            if "l1f" in phases:
                emit_l1_ffn(P, ctx, seq)
            emit_store_x(P, ctx, seq)
        P.barrier()
        P.emit(nc, stack)
        block = stack.enter_context(nc.Block())
        @block.tensor
        def _(t):
            P.run_engine("pe", t)
        @block.scalar
        def _(s):
            P.run_engine("act", s)
        @block.vector
        def _(v):
            P.run_engine("dve", v)
        @block.gpsimd
        def _(g):
            P.run_engine("pool", g)
        @block.sync
        def _(sy):
            P.run_engine("sp", sy)
    return nc





def _carve(ctx, arena, start, shape):
    ar = ctx[arena]
    if len(shape) == 3:
        p, a, b = shape
        assert p == 128
        return ar[:, start:start + a * b].rearrange("p (a b) -> p a b", a=a)
    p, n = shape
    return ar[0:p, start:start + n]


def build_fused_program(phases=("l0a", "l0f", "l1a", "l1f")):
    nc = bass.Bass("TRN2", target_bir_lowering=False)
    stack = ExitStack()
    ctx = {}
    with stack:
        common_setup(nc, stack, ctx, need=("eyes", "ffn4", "masks", "ropearena", "rowsarena"))
        ctx["psC2"] = ctx["psA2"]
        ctx["psVt"] = stack.enter_context(nc.psum_tensor("psV", [128, 128], BF16))
        ctx["psV"] = ctx["psVt"][:, 0:64]
        ctx["psV2"] = ctx["psVt"][:, 0:64]
        ctx["arBF_h"] = stack.enter_context(
            nc.sbuf_tensor("arBF", [128, 49152], BF16))
        ctx["arF_h"] = stack.enter_context(
            nc.sbuf_tensor("arF", [128, 3072], F32))
        ctx["tmp_f32b"] = stack.enter_context(
            nc.sbuf_tensor("tmp_f32b", [128, QC], F32))
        ctx["negb"] = stack.enter_context(nc.sbuf_tensor("negb", [1, QC], BF16))[:]
        ctx["negb2"] = stack.enter_context(nc.sbuf_tensor("negb2", [1, QC], BF16))[:]
        ctx["maxk_sb"] = stack.enter_context(nc.sbuf_tensor("maxk_sb", [1, 2], F32))[:]
        P = Prog()
        P.region_pitch["arBF"] = (49152, 512)
        P.region_pitch["psV"] = (128, 64)
        P.region_pitch["arF"] = (3072, 512)
        ctx["row_m"] = _carve(ctx, "arF_h", 0, (1, QC))
        ctx["row_r"] = _carve(ctx, "arF_h", 512, (1, QC))
        ctx["row_mr"] = _carve(ctx, "arF_h", 1024, (1, QC))
        ctx["row_t"] = _carve(ctx, "arF_h", 1536, (1, QC))
        ctx["rt1"] = _carve(ctx, "arF_h", 2048, (16, QC))
        ctx["rt2"] = _carve(ctx, "arF_h", 2560, (16, QC))
        for nm, shp in [
            ("l0_wdqt", (4, 128, 8, 128)), ("l0_wdkvt", (4, 128, 8, 128)),
            ("l0_wuqt", (16, 128, 4, 64)), ("l0_wukt", (16, 128, 4, 64)),
            ("l0_wuvt", (16, 128, 4, 64)), ("l0_wqrt", (16, 128, 4, 32)),
            ("l0_wot", (8, 128, 8, 128)),
            ("l0_w1t", (16, 128, 8, 256)), ("l0_w2t", (16, 128, 8, 256)),
            ("l0_h1t", (32, 128, 32, 128)), ("l0_h2t", (32, 128, 32, 128)),
            ("l0_w3t", (8, 128, 32, 128)),
            ("l1_wq", (4, 128, 8, 128)), ("l1_wkv", (1, 128, 8, 128)),
            ("l1_wo", (8, 128, 4, 128)),
            ("l1_wrt", (8, 128, 8, 128)), ("l1_wkt", (8, 128, 8, 128)),
            ("l1_wvt", (8, 128, 8, 128)),
        ]:
            declare_stream_weight(P, ctx, nm, shp)
        load_resident_weight(P, ctx, "l0_wkrt", (128, 8, 32))
        for nm in ("l0_s1", "l0_s2", "l0_sh1", "l0_sh2"):
            load_resident_vec(P, ctx, nm, 32)
        load_resident_vec(P, ctx, "l0_s3", 8)
        for nm in ("l1_nw", "l1_nb", "l1_onw", "l1_onb", "l1_mur", "l1_muk"):
            load_resident_vec(P, ctx, nm, NCH)
        ctx["onemur"] = stack.enter_context(
            nc.sbuf_tensor("onemur", [128, NCH], F32))[:]
        ctx["onemuk"] = stack.enter_context(
            nc.sbuf_tensor("onemuk", [128, NCH], F32))[:]
        ctx["wst"] = WeightStream(ctx, nbufs=2, bufshape=[128, 4096])
        wst = ctx["wst"]
        load_common_consts(P, ctx)
        P.add("dve", lambda v: v.tensor_scalar(
            ctx["onemur"], ctx["l1_mur"], -1.0, 1.0, ALU.mult, ALU.add),
            ins=[ctx["l1_mur"]], outs=[ctx["onemur"]])
        P.add("dve", lambda v: v.tensor_scalar(
            ctx["onemuk"], ctx["l1_muk"], -1.0, 1.0, ALU.mult, ALU.add),
            ins=[ctx["l1_muk"]], outs=[ctx["onemuk"]])

        for seq in range(SEQ_PER_CORE):
            emit_load_x(P, ctx, seq)
            if "l0a" not in phases:
                pass
            ctx["attn_fm"] = _carve(ctx, "arBF_h", 0, (128, NCH, S))
            ctx["cq"] = _carve(ctx, "arBF_h", 16384, (128, 4, S))
            ctx["ckv"] = _carve(ctx, "arBF_h", 24576, (128, 4, S))
            ctx["kr1"] = _carve(ctx, "arBF_h", 32768, (16, S))
            ctx["kr2"] = _carve(ctx, "arBF_h", 34816, (16, S))
            ctx["kst"] = _carve(ctx, "arBF_h", 36864, (96, S))
            ctx["vtm"] = _carve(ctx, "arBF_h", 38912, (128, NKT, 65))
            ctx["rope_cos"] = _carve(ctx, "arBF_h", 40448, (16, S))
            ctx["rope_sin"] = _carve(ctx, "arBF_h", 42496, (16, S))
            ctx["e0"] = _carve(ctx, "arBF_h", 44544, (128, QC))
            ctx["e1"] = _carve(ctx, "arBF_h", 45056, (128, QC))
            ctx["e2"] = _carve(ctx, "arBF_h", 45568, (128, QC))
            ctx["e3"] = _carve(ctx, "arBF_h", 32768, (128, QC))
            ctx["qcs"] = _carve(ctx, "arBF_h", 46080, (96, QC))
            ctx["qcs2"] = _carve(ctx, "arBF_h", 46592, (96, QC))
            ctx["qr1s"] = _carve(ctx, "arBF_h", 47104, (16, QC))
            ctx["qr1s2"] = _carve(ctx, "arBF_h", 47616, (16, QC))
            ctx["qr2s"] = _carve(ctx, "arBF_h", 48128, (16, QC))
            ctx["qr2s2"] = _carve(ctx, "arBF_h", 48640, (16, QC))
            ctx["xnc0"] = _carve(ctx, "arBF_h", 0, (128, NCH, QC))
            ctx["xnc1"] = _carve(ctx, "arBF_h", 4096, (128, NCH, QC))
            P.add("dve", lambda v, d=ctx["vtm"]: v.memset(d[:, :, 64:65], 1.0),
                  ins=[], outs=[ctx["vtm"]])
            if "l0a" in phases:
                for nm, eng in (("rope_cos", "pool"), ("rope_sin", "pool")):
                    dst, sap = ctx[nm], ctx[nm + "_dram"]
                    P.add(eng, lambda e, dst=dst, sap=sap: e.dma_start(out=dst, in_=sap),
                          ins=[], outs=[dst], dma=True)
            if "l0a" in phases:
                emit_l0_attn(P, ctx, seq)
            ctx["xn"] = _carve(ctx, "arBF_h", 32768, (128, NCH, S))
            ctx["ffn_a"] = _carve(ctx, "arBF_h", 0, (128, 32, QC))
            ctx["ffn_b"] = _carve(ctx, "arBF_h", 16384, (128, 32, QC))
            if "l0f" in phases:
                emit_l0_ffn(P, ctx, ctx["xres"], ctx["xn"], wst, seq,
                            inline_ln=True)
            ctx["q4"] = _carve(ctx, "arBF_h", 0, (128, 4, S))
            ctx["attn_fm"] = _carve(ctx, "arBF_h", 8192, (128, 4, S))
            ctx["ychunk"] = _carve(ctx, "arBF_h", 16384, (128, NCH, QC))
            ctx["kst"] = _carve(ctx, "arBF_h", 20480, (65, S))
            ctx["vtm"] = _carve(ctx, "arBF_h", 22528, (128, NKT, 65))
            ctx["onesS"] = _carve(ctx, "arBF_h", 24064, (1, S))
            ctx["e0"] = _carve(ctx, "arBF_h", 26112, (128, QC))
            ctx["e1"] = _carve(ctx, "arBF_h", 26624, (128, QC))
            ctx["e2"] = _carve(ctx, "arBF_h", 27136, (128, QC))
            ctx["e3"] = _carve(ctx, "arBF_h", 24064, (128, QC))
            ctx["qcs"] = _carve(ctx, "arBF_h", 27648, (65, QC))
            ctx["qcs2"] = _carve(ctx, "arBF_h", 28160, (65, QC))
            ctx["xnc0"] = _carve(ctx, "arBF_h", 28672, (128, NCH, QC))
            ctx["xnc1"] = _carve(ctx, "arBF_h", 32768, (128, NCH, QC))
            if "l1a" in phases:
                P.add("dve", lambda v, d=ctx["onesS"]: v.memset(d, 1.0),
                      ins=[], outs=[ctx["onesS"]])
                P.add("dve", lambda v, d=ctx["vtm"]: v.memset(d[:, :, 64:65], 1.0),
                      ins=[], outs=[ctx["vtm"]])
            if "l1a" in phases:
                emit_l1_attn(P, ctx, seq)
            ctx["xn"] = _carve(ctx, "arBF_h", 0, (128, NCH, S))
            ctx["xrb"] = _carve(ctx, "arBF_h", 16384, (128, NCH, QC))
            ctx["xkb"] = _carve(ctx, "arBF_h", 20480, (128, NCH, QC))
            ctx["kact"] = _carve(ctx, "arBF_h", 24576, (128, NCH, QC))
            ctx["rbuf"] = _carve(ctx, "arBF_h", 28672, (128, NCH, QC))
            ctx["xrb2"] = _carve(ctx, "arBF_h", 32768, (128, NCH, QC))
            ctx["xkb2"] = _carve(ctx, "arBF_h", 36864, (128, NCH, QC))
            if "l1f" in phases:
                emit_l1_ffn(P, ctx, seq)
            emit_store_x(P, ctx, seq)
        P.barrier()
        P.emit(nc, stack)
        block = stack.enter_context(nc.Block())
        @block.tensor
        def _(t):
            P.run_engine("pe", t)
        @block.scalar
        def _(s):
            P.run_engine("act", s)
        @block.vector
        def _(v):
            P.run_engine("dve", v)
        @block.gpsimd
        def _(g):
            P.run_engine("pool", g)
        @block.sync
        def _(sy):
            P.run_engine("sp", sy)
    return nc



FUSED = True

ALL_WEIGHT_NAMES = [
    "l0_wdqt", "l0_wdkvt", "l0_wuqt", "l0_wukt", "l0_wuvt", "l0_wqrt",
    "l0_wkrt", "l0_wot",
    "l0_w1t", "l0_w2t", "l0_h1t", "l0_h2t", "l0_w3t",
    "l0_s1", "l0_s2", "l0_sh1", "l0_sh2", "l0_s3",
    "l1_wq", "l1_wkv", "l1_wo", "l1_nw", "l1_nb", "l1_onw", "l1_onb",
    "l1_wrt", "l1_wkt", "l1_wvt", "l1_mur", "l1_muk",
]
ALL_CONST_NAMES = ["ones_128x1", "ones_1x128", "rope_cos", "rope_sin",
                   "masks", "eye64", "eye128"]


def kernel(**inputs):
    consts = build_constants()
    w = prep_weights(inputs)
    x = np.asarray(inputs["L0_attn_x"])
    xT = np.ascontiguousarray(x.transpose(0, 2, 1))
    cur = [xT[c * SEQ_PER_CORE:(c + 1) * SEQ_PER_CORE] for c in range(NCORES)]

    if FUSED:
        nc = build_fused_program()
        in_maps = []
        for c in range(NCORES):
            m = {"x_in": cur[c]}
            m.update({k: consts[k] for k in ALL_CONST_NAMES})
            for nm in ALL_WEIGHT_NAMES:
                m[nm] = w[nm]
            in_maps.append(m)
        res = run_bass_kernel_spmd(nc, in_maps, core_ids=list(range(NCORES)))
        cur = [np.asarray(r["x_out"]) for r in res.results]
    else:
        launches = [
            (build_l0_attn_program,
             ["l0_wdqt", "l0_wdkvt", "l0_wuqt", "l0_wukt", "l0_wuvt",
              "l0_wqrt", "l0_wkrt", "l0_wot"],
             ["ones_128x1", "ones_1x128", "rope_cos", "rope_sin", "masks",
              "eye64", "eye128"]),
            (build_l0_ffn_program,
             ["l0_w1t", "l0_w2t", "l0_h1t", "l0_h2t", "l0_w3t",
              "l0_s1", "l0_s2", "l0_sh1", "l0_sh2", "l0_s3"],
             ["ones_128x1", "ones_1x128"]),
            (build_l1_attn_program,
             ["l1_wq", "l1_wkv", "l1_wo", "l1_nw", "l1_nb", "l1_onw", "l1_onb"],
             ["ones_128x1", "ones_1x128", "eye64", "eye128"]),
            (build_l1_ffn_program,
             ["l1_wrt", "l1_wkt", "l1_wvt", "l1_mur", "l1_muk"],
             ["ones_128x1", "ones_1x128"]),
        ]
        for build_fn, wnames, cnames in launches:
            nc = build_fn()
            in_maps = []
            for c in range(NCORES):
                m = {"x_in": cur[c]}
                m.update({k: consts[k] for k in cnames})
                for nm in wnames:
                    m[nm] = w[nm]
                in_maps.append(m)
            res = run_bass_kernel_spmd(nc, in_maps, core_ids=list(range(NCORES)))
            cur = [np.asarray(r["x_out"]) for r in res.results]
    outT = np.concatenate(cur, axis=0)
    return np.ascontiguousarray(outT.transpose(0, 2, 1))


# version 23
# speedup vs baseline: 1.0330x; 1.0192x over previous
import math
from contextlib import ExitStack

import numpy as np
import ml_dtypes
import concourse.bass as bass
import concourse.mybir as mybir
from concourse.bass_utils import run_bass_kernel_spmd

F32 = mybir.dt.float32
BF16 = mybir.dt.bfloat16
AF = mybir.ActivationFunctionType
ALU = mybir.AluOpType
AX = mybir.AxisListType

B, S, D = 16, 2048, 1024
NCORES = 8
SEQ_PER_CORE = B // NCORES
NCH = D // 128
QC = 512
NQC = S // QC
NKT = S // 128


COMPUTE_ENGINES = ("pe", "act", "dve", "pool", "sp")


class Prog:

    def __init__(self):
        self.ops = []
        self.last_writer = {}
        self.readers = {}
        self.deps = []
        self.region_pitch = {}

    def _keys(self, a):
        if not hasattr(a, "tensor"):
            return [a]
        nm = a.tensor.name
        rp = self.region_pitch.get(nm)
        if rp is None:
            return [nm]
        pitch, R = rp
        off = a.offset % pitch
        ext = 1
        for step, cnt in a.ap:
            if step >= pitch:
                continue
            ext += (cnt - 1) * step
        return [f"{nm}#{r}" for r in range(off // R, (off + ext - 1) // R + 1)]

    def add(self, engine, fn, ins=(), outs=(), dma=False):
        idx = len(self.ops)
        dep = set()
        rbufs = [k for a in ins for k in self._keys(a)]
        wbufs = [k for a in outs for k in self._keys(a)]
        for b in rbufs:
            w = self.last_writer.get(b)
            if w is not None:
                dep.add(w)
        for b in wbufs:
            w = self.last_writer.get(b)
            if w is not None:
                dep.add(w)
            for r in self.readers.get(b, ()):
                dep.add(r)
        for b in wbufs:
            self.last_writer[b] = idx
            self.readers[b] = []
        for b in rbufs:
            self.readers.setdefault(b, []).append(idx)
        stream = ("dma", wbufs[0]) if dma else engine
        self.ops.append((engine, fn, rbufs, wbufs, dma, stream))
        self.deps.append(dep)
        return idx

    def barrier(self, engine="sp"):
        idx = len(self.ops)
        dep = set()
        per_stream_last = {}
        for i, op in enumerate(self.ops):
            per_stream_last[op[5]] = i
        dep.update(per_stream_last.values())
        self.ops.append((engine, None, [], [], False, engine))
        self.deps.append(dep)
        self.last_writer = {b: idx for b in set(list(self.last_writer) + list(self.readers))}
        self.readers = {}
        return idx

    def emit(self, nc, stack):
        slim_deps = []
        for i, dep in enumerate(self.deps):
            per_stream = {}
            my_eng = self.ops[i][0]
            my_dma = self.ops[i][4]
            for d in dep:
                ds = self.ops[d][5]
                if ds == "pe" and my_eng == "pe" and not my_dma and not self.ops[i][4]:
                    continue
                per_stream[ds] = max(per_stream.get(ds, -1), d)
            slim_deps.append(per_stream)
        need_inc = set()
        for per_stream in slim_deps:
            need_inc.update(per_stream.values())
        streams = []
        for op in self.ops:
            if op[5] not in streams:
                streams.append(op[5])
        sems = {}
        for sidx, s in enumerate(streams):
            nm = s if isinstance(s, str) else f"dma{sidx}"
            sems[s] = stack.enter_context(nc.semaphore(f"S_{nm}"))
        inc_count = {}
        stream_tot = {}
        for i, op in enumerate(self.ops):
            stream = op[5]
            if op[4]:
                need_inc.add(i)
            if i in need_inc:
                step = 16 if op[4] else 1
                stream_tot[stream] = stream_tot.get(stream, 0) + step
                inc_count[i] = stream_tot[stream]
        for s, v in stream_tot.items():
            assert v < 30000, f"sem overflow risk on {s}: {v}"
        per_engine_ops = {e: [] for e in COMPUTE_ENGINES}
        for i, op in enumerate(self.ops):
            per_engine_ops[op[0]].append(i)
        self._sems = sems
        self._inc_count = inc_count
        self._need_inc = need_inc
        self._slim_deps = slim_deps
        self._per_engine_ops = per_engine_ops

    def run_engine(self, engine, builder):
        sems = self._sems
        hw = {}
        for i in self._per_engine_ops[engine]:
            eng, fn, _, _, dma, stream = self.ops[i]
            for pstream, d in sorted(self._slim_deps[i].items(),
                                     key=lambda kv: str(kv[0])):
                v = self._inc_count[d]
                if hw.get(pstream, -1) >= v:
                    continue
                builder.wait_ge(sems[pstream], v)
                hw[pstream] = v
            if fn is None:
                if i in self._need_inc:
                    builder.sem_inc(sems[stream], 1)
                continue
            inst = fn(builder)
            if i in self._need_inc:
                inst.then_inc(sems[stream], 16 if dma else 1)



def pmaj(wT, osub):
    K, O = wT.shape
    kc = K // 128
    nch = O // osub
    return np.ascontiguousarray(
        wT.reshape(kc, 128, nch, osub).transpose(2, 1, 0, 3))


def vec128(v):
    n = v.shape[0] // 128
    return np.ascontiguousarray(v.reshape(n, 128).T)


def build_constants():
    c = {}
    j = np.arange(16, dtype=np.float64)
    inv_freq = 10000.0 ** (-(2 * j) / 32.0)
    ang = np.arange(S, dtype=np.float64)[None, :] * inv_freq[:, None]
    c["rope_cos"] = np.cos(ang).astype(np.float32)
    c["rope_sin"] = np.sin(ang).astype(np.float32)
    masks = np.zeros((4, 128, QC), dtype=np.float32)
    for jj in range(4):
        kk = np.arange(128)[:, None]
        qq = np.arange(QC)[None, :]
        masks[jj] = (qq - jj * 128 - kk >= 0).astype(np.float32)
    c["masks"] = masks
    c["ones_128x1"] = np.ones((128, 1), dtype=np.float32)
    c["ones_1x128"] = np.ones((1, 128), dtype=np.float32)
    c["eye64"] = np.eye(64, dtype=np.float32)
    c["eye128"] = np.eye(128, dtype=np.float32)
    return c


def prep_weights(inp):
    w = {}
    w["l0_wdqt"] = pmaj(np.ascontiguousarray(inp["L0_attn_W_DQ"].T), 128)
    w["l0_wdkvt"] = pmaj(np.ascontiguousarray(inp["L0_attn_W_DKV"].T), 128)
    w["l0_wuqt"] = pmaj(np.ascontiguousarray(inp["L0_attn_W_UQ"].T), 64)
    w["l0_wukt"] = pmaj(np.ascontiguousarray(inp["L0_attn_W_UK"].T), 64)
    w["l0_wuvt"] = pmaj(np.ascontiguousarray(inp["L0_attn_W_UV"].T), 64)
    wqr = inp["L0_attn_W_QR"]
    perm = np.concatenate([np.concatenate([np.arange(h * 32, (h + 1) * 32, 2),
                                           np.arange(h * 32 + 1, (h + 1) * 32, 2)])
                           for h in range(16)])
    wqr_p = wqr[perm]
    w["l0_wqrt"] = pmaj(np.ascontiguousarray(wqr_p.T), 32)
    wkr = inp["L0_attn_W_KR"]
    perm32 = np.concatenate([np.arange(0, 32, 2), np.arange(1, 32, 2)])
    w["l0_wkrt"] = pmaj(np.ascontiguousarray(wkr[perm32].T), 32)[0]
    w["l0_wot"] = pmaj(np.ascontiguousarray(inp["L0_attn_W_O"].T), 128)
    w["l0_w1t"] = pmaj(np.ascontiguousarray(inp["L0_ffn_w1_w"].T), 256)
    w["l0_w2t"] = pmaj(np.ascontiguousarray(inp["L0_ffn_w2_w"].T), 256)
    w["l0_h1t"] = pmaj(np.ascontiguousarray(inp["L0_ffn_h1_w"].T), 128)
    w["l0_h2t"] = pmaj(np.ascontiguousarray(inp["L0_ffn_h2_w"].T), 128)
    w["l0_w3t"] = pmaj(np.ascontiguousarray(inp["L0_ffn_w3_w"].T), 128)
    w["l0_s1"] = vec128(inp["L0_ffn_w1_s"])
    w["l0_s2"] = vec128(inp["L0_ffn_w2_s"])
    w["l0_sh1"] = vec128(inp["L0_ffn_h1_s"])
    w["l0_sh2"] = vec128(inp["L0_ffn_h2_s"])
    w["l0_s3"] = vec128(inp["L0_ffn_w3_s"])
    w["l1_wq"] = pmaj(np.ascontiguousarray(inp["L1_attn_Wq"]), 128)
    w["l1_wkv"] = pmaj(np.ascontiguousarray(inp["L1_attn_Wkv"]), 128)
    w["l1_wo"] = pmaj(np.ascontiguousarray(inp["L1_attn_Wo"]), 128)
    w["l1_nw"] = vec128(inp["L1_attn_norm_w"])
    w["l1_nb"] = vec128(inp["L1_attn_norm_b"])
    w["l1_onw"] = vec128(inp["L1_attn_out_norm_w"])
    w["l1_onb"] = vec128(inp["L1_attn_out_norm_b"])
    w["l1_wrt"] = pmaj(np.ascontiguousarray(inp["L1_ffn_W_r"].T), 128)
    w["l1_wkt"] = pmaj(np.ascontiguousarray(inp["L1_ffn_W_k"].T), 128)
    w["l1_wvt"] = pmaj(np.ascontiguousarray(inp["L1_ffn_W_v"].T), 128)
    w["l1_mur"] = vec128(inp["L1_ffn_mu_r"])
    w["l1_muk"] = vec128(inp["L1_ffn_mu_k"])
    return w



def emit_ln_chunk(P, ctx, xres, xn_out, qc, gamma=None, beta=None,
                  src_bf16=False):
    bankA, bankB = ctx["psA"], ctx["psB"]
    bankD, bankE = ctx["psD"], ctx["psE"]
    ones1b = ctx["ones128x1_bf"]
    ones1x128f = ctx["ones_1x128_f"]
    row_m, row_r, row_mr, row_t = (ctx[k][:] for k in
                                   ("row_m", "row_r", "row_mr", "row_t"))
    tmp_f = ctx["tmp_f32"]
    cs = slice(qc * QC, (qc + 1) * QC)
    for c in range(NCH):
        tb = ctx["tmp_bf"] if c % 2 == 0 else ctx["tmp_bf2"]
        if src_bf16:
            xb = xres[:, c, cs]
        else:
            xs = xres[:, c, cs]
            P.add("dve", lambda v, xs=xs, tb=tb: v.tensor_copy(tb[:], xs),
                  ins=[xs], outs=[tb[:]])
            xb = tb[:]
        P.add("pe", lambda t, xb=xb, c=c: t.matmul(
            bankD[0:1, :], ones1b, xb, start=(c == 0), stop=(c == NCH - 1)),
            ins=[ones1b, xb], outs=[bankD[0:1, :]])
        tb2 = ctx["tmp_bf"] if c % 2 == 1 else ctx["tmp_bf2"]
        if src_bf16:
            P.add("act", lambda s, xb=xb, tb2=tb2: s.activation(
                tb2[:], xb, AF.Square), ins=[xb], outs=[tb2[:]])
            x2b = tb2[:]
        else:
            P.add("act", lambda s, xb=xb, tb2=tb2: s.activation(
                tb2[:], xb, AF.Square), ins=[xb], outs=[tb2[:]])
            x2b = tb2[:]
        P.add("pe", lambda t, x2b=x2b, c=c: t.matmul(
            bankE[0:1, :], ones1b, x2b, start=(c == 0), stop=(c == NCH - 1)),
            ins=[ones1b, x2b], outs=[bankE[0:1, :]])
    P.add("dve", lambda v: v.tensor_scalar_mul(row_m, bankD[0:1, :], 1.0 / D),
          ins=[bankD[0:1, :]], outs=[row_m])
    P.add("act", lambda s: s.activation(row_t, row_m, AF.Square),
          ins=[row_m], outs=[row_t])
    P.add("dve", lambda v: v.tensor_scalar_mul(row_r, bankE[0:1, :], 1.0 / D),
          ins=[bankE[0:1, :]], outs=[row_r])
    P.add("dve", lambda v: v.tensor_sub(row_r, row_r, row_t),
          ins=[row_r, row_t], outs=[row_r])
    P.add("dve", lambda v: v.tensor_scalar_add(row_r, row_r, 1e-5),
          ins=[row_r], outs=[row_r])
    P.add("act", lambda s: s.activation(row_r, row_r, AF.Ln),
          ins=[row_r], outs=[row_r])
    P.add("dve", lambda v: v.tensor_scalar_mul(row_r, row_r, -0.5),
          ins=[row_r], outs=[row_r])
    P.add("act", lambda s: s.activation(row_r, row_r, AF.Exp),
          ins=[row_r], outs=[row_r])
    P.add("dve", lambda v: v.scalar_tensor_tensor(
        row_mr, row_m, -1.0, row_r, ALU.mult, ALU.mult),
        ins=[row_m, row_r], outs=[row_mr])
    P.add("pe", lambda t: t.matmul(bankA[:], ones1x128f, row_r,
                                   start=True, stop=True),
          ins=[ones1x128f, row_r], outs=[bankA[:]])
    P.add("pe", lambda t: t.matmul(bankB[:], ones1x128f, row_mr,
                                   start=True, stop=True),
          ins=[ones1x128f, row_mr], outs=[bankB[:]])
    for c in range(NCH):
        xs = xres[:, c, cs]
        xno = xn_out[:, c, :] if xn_out.shape[2] == QC else xn_out[:, c, cs]
        P.add("dve", lambda v, xs=xs: v.tensor_mul(tmp_f[:], xs, bankA[:]),
              ins=[xs, bankA[:]], outs=[tmp_f[:]])
        if gamma is None:
            P.add("dve", lambda v, xno=xno: v.tensor_add(xno, tmp_f[:], bankB[:]),
                  ins=[tmp_f[:], bankB[:]], outs=[xno])
        else:
            P.add("dve", lambda v: v.tensor_add(tmp_f[:], tmp_f[:], bankB[:]),
                  ins=[tmp_f[:], bankB[:]], outs=[tmp_f[:]])
            g = gamma[:, c:c + 1]
            bta = beta[:, c:c + 1]
            P.add("dve", lambda v, xno=xno, g=g, bta=bta: v.tensor_scalar(
                xno, tmp_f[:], g, bta, ALU.mult, ALU.add),
                ins=[tmp_f[:], gamma[:, :], beta[:, :]], outs=[xno])


def emit_ln(P, ctx, xres, xn, seq_tag, gamma=None, beta=None, src_bf16=False):
    for qc in range(NQC):
        emit_ln_chunk(P, ctx, xres, xn, qc, gamma, beta, src_bf16)


def emit_l0_ffn(P, ctx, xres, xn, wst, seq, inline_ln=False):
    nc = ctx["nc"]
    bankA, bankB = ctx["psA"], ctx["psB"]
    bankA2, bankB2 = ctx["psA2"], ctx["psB2"]
    rot4 = [bankA, bankB, bankA2, bankB2]
    aBuf, bBuf = ctx["ffn_a"], ctx["ffn_b"]
    sil = ctx["tmp_bf"]
    s1, s2, sh1, sh2, s3 = (ctx[k] for k in ("l0_s1", "l0_s2", "l0_sh1",
                                             "l0_sh2", "l0_s3"))
    for tt in range(NQC):
        cs = slice(tt * QC, (tt + 1) * QC)
        if inline_ln:
            emit_ln_chunk(P, ctx, xres, xn[:, :, cs], tt)
        for oc in range(16):
            w1c = wst.fetch(P, "l0_w1t", oc, (tt, seq))
            w2c = wst.fetch(P, "l0_w2t", oc, (tt, seq))
            for sub in range(2):
                m = oc * 2 + sub
                ps1 = bankA
                ps2 = bankB
                for kc in range(NCH):
                    lhs = w1c[:, kc, sub * 128:(sub + 1) * 128]
                    rhs = xn[:, kc, cs]
                    P.add("pe", lambda t, lhs=lhs, rhs=rhs, kc=kc: t.matmul(
                        ps1[:], lhs, rhs, start=(kc == 0), stop=(kc == NCH - 1)),
                        ins=[lhs, rhs], outs=[ps1[:]])
                for kc in range(NCH):
                    lhs = w2c[:, kc, sub * 128:(sub + 1) * 128]
                    rhs = xn[:, kc, cs]
                    P.add("pe", lambda t, lhs=lhs, rhs=rhs, kc=kc: t.matmul(
                        ps2[:], lhs, rhs, start=(kc == 0), stop=(kc == NCH - 1)),
                        ins=[lhs, rhs], outs=[ps2[:]])
                P.add("act", lambda s, m=m: s.activation(
                    sil[:], ps1[:], AF.Silu, scale=s1[:, m:m + 1]),
                    ins=[ps1[:], s1[:, :]], outs=[sil[:]])
                ao = aBuf[:, m, :]
                P.add("dve", lambda v, ao=ao, m=m: v.scalar_tensor_tensor(
                    ao, ps2[:], s2[:, m:m + 1], sil[:], ALU.mult, ALU.mult),
                    ins=[ps2[:], s2[:, :], sil[:]], outs=[ao])
        for m in range(32):
            hc = wst.fetch(P, "l0_h1t", m, (tt, seq))
            ps = rot4[m % 4]
            for kc in range(32):
                lhs = hc[:, kc, :]
                rhs = aBuf[:, kc, :]
                P.add("pe", lambda t, lhs=lhs, rhs=rhs, kc=kc, ps=ps: t.matmul(
                    ps[:], lhs, rhs, start=(kc == 0), stop=(kc == 31)),
                    ins=[lhs, rhs], outs=[ps[:]])
            bo = bBuf[:, m, :]
            P.add("act", lambda s, bo=bo, m=m, ps=ps: s.activation(
                bo, ps[:], AF.Gelu, scale=sh1[:, m:m + 1]),
                ins=[ps[:], sh1[:, :]], outs=[bo])
        for m in range(32):
            hc = wst.fetch(P, "l0_h2t", m, (tt, seq))
            ps = rot4[m % 4]
            for kc in range(32):
                lhs = hc[:, kc, :]
                rhs = bBuf[:, kc, :]
                P.add("pe", lambda t, lhs=lhs, rhs=rhs, kc=kc, ps=ps: t.matmul(
                    ps[:], lhs, rhs, start=(kc == 0), stop=(kc == 31)),
                    ins=[lhs, rhs], outs=[ps[:]])
            co = aBuf[:, m, :]
            P.add("act", lambda s, co=co, m=m, ps=ps: s.activation(
                co, ps[:], AF.Gelu, scale=sh2[:, m:m + 1]),
                ins=[ps[:], sh2[:, :]], outs=[co])
        for m in range(NCH):
            wc = wst.fetch(P, "l0_w3t", m, (tt, seq))
            ps = rot4[m % 4]
            for kc in range(32):
                lhs = wc[:, kc, :]
                rhs = aBuf[:, kc, :]
                P.add("pe", lambda t, lhs=lhs, rhs=rhs, kc=kc, ps=ps: t.matmul(
                    ps[:], lhs, rhs, start=(kc == 0), stop=(kc == 31)),
                    ins=[lhs, rhs], outs=[ps[:]])
            xs = xres[:, m, cs]
            P.add("dve", lambda v, xs=xs, m=m, ps=ps: v.scalar_tensor_tensor(
                xs, ps[:], s3[:, m:m + 1], xs, ALU.mult, ALU.add),
                ins=[ps[:], s3[:, :], xs], outs=[xs])


class WeightStream:

    def __init__(self, ctx, nbufs=4, bufshape=None):
        nc = ctx["nc"]
        stack = ctx["stack"]
        self.ctx = ctx
        self.bufs = []
        if bufshape is None:
            bufshape = [128, 8192]
        for i in range(nbufs):
            t = stack.enter_context(
                nc.sbuf_tensor(f"wstream{i}", bufshape, BF16))
            self.bufs.append(t)
        self.next = 0

    def fetch(self, P, name, oc, tag=None):
        dram = self.ctx[name + "_dram"]
        _, _, kcn, osub = dram.shape
        buf = self.bufs[self.next]
        self.next = (self.next + 1) % len(self.bufs)
        view = buf[:, 0:kcn * osub].rearrange("p (k o) -> p k o", k=kcn)
        src = dram[oc]
        P.add("pool", lambda g, view=view, src=src: g.dma_start(out=view, in_=src),
              ins=[src], outs=[view], dma=True)
        return view



def common_setup(nc, stack, ctx, need=("xnfull",)):
    ctx["nc"] = nc
    ctx["stack"] = stack
    ctx["need"] = set(need)
    ctx["x_in"] = nc.dram_tensor("x_in", [SEQ_PER_CORE, D, S], F32,
                                 kind="ExternalInput").ap()
    ctx["x_out"] = nc.dram_tensor("x_out", [SEQ_PER_CORE, D, S], F32,
                                  kind="ExternalOutput").ap()
    shapes = {"rope_cos": [16, S], "rope_sin": [16, S], "masks": [4, 128, QC],
              "ones_128x1": [128, 1], "ones_1x128": [1, 128],
              "eye64": [64, 64], "eye128": [128, 128]}
    used = ["ones_128x1", "ones_1x128"]
    if "rope" in ctx["need"] or "ropearena" in ctx["need"]:
        used += ["rope_cos", "rope_sin"]
    if "masks" in ctx["need"]:
        used += ["masks"]
    if "eyes" in ctx["need"]:
        used += ["eye64", "eye128"]
    ctx["const_used"] = used
    for nm in used:
        ctx[nm + "_dram"] = nc.dram_tensor(nm, shapes[nm], F32,
                                           kind="ExternalInput").ap()
    ctx["ones128x1_bf"] = stack.enter_context(
        nc.sbuf_tensor("ones128x1_bf_s", [128, 1], BF16))[:]
    ctx["ones_1x128_f"] = stack.enter_context(
        nc.sbuf_tensor("ones_1x128_f_s", [1, 128], F32))[:]
    ctx["ones_1x128_bf"] = stack.enter_context(
        nc.sbuf_tensor("ones_1x128_bf_s", [1, 128], BF16))[:]
    ctx["ones_1x64_f"] = ctx["ones_1x128_f"][:, 0:64]
    ctx["ones_1x64_bf"] = ctx["ones_1x128_bf"][:, 0:64]
    ctx["rzb"] = stack.enter_context(
        nc.sbuf_tensor("rzb_s", [1, QC], BF16))[:]
    ctx["au0"] = stack.enter_context(
        nc.sbuf_tensor("au0_s", [64, QC], BF16))[:]
    ctx["au1"] = stack.enter_context(
        nc.sbuf_tensor("au1_s", [64, QC], BF16))[:]
    ctx["zbb"] = stack.enter_context(
        nc.sbuf_tensor("zbb_s", [64, QC], BF16))[:]
    if "eyes" in ctx["need"]:
        ctx["eye64_bf"] = stack.enter_context(
            nc.sbuf_tensor("eye64_bf_s", [64, 64], BF16))[:]

    if "rope" in ctx["need"]:
        ctx["rope_cos"] = stack.enter_context(
            nc.sbuf_tensor("rope_cos_s", [16, S], BF16))[:]
        ctx["rope_sin"] = stack.enter_context(
            nc.sbuf_tensor("rope_sin_s", [16, S], BF16))[:]
    if "masks" in ctx["need"]:
        ctx["masks"] = stack.enter_context(
            nc.sbuf_tensor("masks_sb", [128, 4, QC], BF16))[:]
    ctx["xres"] = stack.enter_context(
        nc.sbuf_tensor("xres", [128, NCH, S], F32))[:]
    if "xnfull" in ctx["need"]:
        ctx["xn"] = stack.enter_context(
            nc.sbuf_tensor("xn", [128, NCH, S], BF16))[:]
    ctx["tmp_bf"] = stack.enter_context(
        nc.sbuf_tensor("tmp_bf", [128, QC], BF16))
    ctx["tmp_bf2"] = stack.enter_context(
        nc.sbuf_tensor("tmp_bf2", [128, QC], BF16))
    ctx["tmp_f32"] = stack.enter_context(
        nc.sbuf_tensor("tmp_f32", [128, QC], F32))
    if "rowsarena" not in ctx["need"]:
        ctx["row_m"] = stack.enter_context(nc.sbuf_tensor("row_m", [1, QC], F32))
        ctx["row_r"] = stack.enter_context(nc.sbuf_tensor("row_r", [1, QC], F32))
        ctx["row_mr"] = stack.enter_context(nc.sbuf_tensor("row_mr", [1, QC], F32))
        ctx["row_t"] = stack.enter_context(nc.sbuf_tensor("row_t", [1, QC], F32))
    ctx["psA"] = stack.enter_context(nc.psum_tensor("psA", [128, QC], F32))
    ctx["psB"] = stack.enter_context(nc.psum_tensor("psB", [128, QC], F32))
    if "ffn4" in ctx["need"]:
        ctx["psA2"] = stack.enter_context(nc.psum_tensor("psA2", [128, QC], F32))
        ctx["psB2"] = stack.enter_context(nc.psum_tensor("psB2", [128, QC], F32))
    ctx["psC"] = stack.enter_context(nc.psum_tensor("psC", [65, QC], F32))
    ctx["psD"] = stack.enter_context(nc.psum_tensor("psD", [64, QC], F32))
    ctx["psE"] = stack.enter_context(nc.psum_tensor("psE", [64, QC], F32))


def load_common_consts(P, ctx):
    pairs = [("ones_128x1", ctx["ones128x1_bf"], None, "pool"),
             ("ones_1x128", ctx["ones_1x128_f"], None, "sp"),
             ("ones_1x128", ctx["ones_1x128_bf"], None, "pool")]
    if "eyes" in ctx["need"]:
        pairs += [("eye64", ctx["eye64_bf"], None, "pool")]
    if "rope" in ctx["need"]:
        pairs += [("rope_cos", ctx["rope_cos"], None, "pool"),
                  ("rope_sin", ctx["rope_sin"], None, "pool")]
    if "masks" in ctx["need"]:
        pairs += [("masks", ctx["masks"],
                   ctx["masks_dram"].rearrange("m p q -> p m q"), "pool")]
    for nm, dst, src_ap, eng in pairs:
        if src_ap is None:
            src_ap = ctx[nm + "_dram"]
        P.add(eng, lambda e, dst=dst, src_ap=src_ap: e.dma_start(out=dst, in_=src_ap),
              ins=[], outs=[dst], dma=True)
    P.barrier()


def load_resident_weight(P, ctx, name, shape):
    nc, stack = ctx["nc"], ctx["stack"]
    dram = nc.dram_tensor(name, list(shape), F32, kind="ExternalInput").ap()
    sb = stack.enter_context(nc.sbuf_tensor(name + "_sb", list(shape), BF16))[:]
    ctx[name + "_dram"] = dram
    ctx[name] = sb
    P.add("pool", lambda g: g.dma_start(out=sb, in_=dram),
          ins=[], outs=[sb], dma=True)
    return sb


def load_resident_vec(P, ctx, name, ncols):
    nc, stack = ctx["nc"], ctx["stack"]
    dram = nc.dram_tensor(name, [128, ncols], F32, kind="ExternalInput").ap()
    sb = stack.enter_context(nc.sbuf_tensor(name + "_sb", [128, ncols], F32))[:]
    ctx[name + "_dram"] = dram
    ctx[name] = sb
    P.add("sp", lambda e: e.dma_start(out=sb, in_=dram),
          ins=[], outs=[sb], dma=True)
    return sb


def declare_stream_weight(P, ctx, name, shape):
    nc = ctx["nc"]
    ctx[name + "_dram"] = nc.dram_tensor(name, list(shape), F32,
                                         kind="ExternalInput").ap()


def emit_load_x(P, ctx, seq):
    xres = ctx["xres"]
    src = ctx["x_in"][seq].rearrange("(c p) t -> p c t", p=128)
    P.add("sp", lambda e: e.dma_start(out=xres, in_=src),
          ins=[], outs=[xres], dma=True)


def emit_store_x(P, ctx, seq):
    xres = ctx["xres"]
    dst = ctx["x_out"][seq].rearrange("(c p) t -> p c t", p=128)
    P.add("sp", lambda e: e.dma_start(out=dst, in_=xres),
          ins=[xres], outs=[dst, "x_out_done"], dma=True)


def build_l0_ffn_program():
    nc = bass.Bass("TRN2", target_bir_lowering=False)
    stack = ExitStack()
    ctx = {}
    with stack:
        common_setup(nc, stack, ctx, need=("xnfull", "ffn4"))
        declare_stream_weight(None, ctx, "l0_w1t", (16, 128, 8, 256))
        declare_stream_weight(None, ctx, "l0_w2t", (16, 128, 8, 256))
        declare_stream_weight(None, ctx, "l0_h1t", (32, 128, 32, 128))
        declare_stream_weight(None, ctx, "l0_h2t", (32, 128, 32, 128))
        declare_stream_weight(None, ctx, "l0_w3t", (8, 128, 32, 128))
        ctx["ffn_a"] = stack.enter_context(
            nc.sbuf_tensor("ffn_a", [128, 32, QC], BF16))[:]
        ctx["ffn_b"] = stack.enter_context(
            nc.sbuf_tensor("ffn_b", [128, 32, QC], BF16))[:]
        P = Prog()
        for nm in ("l0_s1", "l0_s2", "l0_sh1", "l0_sh2"):
            load_resident_vec(P, ctx, nm, 32)
        load_resident_vec(P, ctx, "l0_s3", 8)
        load_common_consts(P, ctx)
        wst = WeightStream(ctx, nbufs=3, bufshape=[128, 2048])
        for seq in range(SEQ_PER_CORE):
            emit_load_x(P, ctx, seq)
            emit_ln(P, ctx, ctx["xres"], ctx["xn"], f"s{seq}")
            emit_l0_ffn(P, ctx, ctx["xres"], ctx["xn"], wst, seq)
            emit_store_x(P, ctx, seq)
        P.barrier()
        P.emit(nc, stack)
        block = stack.enter_context(nc.Block())
        @block.tensor
        def _(t):
            P.run_engine("pe", t)
        @block.scalar
        def _(s):
            P.run_engine("act", s)
        @block.vector
        def _(v):
            P.run_engine("dve", v)
        @block.gpsimd
        def _(g):
            P.run_engine("pool", g)
        @block.sync
        def _(sy):
            P.run_engine("sp", sy)
    return nc





RSQ96 = 1.0 / math.sqrt(96.0)
CLIP_HI = math.exp(80.0 / math.sqrt(96.0))
CLIP_LO = math.exp(-80.0 / math.sqrt(96.0))


def emit_l0_attn(P, ctx, seq):
    xres = ctx["xres"]
    bankA, bankB = ctx["psA"], ctx["psB"]
    bankC, bankD, bankE = ctx["psC"], ctx["psD"], ctx["psE"]
    psV, psV2 = ctx["psV"], ctx["psV2"]
    cq, ckv = ctx["cq"], ctx["ckv"]
    kr1, kr2 = ctx["kr1"], ctx["kr2"]
    kst, vtm = ctx["kst"], ctx["vtm"]
    qcs_bufs = (ctx["qcs"], ctx["qcs2"])
    qr1s_bufs = (ctx["qr1s"], ctx["qr1s2"])
    qr2s_bufs = (ctx["qr2s"], ctx["qr2s2"])
    rt1, rt2 = ctx["rt1"], ctx["rt2"]
    ebufs = (ctx["e0"], ctx["e1"], ctx["e2"], ctx["e3"])
    afm = ctx["attn_fm"]
    cosv, sinv = ctx["rope_cos"], ctx["rope_sin"]
    masks = ctx["masks"]
    ones1b = ctx["ones128x1_bf"]
    ones1x64b = ctx["ones_1x64_bf"]
    bankB2 = ctx["psB2"]
    zrow, rzb = ctx["row_m"][:], ctx["rzb"]
    zb = ctx["zbb"]
    aubufs = (ctx["au0"], ctx["au1"])
    ctx["_pend"] = ctx.get("_pend", [])
    wkrt = ctx["l0_wkrt"]
    wst = ctx["wst"]

    def rope_pair(psX1, psX2, out1, out2, cols):
        cs_ = cosv[:, cols]
        sn_ = sinv[:, cols]
        P.add("dve", lambda v: v.tensor_mul(rt1[:], psX1[0:16, :], cs_),
              ins=[psX1[0:16, :], cs_], outs=[rt1[:]])
        P.add("dve", lambda v: v.tensor_mul(rt2[:], psX2[0:16, :], sn_),
              ins=[psX2[0:16, :], sn_], outs=[rt2[:]])
        P.add("dve", lambda v: v.tensor_sub(out1, rt1[:], rt2[:]),
              ins=[rt1[:], rt2[:]], outs=[out1])
        P.add("dve", lambda v: v.tensor_mul(rt1[:], psX1[0:16, :], sn_),
              ins=[psX1[0:16, :], sn_], outs=[rt1[:]])
        P.add("dve", lambda v: v.tensor_mul(rt2[:], psX2[0:16, :], cs_),
              ins=[psX2[0:16, :], cs_], outs=[rt2[:]])
        P.add("dve", lambda v: v.tensor_add(out2, rt1[:], rt2[:]),
              ins=[rt1[:], rt2[:]], outs=[out2])

    for qc in range(NQC):
        cs = slice(qc * QC, (qc + 1) * QC)
        xnc = ctx["xnc0"] if qc % 2 == 0 else ctx["xnc1"]
        emit_ln_chunk(P, ctx, xres, xnc[:], qc)
        for ch in range(4):
            ps = ctx["psA2"] if ch % 2 == 0 else ctx["psB2"]
            wdq_c = wst.fetch(P, "l0_wdqt", ch)
            for kc in range(NCH):
                lhs = wdq_c[:, kc, :]
                rhs = xnc[:, kc, :]
                P.add("pe", lambda t, lhs=lhs, rhs=rhs, kc=kc, ps=ps: t.matmul(
                    ps[:], lhs, rhs, start=(kc == 0), stop=(kc == NCH - 1)),
                    ins=[lhs, rhs], outs=[ps[:]])
            dst = cq[:, ch, cs]
            P.add("dve", lambda v, dst=dst, ps=ps: v.tensor_scalar_mul(
                dst, ps[:], RSQ96), ins=[ps[:]], outs=[dst])
        for ch in range(4):
            ps = ctx["psA2"] if ch % 2 == 0 else ctx["psB2"]
            wdkv_c = wst.fetch(P, "l0_wdkvt", ch)
            for kc in range(NCH):
                lhs = wdkv_c[:, kc, :]
                rhs = xnc[:, kc, :]
                P.add("pe", lambda t, lhs=lhs, rhs=rhs, kc=kc, ps=ps: t.matmul(
                    ps[:], lhs, rhs, start=(kc == 0), stop=(kc == NCH - 1)),
                    ins=[lhs, rhs], outs=[ps[:]])
            dst = ckv[:, ch, cs]
            P.add("act", lambda s, dst=dst, ps=ps: s.copy(dst, ps[:]),
                  ins=[ps[:]], outs=[dst])
        for kc in range(NCH):
            lhs1 = wkrt[:, kc, 0:16]
            lhs2 = wkrt[:, kc, 16:32]
            rhs = xnc[:, kc, :]
            P.add("pe", lambda t, lhs1=lhs1, rhs=rhs, kc=kc: t.matmul(
                bankD[0:16, :], lhs1, rhs, start=(kc == 0), stop=(kc == NCH - 1)),
                ins=[lhs1, rhs], outs=[bankD[0:16, :]])
            P.add("pe", lambda t, lhs2=lhs2, rhs=rhs, kc=kc: t.matmul(
                bankE[0:16, :], lhs2, rhs, start=(kc == 0), stop=(kc == NCH - 1)),
                ins=[lhs2, rhs], outs=[bankE[0:16, :]])
        rope_pair(bankD, bankE, kst[64:80, cs], kr2[0:16, cs], cs)

    P.add("sp", lambda e: e.dma_start(out=kst[80:96, :], in_=kr2[0:16, :]),
          ins=[kr2[0:16, :]], outs=[kst[80:96, :]], dma=True)

    for h in range(16):
        def stage(qc):
            wuqt_h = wst.fetch(P, "l0_wuqt", h)
            wqrt_h = wst.fetch(P, "l0_wqrt", h)
            qs_ = slice(qc * QC, (qc + 1) * QC)
            qcs_ = qcs_bufs[qc % 2]
            qr1s_ = qr1s_bufs[qc % 2]
            qr2s_ = qr2s_bufs[qc % 2]
            for lc in range(4):
                lhs = wuqt_h[:, lc, :]
                rhs = cq[:, lc, qs_]
                P.add("pe", lambda t, lhs=lhs, rhs=rhs, lc=lc: t.matmul(
                    bankD[0:64, :], lhs, rhs, start=(lc == 0), stop=(lc == 3)),
                    ins=[lhs, rhs], outs=[bankD[:]])
            P.add("dve", lambda v, qcs_=qcs_: v.tensor_copy(qcs_[0:64, :], bankD[0:64, :]),
                  ins=[bankD[:]], outs=[qcs_[:]])
            qr2s_stage = qr2s_
            for lc in range(4):
                lhs1 = wqrt_h[:, lc, 0:16]
                lhs2 = wqrt_h[:, lc, 16:32]
                rhs = cq[:, lc, qs_]
                P.add("pe", lambda t, lhs1=lhs1, rhs=rhs, lc=lc: t.matmul(
                    bankE[0:16, :], lhs1, rhs, start=(lc == 0), stop=(lc == 3)),
                    ins=[lhs1, rhs], outs=[bankE[:]])
                P.add("pe", lambda t, lhs2=lhs2, rhs=rhs, lc=lc: t.matmul(
                    ctx["psB2"][0:16, :], lhs2, rhs, start=(lc == 0), stop=(lc == 3)),
                    ins=[lhs2, rhs], outs=[ctx["psB2"][:]])
            rope_pair(bankE, ctx["psB2"], qcs_[64:80, :], qr2s_[0:16, :], qs_)
            P.add("sp", lambda e, qcs_=qcs_, qr2s_=qr2s_: e.dma_start(
                out=qcs_[80:96, :], in_=qr2s_[0:16, :]),
                ins=[qr2s_[0:16, :]], outs=[qcs_[80:96, :]], dma=True)

        stage(0)
        wukt_h = wst.fetch(P, "l0_wukt", h)
        for cc in range(NQC):
            ccs = slice(cc * QC, (cc + 1) * QC)
            kb = bankE if cc % 2 == 0 else ctx["psB2"]
            for lc in range(4):
                lhs = wukt_h[:, lc, :]
                rhs = ckv[:, lc, ccs]
                P.add("pe", lambda t, lhs=lhs, rhs=rhs, lc=lc, kb=kb: t.matmul(
                    kb[0:64, :], lhs, rhs, start=(lc == 0), stop=(lc == 3)),
                    ins=[lhs, rhs], outs=[kb[:]])
            dst = kst[0:64, ccs]
            P.add("act", lambda s, dst=dst, kb=kb: s.copy(dst, kb[0:64, :]),
                  ins=[kb[:]], outs=[dst])
        wuvt_h = wst.fetch(P, "l0_wuvt", h)
        for cc in range(NQC):
            ccs = slice(cc * QC, (cc + 1) * QC)
            kb = bankE if cc % 2 == 0 else ctx["psB2"]
            for lc in range(4):
                lhs = wuvt_h[:, lc, :]
                rhs = ckv[:, lc, ccs]
                P.add("pe", lambda t, lhs=lhs, rhs=rhs, lc=lc, kb=kb: t.matmul(
                    kb[0:64, :], lhs, rhs, start=(lc == 0), stop=(lc == 3)),
                    ins=[lhs, rhs], outs=[kb[:]])
            vt = ctx["tmp_bf"] if cc % 2 == 0 else ctx["tmp_bf2"]
            P.add("act", lambda s, vt=vt, kb=kb: s.copy(vt[0:64, :], kb[0:64, :]),
                  ins=[kb[:]], outs=[vt[:]])
            for j in range(4):
                kt = cc * 4 + j
                pv = psV if j % 2 == 0 else psV2
                src_ = vt[0:64, j * 128:(j + 1) * 128]
                P.add("pe", lambda t, src_=src_, pv=pv: t.transpose(
                    pv[:], src_, ctx["eye64_bf"]),
                    ins=[vt[:], ctx["eye64_bf"]], outs=[pv[:]])
                dst = vtm[:, kt, 0:64]
                P.add("act", lambda s, dst=dst, pv=pv: s.copy(dst, pv[:]),
                      ins=[pv[:]], outs=[dst])
        for qc in range(NQC):
            qs = slice(qc * QC, (qc + 1) * QC)
            qcs = qcs_bufs[qc % 2]
            qr1s = qr1s_bufs[qc % 2]
            qr2s = qr2s_bufs[qc % 2]
            nkt = 4 * qc + 4
            sbanks = (bankA, bankB, ctx["psA2"])
            pend_av = []
            for kt in range(nkt):
                ks = slice(kt * 128, (kt + 1) * 128)
                ps = sbanks[kt % 3]
                eb = ebufs[kt % 4][:]
                P.add("pe", lambda t, ks=ks, ps=ps, qcs=qcs: t.matmul(
                    ps[:], kst[0:96, ks], qcs[0:96, :], start=True, stop=True),
                    ins=[kst[0:96, ks], qcs[0:96, :]], outs=[ps[:]])
                if len(pend_av) >= 3:
                    pend_av.pop(0)()
                if kt == 1:
                    for fn in ctx["_pend"]:
                        fn()
                    ctx["_pend"] = []
                    if qc + 1 < NQC:
                        stage(qc + 1)
                P.add("act", lambda s, eb=eb, ps=ps: s.activation(
                    eb, ps[:], AF.Exp), ins=[ps[:]], outs=[eb])
                j = kt - 4 * qc
                if j >= 0:
                    mk = masks[:, j, :]
                    P.add("dve", lambda v, eb=eb, mk=mk: v.scalar_tensor_tensor(
                        eb, eb, CLIP_LO, mk, ALU.max, ALU.mult),
                        ins=[eb, mk], outs=[eb])
                    P.add("dve", lambda v, eb=eb: v.tensor_scalar_min(
                        eb, eb, CLIP_HI), ins=[eb], outs=[eb])
                else:
                    P.add("pool", lambda g, eb=eb: g.tensor_scalar(
                        eb, eb, CLIP_HI, CLIP_LO, ALU.min, ALU.max),
                        ins=[eb], outs=[eb])
                def mk_av(eb=eb, kt=kt, nkt=nkt):
                    P.add("pe", lambda t: t.matmul(
                        bankC[:], vtm[:, kt, :], eb,
                        start=(kt == 0), stop=(kt == nkt - 1)),
                        ins=[vtm[:, kt, :], eb], outs=[bankC[:]])
                pend_av.append(mk_av)
            au = aubufs[(h * NQC + qc) % 2]
            dst = afm[(h % 2) * 64:(h % 2) * 64 + 64, h // 2, qs]

            def tail_fn(tail=tuple(pend_av), au=au):
                for fn in tail:
                    fn()
                P.add("dve", lambda v, au=au: v.tensor_copy(au[0:64, :], bankC[0:64, :]),
                      ins=[bankC[:]], outs=[au[:]])
                P.add("act", lambda s: s.copy(zrow, bankC[64:65, :]),
                      ins=[bankC[:]], outs=[zrow])
                P.add("act", lambda s: s.activation(zrow, zrow, AF.Ln),
                      ins=[zrow], outs=[zrow])
                P.add("dve", lambda v: v.tensor_scalar_mul(zrow, zrow, -1.0),
                      ins=[zrow], outs=[zrow])
                P.add("act", lambda s: s.activation(rzb, zrow, AF.Exp),
                      ins=[zrow], outs=[rzb])
            pend_av = []

            def mk_norm(au=au, dst=dst):
                P.add("pe", lambda t: t.matmul(bankB2[0:64, :], ones1x64b, rzb,
                                               start=True, stop=True),
                      ins=[ones1x64b, rzb], outs=[bankB2[:]])
                P.add("act", lambda s: s.copy(zb[:], bankB2[0:64, :]),
                      ins=[bankB2[:]], outs=[zb[:]])
                P.add("dve", lambda v: v.tensor_mul(dst, au[0:64, :], zb[:]),
                      ins=[au[:], zb[:]], outs=[dst])
            ctx["_pend"] = [tail_fn, mk_norm]
    for fn in ctx["_pend"]:
        fn()
    ctx["_pend"] = []
    for qc in range(NQC):
        qs = slice(qc * QC, (qc + 1) * QC)
        for dch in range(NCH):
            wc = wst.fetch(P, "l0_wot", dch)
            ps = bankA if dch % 2 == 0 else bankB
            for oc in range(NCH):
                lhs = wc[:, oc, :]
                rhs = afm[:, oc, qs]
                P.add("pe", lambda t, lhs=lhs, rhs=rhs, oc=oc, ps=ps: t.matmul(
                    ps[:], lhs, rhs, start=(oc == 0), stop=(oc == NCH - 1)),
                    ins=[lhs, rhs], outs=[ps[:]])
            xs = xres[:, dch, qs]
            P.add("dve", lambda v, xs=xs, ps=ps: v.tensor_add(xs, xs, ps[:]),
                  ins=[xs, ps[:]], outs=[xs])


def build_l0_attn_program():
    nc = bass.Bass("TRN2", target_bir_lowering=False)
    stack = ExitStack()
    ctx = {}
    with stack:
        common_setup(nc, stack, ctx, need=("rope", "masks", "eyes", "ffn4"))
        P = Prog()
        declare_stream_weight(P, ctx, "l0_wdqt", (4, 128, 8, 128))
        declare_stream_weight(P, ctx, "l0_wdkvt", (4, 128, 8, 128))
        declare_stream_weight(P, ctx, "l0_wuqt", (16, 128, 4, 64))
        declare_stream_weight(P, ctx, "l0_wukt", (16, 128, 4, 64))
        declare_stream_weight(P, ctx, "l0_wuvt", (16, 128, 4, 64))
        declare_stream_weight(P, ctx, "l0_wqrt", (16, 128, 4, 32))
        load_resident_weight(P, ctx, "l0_wkrt", (128, 8, 32))
        declare_stream_weight(P, ctx, "l0_wot", (8, 128, 8, 128))
        sb = lambda nm, shp, dt: stack.enter_context(nc.sbuf_tensor(nm, shp, dt))[:]
        ctx["xnc0"] = stack.enter_context(nc.sbuf_tensor("xnc0", [128, NCH, QC], BF16))
        ctx["attn_fm"] = sb("attn_fm", [128, NCH, S], BF16)
        ctx["cq"] = sb("cq", [128, 4, S], BF16)
        ctx["ckv"] = sb("ckv", [128, 4, S], BF16)
        ctx["kr1"] = sb("kr1", [16, S], BF16)
        ctx["kr2"] = sb("kr2", [16, S], BF16)
        ctx["kst"] = sb("kst", [64, S], BF16)
        ctx["vtm"] = sb("vtm", [128, NKT, 65], BF16)
        ctx["qcs"] = sb("qcs", [64, QC], BF16)
        ctx["qcs2"] = sb("qcs2", [64, QC], BF16)
        ctx["qr1s"] = sb("qr1s", [16, QC], BF16)
        ctx["qr1s2"] = sb("qr1s2", [16, QC], BF16)
        ctx["qr2s"] = sb("qr2s", [16, QC], BF16)
        ctx["qr2s2"] = sb("qr2s2", [16, QC], BF16)
        ctx["rt1"] = sb("rt1", [16, QC], F32)
        ctx["rt2"] = sb("rt2", [16, QC], F32)
        ctx["e0"] = sb("e0", [128, QC], BF16)
        ctx["e1"] = sb("e1", [128, QC], BF16)
        ctx["e2"] = sb("e2", [128, QC], BF16)
        ctx["zb"] = sb("zb", [64, QC], F32)
        ctx.setdefault("zbb", ctx["zb"])
        ctx["psVt"] = stack.enter_context(nc.psum_tensor("psV", [128, 128], BF16))
        ctx["psV"] = ctx["psVt"][:, 0:64]
        ctx["psV2"] = ctx["psVt"][:, 0:64]
        ctx["wst"] = WeightStream(ctx, nbufs=3, bufshape=[128, 2048])
        load_common_consts(P, ctx)
        P.add("dve", lambda v, d=ctx["vtm"]: v.memset(d[:, :, 64:65], 1.0),
              ins=[], outs=[ctx["vtm"]])
        for seq in range(SEQ_PER_CORE):
            emit_load_x(P, ctx, seq)
            emit_l0_attn(P, ctx, seq)
            emit_store_x(P, ctx, seq)
        P.barrier()
        P.emit(nc, stack)
        block = stack.enter_context(nc.Block())
        @block.tensor
        def _(t):
            P.run_engine("pe", t)
        @block.scalar
        def _(s):
            P.run_engine("act", s)
        @block.vector
        def _(v):
            P.run_engine("dve", v)
        @block.gpsimd
        def _(g):
            P.run_engine("pool", g)
        @block.sync
        def _(sy):
            P.run_engine("sp", sy)
    return nc





def emit_ln_chunk_into_resid(P, ctx, ysrc, xres, qc, gamma, beta):
    bankA, bankB = ctx["psA"], ctx["psB"]
    bankD, bankE = ctx["psD"], ctx["psE"]
    ones1b = ctx["ones128x1_bf"]
    ones1x128f = ctx["ones_1x128_f"]
    row_m, row_r, row_mr, row_t = (ctx[k][:] for k in
                                   ("row_m", "row_r", "row_mr", "row_t"))
    tmp_f = ctx["tmp_f32"]
    cs = slice(qc * QC, (qc + 1) * QC)
    for c in range(NCH):
        yb = ysrc[:, c, :]
        tb2 = ctx["tmp_bf"] if c % 2 == 0 else ctx["tmp_bf2"]
        P.add("pe", lambda t, yb=yb, c=c: t.matmul(
            bankD[0:1, :], ones1b, yb, start=(c == 0), stop=(c == NCH - 1)),
            ins=[ones1b, yb], outs=[bankD[0:1, :]])
        P.add("act", lambda s, yb=yb, tb2=tb2: s.activation(
            tb2[:], yb, AF.Square), ins=[yb], outs=[tb2[:]])
        P.add("pe", lambda t, tb2=tb2, c=c: t.matmul(
            bankE[0:1, :], ones1b, tb2[:], start=(c == 0), stop=(c == NCH - 1)),
            ins=[ones1b, tb2[:]], outs=[bankE[0:1, :]])
    P.add("dve", lambda v: v.tensor_scalar_mul(row_m, bankD[0:1, :], 1.0 / D),
          ins=[bankD[0:1, :]], outs=[row_m])
    P.add("act", lambda s: s.activation(row_t, row_m, AF.Square),
          ins=[row_m], outs=[row_t])
    P.add("dve", lambda v: v.tensor_scalar_mul(row_r, bankE[0:1, :], 1.0 / D),
          ins=[bankE[0:1, :]], outs=[row_r])
    P.add("dve", lambda v: v.tensor_sub(row_r, row_r, row_t),
          ins=[row_r, row_t], outs=[row_r])
    P.add("dve", lambda v: v.tensor_scalar_add(row_r, row_r, 1e-5),
          ins=[row_r], outs=[row_r])
    P.add("act", lambda s: s.activation(row_r, row_r, AF.Ln),
          ins=[row_r], outs=[row_r])
    P.add("dve", lambda v: v.tensor_scalar_mul(row_r, row_r, -0.5),
          ins=[row_r], outs=[row_r])
    P.add("act", lambda s: s.activation(row_r, row_r, AF.Exp),
          ins=[row_r], outs=[row_r])
    P.add("dve", lambda v: v.scalar_tensor_tensor(
        row_mr, row_m, -1.0, row_r, ALU.mult, ALU.mult),
        ins=[row_m, row_r], outs=[row_mr])
    P.add("pe", lambda t: t.matmul(bankA[:], ones1x128f, row_r,
                                   start=True, stop=True),
          ins=[ones1x128f, row_r], outs=[bankA[:]])
    P.add("pe", lambda t: t.matmul(bankB[:], ones1x128f, row_mr,
                                   start=True, stop=True),
          ins=[ones1x128f, row_mr], outs=[bankB[:]])
    for c in range(NCH):
        yb = ysrc[:, c, :]
        xs = xres[:, c, cs]
        g = gamma[:, c:c + 1]
        bta = beta[:, c:c + 1]
        P.add("dve", lambda v, yb=yb: v.tensor_mul(tmp_f[:], yb, bankA[:]),
              ins=[yb, bankA[:]], outs=[tmp_f[:]])
        P.add("dve", lambda v: v.tensor_add(tmp_f[:], tmp_f[:], bankB[:]),
              ins=[tmp_f[:], bankB[:]], outs=[tmp_f[:]])
        P.add("dve", lambda v, xs=xs, g=g: v.scalar_tensor_tensor(
            tmp_f[:], tmp_f[:], g, xs, ALU.mult, ALU.add),
            ins=[tmp_f[:], gamma[:, :], xs], outs=[tmp_f[:]])
        P.add("dve", lambda v, xs=xs, bta=bta: v.tensor_scalar(
            xs, tmp_f[:], bta, None, ALU.add),
            ins=[tmp_f[:], beta[:, :]], outs=[xs])


def emit_l1_attn(P, ctx, seq):
    xres = ctx["xres"]
    bankA, bankB = ctx["psA"], ctx["psB"]
    bankC, bankD, bankE = ctx["psC"], ctx["psD"], ctx["psE"]
    bankC2 = ctx["psC2"]
    psV, psV2 = ctx["psV"], ctx["psV2"]
    q4 = ctx["q4"]
    kst, vtm = ctx["kst"], ctx["vtm"]
    qst_bufs = (ctx["qcs"], ctx["qcs2"])
    negb_bufs = (ctx["negb"], ctx["negb2"])
    onesS = ctx["onesS"]
    ebufs = (ctx["e0"], ctx["e1"], ctx["e2"], ctx["e3"])
    afm = ctx["attn_fm"]
    ychunk = ctx["ychunk"]
    zrow = ctx["row_m"][:]
    rzb = ctx["rzb"]
    aubufs = (ctx["au0"], ctx["au1"])
    ctx["_pend"] = ctx.get("_pend", [])
    rowq = ctx["row_mr"][:]
    maxk = ctx["maxk_sb"][0:1, 0:1]
    mk2 = ctx["maxk_sb"][0:1, 1:2]
    zb = ctx["zbb"]
    bankB2 = ctx["psB2"]
    wst = ctx["wst"]
    nw, nb = ctx["l1_nw"], ctx["l1_nb"]
    onw, onb = ctx["l1_onw"], ctx["l1_onb"]
    ones1b = ctx["ones128x1_bf"]
    ones1x64b = ctx["ones_1x64_bf"]

    for qc in range(NQC):
        qs = slice(qc * QC, (qc + 1) * QC)
        xn1 = ctx["xnc0"]
        xn2 = ctx["xnc1"]
        emit_ln_chunk(P, ctx, xres, xn1[:], qc)
        emit_ln_chunk(P, ctx, xn1[:], xn2[:], 0, gamma=nw, beta=nb,
                      src_bf16=True)
        for ch in range(4):
            ps = bankA if ch % 2 == 0 else bankB
            wq_c = wst.fetch(P, "l1_wq", ch)
            for kc in range(NCH):
                lhs = wq_c[:, kc, :]
                rhs = xn2[:, kc, :]
                P.add("pe", lambda t, lhs=lhs, rhs=rhs, kc=kc, ps=ps: t.matmul(
                    ps[:], lhs, rhs, start=(kc == 0), stop=(kc == NCH - 1)),
                    ins=[lhs, rhs], outs=[ps[:]])
            dst = q4[:, ch, qs]
            P.add("dve", lambda v, dst=dst, ps=ps: v.tensor_scalar_mul(
                dst, ps[:], 0.125), ins=[ps[:]], outs=[dst])
        wkv_c = wst.fetch(P, "l1_wkv", 0)
        for kc in range(NCH):
            lhs = wkv_c[:, kc, :]
            rhs = xn2[:, kc, :]
            P.add("pe", lambda t, lhs=lhs, rhs=rhs, kc=kc: t.matmul(
                bankC2[:], lhs, rhs, start=(kc == 0), stop=(kc == NCH - 1)),
                ins=[lhs, rhs], outs=[bankC2[:]])
        dstk = kst[0:64, qs]
        P.add("act", lambda s, dstk=dstk: s.copy(dstk, bankC2[0:64, :]),
              ins=[bankC2[:]], outs=[dstk])
        vt = ctx["tmp_bf"]
        P.add("act", lambda s, vt=vt: s.copy(vt[0:64, :], bankC2[64:128, :]),
              ins=[bankC2[:]], outs=[vt[:]])
        for j in range(4):
            kt = qc * 4 + j
            pv = psV if j % 2 == 0 else psV2
            src_ = vt[0:64, j * 128:(j + 1) * 128]
            P.add("pe", lambda t, src_=src_, pv=pv: t.transpose(
                pv[:], src_, ctx["eye64_bf"]),
                ins=[vt[:], ctx["eye64_bf"]], outs=[pv[:]])
            dst = vtm[:, kt, 0:64]
            P.add("act", lambda s, dst=dst, pv=pv: s.copy(dst, pv[:]),
                  ins=[pv[:]], outs=[dst])
        tb2 = ctx["tmp_bf2"]
        P.add("act", lambda s, dstk=dstk, tb2=tb2: s.activation(
            tb2[0:64, :], dstk, AF.Square), ins=[dstk], outs=[tb2[:]])
        P.add("pe", lambda t, tb2=tb2: t.matmul(
            bankE[0:1, :], ones1b[0:64, :], tb2[0:64, :], start=True, stop=True),
            ins=[ones1b, tb2[:]], outs=[bankE[:]])
        if qc == 0:
            P.add("dve", lambda v: v.reduce_max(maxk, bankE[0:1, :], axis=AX.X),
                  ins=[bankE[:]], outs=[maxk])
        else:
            P.add("dve", lambda v: v.reduce_max(mk2, bankE[0:1, :], axis=AX.X),
                  ins=[bankE[:]], outs=[mk2])
            P.add("dve", lambda v: v.tensor_max(maxk, maxk, mk2),
                  ins=[maxk, mk2], outs=[maxk])
    P.add("dve", lambda v: v.memset(kst[64:65, :], 1.0),
          ins=[], outs=[kst[64:65, :]])
    P.add("act", lambda s: s.activation(mk2, maxk, AF.Ln),
          ins=[maxk], outs=[mk2])
    P.add("dve", lambda v: v.tensor_scalar_mul(mk2, mk2, 0.5),
          ins=[mk2], outs=[mk2])
    P.add("act", lambda s: s.activation(mk2, mk2, AF.Exp),
          ins=[mk2], outs=[mk2])

    for qc in range(NQC):
        qs = slice(qc * QC, (qc + 1) * QC)

        def stage_l1(h, sqc=None):
            if sqc is None:
                sqc = qc
            sqs = slice(sqc * QC, (sqc + 1) * QC)
            qst_ = qst_bufs[h % 2]
            negb_ = negb_bufs[h % 2]
            qsrc = q4[(h % 2) * 64:(h % 2) * 64 + 64, h // 2, sqs]
            P.add("dve", lambda v, qsrc=qsrc, qst_=qst_: v.tensor_copy(
                qst_[0:64, :], qsrc), ins=[qsrc], outs=[qst_[:]])
            tbq = ctx["tmp_bf2"]
            P.add("act", lambda s, tbq=tbq, qst_=qst_: s.activation(
                tbq[0:64, :], qst_[0:64, :], AF.Square), ins=[qst_[:]], outs=[tbq[:]])
            P.add("pe", lambda t, tbq=tbq: t.matmul(
                bankE[0:1, :], ones1b[0:64, :], tbq[0:64, :], start=True, stop=True),
                ins=[ones1b, tbq[:]], outs=[bankE[:]])
            P.add("act", lambda s: s.copy(rowq, bankE[0:1, :]),
                  ins=[bankE[:]], outs=[rowq])
            P.add("act", lambda s: s.activation(rowq, rowq, AF.Ln),
                  ins=[rowq], outs=[rowq])
            P.add("dve", lambda v: v.tensor_scalar_mul(rowq, rowq, 0.5),
                  ins=[rowq], outs=[rowq])
            P.add("act", lambda s: s.activation(rowq, rowq, AF.Exp),
                  ins=[rowq], outs=[rowq])
            P.add("dve", lambda v, qst_=qst_: v.tensor_scalar(
                qst_[64:65, :], rowq, mk2, -1.0, ALU.mult, ALU.mult),
                ins=[rowq, ctx["maxk_sb"][0:1, 1:2]], outs=[qst_[64:65, :]])

        if qc == 0:
            stage_l1(0)
        for h in range(8):
            qst = qst_bufs[h % 2]
            negb = negb_bufs[h % 2]
            sbanks = (bankA, bankB, bankC2)
            pend_av = []
            for kt in range(NKT):
                ks = slice(kt * 128, (kt + 1) * 128)
                ps = sbanks[kt % 3]
                eb = ebufs[kt % 4][:]
                P.add("pe", lambda t, ks=ks, ps=ps, qst=qst: t.matmul(
                    ps[:], kst[0:65, ks], qst[0:65, :], start=True, stop=True),
                    ins=[kst[0:65, ks], qst[0:65, :]], outs=[ps[:]])
                if len(pend_av) >= 3:
                    pend_av.pop(0)()
                if kt == 1:
                    for fn in ctx["_pend"]:
                        fn()
                    ctx["_pend"] = []
                    if h + 1 < 8:
                        stage_l1(h + 1)
                P.add("act", lambda s, eb=eb, ps=ps: s.activation(
                    eb, ps[:], AF.Exp), ins=[ps[:]], outs=[eb])
                def mk_av(eb=eb, kt=kt):
                    P.add("pe", lambda t: t.matmul(
                        bankC[:], vtm[:, kt, :], eb,
                        start=(kt == 0), stop=(kt == NKT - 1)),
                        ins=[vtm[:, kt, :], eb], outs=[bankC[:]])
                pend_av.append(mk_av)
            au = aubufs[(h + qc) % 2]
            dst = afm[(h % 2) * 64:(h % 2) * 64 + 64, h // 2, qs]

            def tail_fn(tail=tuple(pend_av), au=au):
                for fn in tail:
                    fn()
                P.add("dve", lambda v, au=au: v.tensor_copy(au[0:64, :], bankC[0:64, :]),
                      ins=[bankC[:]], outs=[au[:]])
                P.add("act", lambda s: s.copy(zrow, bankC[64:65, :]),
                      ins=[bankC[:]], outs=[zrow])
                P.add("act", lambda s: s.activation(zrow, zrow, AF.Ln),
                      ins=[zrow], outs=[zrow])
                P.add("dve", lambda v: v.tensor_scalar_mul(zrow, zrow, -1.0),
                      ins=[zrow], outs=[zrow])
                P.add("act", lambda s: s.activation(rzb, zrow, AF.Exp),
                      ins=[zrow], outs=[rzb])
            pend_av = []

            def mk_norm(au=au, dst=dst):
                P.add("pe", lambda t: t.matmul(bankB2[0:64, :], ones1x64b, rzb,
                                               start=True, stop=True),
                      ins=[ones1x64b, rzb], outs=[bankB2[:]])
                P.add("act", lambda s: s.copy(zb[:], bankB2[0:64, :]),
                      ins=[bankB2[:]], outs=[zb[:]])
                P.add("dve", lambda v: v.tensor_mul(dst, au[0:64, :], zb[:]),
                      ins=[au[:], zb[:]], outs=[dst])
            ctx["_pend"] = [tail_fn, mk_norm]
        for fn in ctx["_pend"]:
            fn()
        ctx["_pend"] = []
        if qc + 1 < NQC:
            stage_l1(0, qc + 1)
        for fn in ctx["_pend"]:
            fn()
        ctx["_pend"] = []
        for dch in range(NCH):
            ps = bankA if dch % 2 == 0 else bankB
            wo_c = wst.fetch(P, "l1_wo", dch)
            for oc in range(4):
                lhs = wo_c[:, oc, :]
                rhs = afm[:, oc, qs]
                P.add("pe", lambda t, lhs=lhs, rhs=rhs, oc=oc, ps=ps: t.matmul(
                    ps[:], lhs, rhs, start=(oc == 0), stop=(oc == 3)),
                    ins=[lhs, rhs], outs=[ps[:]])
            dst = ychunk[:, dch, :]
            P.add("act", lambda s, dst=dst, ps=ps: s.copy(dst, ps[:]),
                  ins=[ps[:]], outs=[dst])
        emit_ln_chunk_into_resid(P, ctx, ychunk, xres, qc, onw, onb)


def build_l1_attn_program():
    nc = bass.Bass("TRN2", target_bir_lowering=False)
    stack = ExitStack()
    ctx = {}
    with stack:
        common_setup(nc, stack, ctx, need=("eyes", "ffn4"))
        P = Prog()
        declare_stream_weight(P, ctx, "l1_wq", (4, 128, 8, 128))
        declare_stream_weight(P, ctx, "l1_wkv", (1, 128, 8, 128))
        declare_stream_weight(P, ctx, "l1_wo", (8, 128, 4, 128))
        for nm in ("l1_nw", "l1_nb", "l1_onw", "l1_onb"):
            load_resident_vec(P, ctx, nm, NCH)
        sb = lambda nm, shp, dt: stack.enter_context(nc.sbuf_tensor(nm, shp, dt))[:]
        ctx["xnc0"] = stack.enter_context(nc.sbuf_tensor("xnc0", [128, NCH, QC], BF16))
        ctx["xnc1"] = stack.enter_context(nc.sbuf_tensor("xnc1", [128, NCH, QC], BF16))
        ctx["q4"] = sb("q4", [128, 4, S], BF16)
        ctx["attn_fm"] = sb("attn_fm", [128, 4, S], BF16)
        ctx["ychunk"] = sb("ychunk", [128, NCH, QC], BF16)
        ctx["kst"] = sb("kst", [64, S], BF16)
        ctx["vtm"] = sb("vtm", [128, NKT, 65], BF16)
        ctx["qcs"] = sb("qcs", [64, QC], BF16)
        ctx["e2"] = sb("e2", [128, QC], BF16)
        ctx["negb"] = sb("negb", [1, QC], BF16)
        ctx["negb2"] = sb("negb2", [1, QC], BF16)
        ctx["qcs2"] = sb("qcs2l1", [64, QC], BF16)
        ctx["onesS"] = sb("onesS", [1, S], BF16)
        ctx["maxk_sb"] = ctx["nc"] and stack.enter_context(nc.sbuf_tensor("maxk_sbx", [1, 2], F32))[:]

        ctx["e0"] = sb("e0", [128, QC], BF16)
        ctx["e1"] = sb("e1", [128, QC], BF16)
        ctx["zb"] = sb("zb", [64, QC], F32)
        ctx.setdefault("zbb", ctx["zb"])
        ctx["psVt"] = stack.enter_context(nc.psum_tensor("psV", [128, 128], BF16))
        ctx["psV"] = ctx["psVt"][:, 0:64]
        ctx["psV2"] = ctx["psVt"][:, 0:64]
        ctx["psC2"] = ctx["psA2"]
        ctx["wst"] = WeightStream(ctx, nbufs=3, bufshape=[128, 4096])
        load_common_consts(P, ctx)
        P.add("dve", lambda v, d=ctx["vtm"]: v.memset(d[:, :, 64:65], 1.0),
              ins=[], outs=[ctx["vtm"]])
        P.add("dve", lambda v: v.memset(ctx["onesS"], 1.0),
              ins=[], outs=[ctx["onesS"]])
        for seq in range(SEQ_PER_CORE):
            emit_load_x(P, ctx, seq)
            emit_l1_attn(P, ctx, seq)
            emit_store_x(P, ctx, seq)
        P.barrier()
        P.emit(nc, stack)
        block = stack.enter_context(nc.Block())
        @block.tensor
        def _(t):
            P.run_engine("pe", t)
        @block.scalar
        def _(s):
            P.run_engine("act", s)
        @block.vector
        def _(v):
            P.run_engine("dve", v)
        @block.gpsimd
        def _(g):
            P.run_engine("pool", g)
        @block.sync
        def _(sy):
            P.run_engine("sp", sy)
    return nc



def emit_l1_ffn(P, ctx, seq):
    xres = ctx["xres"]
    xn = ctx["xn"]
    bankA, bankB = ctx["psA"], ctx["psB"]
    rbuf = ctx["rbuf"]
    xr_bufs = (ctx["xrb"], ctx["xrb2"])
    xk_bufs = (ctx["xkb"], ctx["xkb2"])
    kact = ctx["kact"]
    wst = ctx["wst"]
    mur, muk = ctx["l1_mur"], ctx["l1_muk"]
    onemur, onemuk = ctx["onemur"], ctx["onemuk"]
    tmp_f = ctx["tmp_f32"]
    rot4f = [bankA, bankB, ctx.get("psA2", bankA), ctx.get("psB2", bankB)]

    def lnmix(qc):
        cs = slice(qc * QC, (qc + 1) * QC)
        xr = xr_bufs[qc % 2]
        xk = xk_bufs[qc % 2]
        emit_ln_chunk(P, ctx, xres, xn[:, :, cs], qc)
        for c in range(NCH):
            for (dst, mu, onemu) in ((xr, mur, onemur), (xk, muk, onemuk)):
                t1 = ctx["tmp_bf"]
                xcs = xn[:, c, cs]
                P.add("dve", lambda v, xcs=xcs, mu=mu, c=c, t1=t1: v.tensor_scalar(
                    t1[:], xcs, mu[:, c:c + 1], None, ALU.mult),
                    ins=[xcs, mu[:, :]], outs=[t1[:]])
                d = dst[:, c, :]
                if qc == 0:
                    P.add("dve", lambda v, d=d, t1=t1: v.tensor_copy(d[:, 0:1], t1[:, 0:1]),
                          ins=[t1[:]], outs=[d])
                    xprev = xn[:, c, 0:QC - 1]
                    P.add("dve", lambda v, d=d, xprev=xprev, onemu=onemu, c=c, t1=t1:
                          v.scalar_tensor_tensor(d[:, 1:QC], xprev, onemu[:, c:c + 1],
                                                 t1[:, 1:QC], ALU.mult, ALU.add),
                          ins=[xprev, onemu[:, :], t1[:]], outs=[d])
                else:
                    xprev = xn[:, c, qc * QC - 1:(qc + 1) * QC - 1]
                    P.add("dve", lambda v, d=d, xprev=xprev, onemu=onemu, c=c, t1=t1:
                          v.scalar_tensor_tensor(d, xprev, onemu[:, c:c + 1],
                                                 t1[:], ALU.mult, ALU.add),
                          ins=[xprev, onemu[:, :], t1[:]], outs=[d])

    lnmix(0)
    for qc in range(NQC):
        cs = slice(qc * QC, (qc + 1) * QC)
        xr = xr_bufs[qc % 2]
        xk = xk_bufs[qc % 2]
        for dch in range(NCH):
            ps = rot4f[dch % 4]
            wr_c = wst.fetch(P, "l1_wrt", dch)
            for kc in range(NCH):
                lhs = wr_c[:, kc, :]
                rhs = xr[:, kc, :]
                P.add("pe", lambda t, lhs=lhs, rhs=rhs, kc=kc, ps=ps: t.matmul(
                    ps[:], lhs, rhs, start=(kc == 0), stop=(kc == NCH - 1)),
                    ins=[lhs, rhs], outs=[ps[:]])
            dst = rbuf[:, dch, :]
            P.add("act", lambda s, dst=dst, ps=ps: s.activation(
                dst, ps[:], AF.Sigmoid), ins=[ps[:]], outs=[dst])
            if dch == 0 and qc + 1 < NQC:
                lnmix(qc + 1)
        for dch in range(NCH):
            ps = rot4f[dch % 4]
            wk_c = wst.fetch(P, "l1_wkt", dch)
            for kc in range(NCH):
                lhs = wk_c[:, kc, :]
                rhs = xk[:, kc, :]
                P.add("pe", lambda t, lhs=lhs, rhs=rhs, kc=kc, ps=ps: t.matmul(
                    ps[:], lhs, rhs, start=(kc == 0), stop=(kc == NCH - 1)),
                    ins=[lhs, rhs], outs=[ps[:]])
            tb = ctx["tmp_bf2"]
            P.add("dve", lambda v, tb=tb, ps=ps: v.tensor_scalar_max(
                tb[:], ps[:], 0.0), ins=[ps[:]], outs=[tb[:]])
            dst = kact[:, dch, :]
            P.add("act", lambda s, dst=dst, tb=tb: s.activation(
                dst, tb[:], AF.Square), ins=[tb[:]], outs=[dst])
        for dch in range(NCH):
            ps = rot4f[dch % 4]
            wv_c = wst.fetch(P, "l1_wvt", dch)
            for kc in range(NCH):
                lhs = wv_c[:, kc, :]
                rhs = kact[:, kc, :]
                P.add("pe", lambda t, lhs=lhs, rhs=rhs, kc=kc, ps=ps: t.matmul(
                    ps[:], lhs, rhs, start=(kc == 0), stop=(kc == NCH - 1)),
                    ins=[lhs, rhs], outs=[ps[:]])
            rs = rbuf[:, dch, :]
            xs = xres[:, dch, cs]
            P.add("dve", lambda v, rs=rs, ps=ps: v.tensor_mul(tmp_f[:], ps[:], rs),
                  ins=[ps[:], rs], outs=[tmp_f[:]])
            P.add("dve", lambda v, xs=xs: v.tensor_add(xs, xs, tmp_f[:]),
                  ins=[xs, tmp_f[:]], outs=[xs])


def build_l1_ffn_program():
    phases = ("l0a", "l0f", "l1a", "l1f")
    nc = bass.Bass("TRN2", target_bir_lowering=False)
    stack = ExitStack()
    ctx = {}
    with stack:
        common_setup(nc, stack, ctx, need=("xnfull",))
        P = Prog()
        declare_stream_weight(P, ctx, "l1_wrt", (8, 128, 8, 128))
        declare_stream_weight(P, ctx, "l1_wkt", (8, 128, 8, 128))
        declare_stream_weight(P, ctx, "l1_wvt", (8, 128, 8, 128))
        load_resident_vec(P, ctx, "l1_mur", NCH)
        load_resident_vec(P, ctx, "l1_muk", NCH)
        sb = lambda nm, shp, dt: stack.enter_context(nc.sbuf_tensor(nm, shp, dt))[:]
        ctx["rbuf"] = sb("rbuf", [128, NCH, QC], F32)
        ctx["xrb"] = sb("xrb", [128, NCH, QC], BF16)
        ctx["xkb"] = sb("xkb", [128, NCH, QC], BF16)
        ctx["xrb2"] = sb("xrb2", [128, NCH, QC], BF16)
        ctx["xkb2"] = sb("xkb2", [128, NCH, QC], BF16)
        ctx["kact"] = sb("kact", [128, NCH, QC], BF16)
        ctx["onemur"] = sb("onemur", [128, NCH], F32)
        ctx["onemuk"] = sb("onemuk", [128, NCH], F32)
        ctx["wst"] = WeightStream(ctx, nbufs=3, bufshape=[128, 4096])
        load_common_consts(P, ctx)
        P.add("dve", lambda v: v.tensor_scalar(
            ctx["onemur"], ctx["l1_mur"], -1.0, 1.0, ALU.mult, ALU.add),
            ins=[ctx["l1_mur"]], outs=[ctx["onemur"]])
        P.add("dve", lambda v: v.tensor_scalar(
            ctx["onemuk"], ctx["l1_muk"], -1.0, 1.0, ALU.mult, ALU.add),
            ins=[ctx["l1_muk"]], outs=[ctx["onemuk"]])
        for seq in range(SEQ_PER_CORE):
            emit_load_x(P, ctx, seq)
            if "l1f" in phases:
                emit_l1_ffn(P, ctx, seq)
            emit_store_x(P, ctx, seq)
        P.barrier()
        P.emit(nc, stack)
        block = stack.enter_context(nc.Block())
        @block.tensor
        def _(t):
            P.run_engine("pe", t)
        @block.scalar
        def _(s):
            P.run_engine("act", s)
        @block.vector
        def _(v):
            P.run_engine("dve", v)
        @block.gpsimd
        def _(g):
            P.run_engine("pool", g)
        @block.sync
        def _(sy):
            P.run_engine("sp", sy)
    return nc





def _carve(ctx, arena, start, shape):
    ar = ctx[arena]
    if len(shape) == 3:
        p, a, b = shape
        assert p == 128
        return ar[:, start:start + a * b].rearrange("p (a b) -> p a b", a=a)
    p, n = shape
    return ar[0:p, start:start + n]


def build_fused_program(phases=("l0a", "l0f", "l1a", "l1f")):
    nc = bass.Bass("TRN2", target_bir_lowering=False)
    stack = ExitStack()
    ctx = {}
    with stack:
        common_setup(nc, stack, ctx, need=("eyes", "ffn4", "masks", "ropearena", "rowsarena"))
        ctx["psC2"] = ctx["psA2"]
        ctx["psVt"] = stack.enter_context(nc.psum_tensor("psV", [128, 128], BF16))
        ctx["psV"] = ctx["psVt"][:, 0:64]
        ctx["psV2"] = ctx["psVt"][:, 0:64]
        ctx["arBF_h"] = stack.enter_context(
            nc.sbuf_tensor("arBF", [128, 49152], BF16))
        ctx["arF_h"] = stack.enter_context(
            nc.sbuf_tensor("arF", [128, 3072], F32))
        ctx["tmp_f32b"] = stack.enter_context(
            nc.sbuf_tensor("tmp_f32b", [128, QC], F32))
        ctx["negb"] = stack.enter_context(nc.sbuf_tensor("negb", [1, QC], BF16))[:]
        ctx["negb2"] = stack.enter_context(nc.sbuf_tensor("negb2", [1, QC], BF16))[:]
        ctx["maxk_sb"] = stack.enter_context(nc.sbuf_tensor("maxk_sb", [1, 2], F32))[:]
        P = Prog()
        P.region_pitch["arBF"] = (49152, 512)
        P.region_pitch["psV"] = (128, 64)
        P.region_pitch["arF"] = (3072, 512)
        ctx["row_m"] = _carve(ctx, "arF_h", 0, (1, QC))
        ctx["row_r"] = _carve(ctx, "arF_h", 512, (1, QC))
        ctx["row_mr"] = _carve(ctx, "arF_h", 1024, (1, QC))
        ctx["row_t"] = _carve(ctx, "arF_h", 1536, (1, QC))
        ctx["rt1"] = _carve(ctx, "arF_h", 2048, (16, QC))
        ctx["rt2"] = _carve(ctx, "arF_h", 2560, (16, QC))
        for nm, shp in [
            ("l0_wdqt", (4, 128, 8, 128)), ("l0_wdkvt", (4, 128, 8, 128)),
            ("l0_wuqt", (16, 128, 4, 64)), ("l0_wukt", (16, 128, 4, 64)),
            ("l0_wuvt", (16, 128, 4, 64)), ("l0_wqrt", (16, 128, 4, 32)),
            ("l0_wot", (8, 128, 8, 128)),
            ("l0_w1t", (16, 128, 8, 256)), ("l0_w2t", (16, 128, 8, 256)),
            ("l0_h1t", (32, 128, 32, 128)), ("l0_h2t", (32, 128, 32, 128)),
            ("l0_w3t", (8, 128, 32, 128)),
            ("l1_wq", (4, 128, 8, 128)), ("l1_wkv", (1, 128, 8, 128)),
            ("l1_wo", (8, 128, 4, 128)),
            ("l1_wrt", (8, 128, 8, 128)), ("l1_wkt", (8, 128, 8, 128)),
            ("l1_wvt", (8, 128, 8, 128)),
        ]:
            declare_stream_weight(P, ctx, nm, shp)
        load_resident_weight(P, ctx, "l0_wkrt", (128, 8, 32))
        for nm in ("l0_s1", "l0_s2", "l0_sh1", "l0_sh2"):
            load_resident_vec(P, ctx, nm, 32)
        load_resident_vec(P, ctx, "l0_s3", 8)
        for nm in ("l1_nw", "l1_nb", "l1_onw", "l1_onb", "l1_mur", "l1_muk"):
            load_resident_vec(P, ctx, nm, NCH)
        ctx["onemur"] = stack.enter_context(
            nc.sbuf_tensor("onemur", [128, NCH], F32))[:]
        ctx["onemuk"] = stack.enter_context(
            nc.sbuf_tensor("onemuk", [128, NCH], F32))[:]
        ctx["wst"] = WeightStream(ctx, nbufs=2, bufshape=[128, 4096])
        wst = ctx["wst"]
        load_common_consts(P, ctx)
        P.add("dve", lambda v: v.tensor_scalar(
            ctx["onemur"], ctx["l1_mur"], -1.0, 1.0, ALU.mult, ALU.add),
            ins=[ctx["l1_mur"]], outs=[ctx["onemur"]])
        P.add("dve", lambda v: v.tensor_scalar(
            ctx["onemuk"], ctx["l1_muk"], -1.0, 1.0, ALU.mult, ALU.add),
            ins=[ctx["l1_muk"]], outs=[ctx["onemuk"]])

        for seq in range(SEQ_PER_CORE):
            emit_load_x(P, ctx, seq)
            if "l0a" not in phases:
                pass
            ctx["attn_fm"] = _carve(ctx, "arBF_h", 0, (128, NCH, S))
            ctx["cq"] = _carve(ctx, "arBF_h", 16384, (128, 4, S))
            ctx["ckv"] = _carve(ctx, "arBF_h", 24576, (128, 4, S))
            ctx["kr1"] = _carve(ctx, "arBF_h", 32768, (16, S))
            ctx["kr2"] = _carve(ctx, "arBF_h", 34816, (16, S))
            ctx["kst"] = _carve(ctx, "arBF_h", 36864, (96, S))
            ctx["vtm"] = _carve(ctx, "arBF_h", 38912, (128, NKT, 65))
            ctx["rope_cos"] = _carve(ctx, "arBF_h", 40448, (16, S))
            ctx["rope_sin"] = _carve(ctx, "arBF_h", 42496, (16, S))
            ctx["e0"] = _carve(ctx, "arBF_h", 44544, (128, QC))
            ctx["e1"] = _carve(ctx, "arBF_h", 45056, (128, QC))
            ctx["e2"] = _carve(ctx, "arBF_h", 45568, (128, QC))
            ctx["e3"] = _carve(ctx, "arBF_h", 32768, (128, QC))
            ctx["qcs"] = _carve(ctx, "arBF_h", 46080, (96, QC))
            ctx["qcs2"] = _carve(ctx, "arBF_h", 46592, (96, QC))
            ctx["qr1s"] = _carve(ctx, "arBF_h", 47104, (16, QC))
            ctx["qr1s2"] = _carve(ctx, "arBF_h", 47616, (16, QC))
            ctx["qr2s"] = _carve(ctx, "arBF_h", 48128, (16, QC))
            ctx["qr2s2"] = _carve(ctx, "arBF_h", 48640, (16, QC))
            ctx["xnc0"] = _carve(ctx, "arBF_h", 0, (128, NCH, QC))
            ctx["xnc1"] = _carve(ctx, "arBF_h", 4096, (128, NCH, QC))
            P.add("dve", lambda v, d=ctx["vtm"]: v.memset(d[:, :, 64:65], 1.0),
                  ins=[], outs=[ctx["vtm"]])
            if "l0a" in phases:
                for nm, eng in (("rope_cos", "pool"), ("rope_sin", "pool")):
                    dst, sap = ctx[nm], ctx[nm + "_dram"]
                    P.add(eng, lambda e, dst=dst, sap=sap: e.dma_start(out=dst, in_=sap),
                          ins=[], outs=[dst], dma=True)
            if "l0a" in phases:
                emit_l0_attn(P, ctx, seq)
            ctx["xn"] = _carve(ctx, "arBF_h", 32768, (128, NCH, S))
            ctx["ffn_a"] = _carve(ctx, "arBF_h", 0, (128, 32, QC))
            ctx["ffn_b"] = _carve(ctx, "arBF_h", 16384, (128, 32, QC))
            if "l0f" in phases:
                emit_l0_ffn(P, ctx, ctx["xres"], ctx["xn"], wst, seq,
                            inline_ln=True)
            ctx["q4"] = _carve(ctx, "arBF_h", 0, (128, 4, S))
            ctx["attn_fm"] = _carve(ctx, "arBF_h", 8192, (128, 4, S))
            ctx["ychunk"] = _carve(ctx, "arBF_h", 16384, (128, NCH, QC))
            ctx["kst"] = _carve(ctx, "arBF_h", 20480, (65, S))
            ctx["vtm"] = _carve(ctx, "arBF_h", 22528, (128, NKT, 65))
            ctx["onesS"] = _carve(ctx, "arBF_h", 24064, (1, S))
            ctx["e0"] = _carve(ctx, "arBF_h", 26112, (128, QC))
            ctx["e1"] = _carve(ctx, "arBF_h", 26624, (128, QC))
            ctx["e2"] = _carve(ctx, "arBF_h", 27136, (128, QC))
            ctx["e3"] = _carve(ctx, "arBF_h", 24064, (128, QC))
            ctx["qcs"] = _carve(ctx, "arBF_h", 27648, (65, QC))
            ctx["qcs2"] = _carve(ctx, "arBF_h", 28160, (65, QC))
            ctx["xnc0"] = _carve(ctx, "arBF_h", 28672, (128, NCH, QC))
            ctx["xnc1"] = _carve(ctx, "arBF_h", 32768, (128, NCH, QC))
            if "l1a" in phases:
                P.add("dve", lambda v, d=ctx["onesS"]: v.memset(d, 1.0),
                      ins=[], outs=[ctx["onesS"]])
                P.add("dve", lambda v, d=ctx["vtm"]: v.memset(d[:, :, 64:65], 1.0),
                      ins=[], outs=[ctx["vtm"]])
            if "l1a" in phases:
                emit_l1_attn(P, ctx, seq)
            ctx["xn"] = _carve(ctx, "arBF_h", 0, (128, NCH, S))
            ctx["xrb"] = _carve(ctx, "arBF_h", 16384, (128, NCH, QC))
            ctx["xkb"] = _carve(ctx, "arBF_h", 20480, (128, NCH, QC))
            ctx["kact"] = _carve(ctx, "arBF_h", 24576, (128, NCH, QC))
            ctx["rbuf"] = _carve(ctx, "arBF_h", 28672, (128, NCH, QC))
            ctx["xrb2"] = _carve(ctx, "arBF_h", 32768, (128, NCH, QC))
            ctx["xkb2"] = _carve(ctx, "arBF_h", 36864, (128, NCH, QC))
            if "l1f" in phases:
                emit_l1_ffn(P, ctx, seq)
            emit_store_x(P, ctx, seq)
        P.barrier()
        P.emit(nc, stack)
        block = stack.enter_context(nc.Block())
        @block.tensor
        def _(t):
            P.run_engine("pe", t)
        @block.scalar
        def _(s):
            P.run_engine("act", s)
        @block.vector
        def _(v):
            P.run_engine("dve", v)
        @block.gpsimd
        def _(g):
            P.run_engine("pool", g)
        @block.sync
        def _(sy):
            P.run_engine("sp", sy)
    return nc



FUSED = True

ALL_WEIGHT_NAMES = [
    "l0_wdqt", "l0_wdkvt", "l0_wuqt", "l0_wukt", "l0_wuvt", "l0_wqrt",
    "l0_wkrt", "l0_wot",
    "l0_w1t", "l0_w2t", "l0_h1t", "l0_h2t", "l0_w3t",
    "l0_s1", "l0_s2", "l0_sh1", "l0_sh2", "l0_s3",
    "l1_wq", "l1_wkv", "l1_wo", "l1_nw", "l1_nb", "l1_onw", "l1_onb",
    "l1_wrt", "l1_wkt", "l1_wvt", "l1_mur", "l1_muk",
]
ALL_CONST_NAMES = ["ones_128x1", "ones_1x128", "rope_cos", "rope_sin",
                   "masks", "eye64", "eye128"]


def kernel(**inputs):
    consts = build_constants()
    w = prep_weights(inputs)
    x = np.asarray(inputs["L0_attn_x"])
    xT = np.ascontiguousarray(x.transpose(0, 2, 1))
    cur = [xT[c * SEQ_PER_CORE:(c + 1) * SEQ_PER_CORE] for c in range(NCORES)]

    if FUSED:
        nc = build_fused_program()
        in_maps = []
        for c in range(NCORES):
            m = {"x_in": cur[c]}
            m.update({k: consts[k] for k in ALL_CONST_NAMES})
            for nm in ALL_WEIGHT_NAMES:
                m[nm] = w[nm]
            in_maps.append(m)
        res = run_bass_kernel_spmd(nc, in_maps, core_ids=list(range(NCORES)))
        cur = [np.asarray(r["x_out"]) for r in res.results]
    else:
        launches = [
            (build_l0_attn_program,
             ["l0_wdqt", "l0_wdkvt", "l0_wuqt", "l0_wukt", "l0_wuvt",
              "l0_wqrt", "l0_wkrt", "l0_wot"],
             ["ones_128x1", "ones_1x128", "rope_cos", "rope_sin", "masks",
              "eye64", "eye128"]),
            (build_l0_ffn_program,
             ["l0_w1t", "l0_w2t", "l0_h1t", "l0_h2t", "l0_w3t",
              "l0_s1", "l0_s2", "l0_sh1", "l0_sh2", "l0_s3"],
             ["ones_128x1", "ones_1x128"]),
            (build_l1_attn_program,
             ["l1_wq", "l1_wkv", "l1_wo", "l1_nw", "l1_nb", "l1_onw", "l1_onb"],
             ["ones_128x1", "ones_1x128", "eye64", "eye128"]),
            (build_l1_ffn_program,
             ["l1_wrt", "l1_wkt", "l1_wvt", "l1_mur", "l1_muk"],
             ["ones_128x1", "ones_1x128"]),
        ]
        for build_fn, wnames, cnames in launches:
            nc = build_fn()
            in_maps = []
            for c in range(NCORES):
                m = {"x_in": cur[c]}
                m.update({k: consts[k] for k in cnames})
                for nm in wnames:
                    m[nm] = w[nm]
                in_maps.append(m)
            res = run_bass_kernel_spmd(nc, in_maps, core_ids=list(range(NCORES)))
            cur = [np.asarray(r["x_out"]) for r in res.results]
    outT = np.concatenate(cur, axis=0)
    return np.ascontiguousarray(outT.transpose(0, 2, 1))
